# version 14
# baseline (speedup 1.0000x reference)
import math
from contextlib import ExitStack
import numpy as np
import concourse.bass as bass
import concourse.mybir as mybir
from concourse.bass_utils import run_bass_kernel_spmd

F32 = mybir.dt.float32
BF16 = mybir.dt.bfloat16
AF = mybir.ActivationFunctionType
ALU = mybir.AluOpType
AX = mybir.AxisListType

P = 128
D = 1024
DT = D // P
L = 4096
NB = 2
NMODE = 64
QC = 128
NCH = L // QC
FF = 2048
FT = FF // P
NE = 8
RQ = 256
NRC = L // RQ
NH = 16
HD = 64
ECAP = 768
GCH = 256
U32 = mybir.dt.uint32

TWO_PI = 2.0 * math.pi
LN_EPS = 1e-5


def _round_hi(v, bits):
    if v == 0.0:
        return 0.0
    e = math.floor(math.log2(abs(v)))
    q = 2.0 ** (e - bits + 1)
    return round(v / q) * q


CW1 = float(np.float32(_round_hi(TWO_PI, 10)))
CW2 = float(np.float32(_round_hi(TWO_PI - CW1, 10)))
CW3 = float(np.float32(TWO_PI - CW1 - CW2))
MAGIC = float(2 ** 23)


class Sem:
    def __init__(self, nc, name):
        self.h = nc.semaphore(name).__enter__()
        self.v = 0


class MK:
    def __init__(self, nc):
        self.nc = nc
        self._uid = 0

    def nm(self, base):
        self._uid += 1
        return f"{base}_{self._uid}"

    def sb(self, name, shape, dtype):
        return self.nc.sbuf_tensor(name, shape, dtype).__enter__()

    def emit_setup(self):
        nc = self.nc
        self.x_in = nc.dram_tensor("x", [NB, L, D], F32, kind="ExternalInput").ap()
        w = {}
        for nm, shp in [
            ("ss_W", [D, D]), ("ss_b", [D]), ("gate_W", [D, D]), ("gate_b", [D]),
            ("fin_W", [D, D]), ("fin_b", [D]),
            ("in_proj", [D, NMODE]), ("out_proj", [NMODE, D]),
            ("frequencies", [NMODE]), ("decays", [NMODE]),
            ("router_w", [NE, D]),
            ("expert_gate", [NE, D, FF]), ("expert_up", [NE, D, FF]),
            ("expert_down", [NE, FF, D]),
            ("shared_gate", [D, FF]), ("shared_up", [D, FF]), ("shared_down", [FF, D]),
            ("maa_k", [D]), ("maa_v", [D]), ("maa_r", [D]),
            ("t_decay", [NH]), ("t_faaaa", [NH]),
            ("Wr", [D, D]), ("br", [D]), ("Wk", [D, D]), ("bk", [D]),
            ("Wv", [D, D]), ("bv", [D]), ("Wo", [D, D]), ("bo", [D]),
            ("CB", [D, D]), ("SCB", [D]), ("ffn_bias", [D]),
        ]:
            w[nm] = nc.dram_tensor(nm, shp, F32, kind="ExternalInput").ap()
        self.w = w
        self.out = nc.dram_tensor("out", [NB, L, D], F32, kind="ExternalOutput").ap()

        self.wbf = {}
        for nm in ["ss_W", "gate_W", "fin_W", "Wr", "Wk", "Wv", "Wo", "CB"]:
            self.wbf[nm] = nc.dram_tensor("bf_" + nm, [D, D], BF16, kind="Internal").ap()
        self.xn_ct_dram = nc.dram_tensor("xn_ct", [DT, P, NB * L], BF16, kind="Internal").ap()
        self.y_ct_dram = nc.dram_tensor("y_ctd", [DT, P, NB * L], BF16, kind="Internal").ap()
        self.x1_dram = nc.dram_tensor("x1", [NB, L, D], F32, kind="Internal").ap()
        self.xn1_ct_dram = nc.dram_tensor("xn1_ct", [DT, P, NB * L], BF16, kind="Internal").ap()
        self.rkv_dram = nc.dram_tensor("rkv", [3, DT, P, NB * L], BF16, kind="Internal").ap()
        self.s1_ct_dram = nc.dram_tensor("s1_ct", [DT, P, NB * L], BF16, kind="Internal").ap()
        self.k_dram_h = nc.dram_tensor("k_dram", [D, 256], BF16, kind="Internal")
        self.k_dram = self.k_dram_h.ap()
        self.yt_dram = nc.dram_tensor("yt_dram", [NB * L, D], BF16, kind="Internal").ap()
        self.xn_tc2_dram = nc.dram_tensor("xn_tc2", [NB * L, D], BF16, kind="Internal").ap()
        ECAP = 768
        self.xg_dram = nc.dram_tensor("xg", [NE * ECAP, D], BF16, kind="Internal").ap()
        self.eog_dram = nc.dram_tensor("eog", [NE * ECAP, D], BF16, kind="Internal").ap()
        self.shg_dram = nc.dram_tensor("shg", [L, D], BF16, kind="Internal").ap()

        self.s_load = Sem(nc, "s_load")
        self.s_ld2 = Sem(nc, "s_ld2")
        self.s_st = Sem(nc, "s_st")
        self.s_gld = Sem(nc, "s_gld")
        self.s_gst = Sem(nc, "s_gst")
        self.s_mm = Sem(nc, "s_mm")
        self.s_ev = Sem(nc, "s_ev")
        self.s_ac = Sem(nc, "s_ac")
        self.s_g = Sem(nc, "s_g")
        self.s_x = Sem(nc, "s_x")
        self.s_xw = Sem(nc, "s_xw")
        self.s_lp = [Sem(nc, "s_lp0"), Sem(nc, "s_lp1")]
        self.s_sp = [Sem(nc, "s_sp0"), Sem(nc, "s_sp1")]
        self.s_lp2 = [Sem(nc, "s_lp2a"), Sem(nc, "s_lp2b")]
        self.s_lp3 = [Sem(nc, "s_lp3a"), Sem(nc, "s_lp3b")]
        self.s_lp4 = [Sem(nc, "s_lp4a"), Sem(nc, "s_lp4b")]
        self.s_gp = [Sem(nc, "s_gp0"), Sem(nc, "s_gp1")]
        self.s_dn = [Sem(nc, "s_dn0"), Sem(nc, "s_dn1")]
        self.s_vch = Sem(nc, "s_vch")
        self.s_sch = Sem(nc, "s_sch")
        self.s_gch = Sem(nc, "s_gch")

        self.ident = self.sb("ident", [P, P], BF16)
        self.identf = self.sb("identf", [P, P], F32)
        self.ones_col = self.sb("ones_col", [1, P], BF16)
        self.ones_colf = self.sb("ones_colf", [1, P], F32)
        self.onesc_b = self.sb("onesc_b", [P, 1], BF16)
        self.LTb = self.sb("LTb", [P, P], BF16)
        self.selb = self.sb("selb", [NE, NE * P], BF16)
        self.vecs = {}
        for nm in ["ss_b", "gate_b", "fin_b", "br", "bk", "bv", "bo",
                   "SCB", "ffn_bias", "maa_k", "maa_v", "maa_r"]:
            self.vecs[nm] = self.sb("v_" + nm, [P, DT], F32)
        self.rwT = self.sb("rwT", [P, DT * NE], F32)
        self.cst = self.sb("cst", [P, 4], F32)
        self.r_mxs = self.sb("rp_mxs", [P, 6], F32)
        self.r_sc = self.sb("rp_sc", [P, NCH], F32)
        self.r_m01 = [self.sb(f"rp_m{i}", [P, NE], BF16) for i in range(2)]
        self.r_didxf = self.sb("rp_didx", [P, NCH], F32)
        self.r_didxu = self.sb("rp_didxu", [P, NCH], U32)
        self.r_jnk = self.sb("rp_jnk", [P, NE], F32)
        self.r_off = self.sb("rp_off", [1, NE], F32)
        self.r_offf = self.sb("rp_offf", [1, NE], F32)
        self.arena1 = self.sb("arena1", [P, NCH * D], BF16)
        self.arena2 = self.sb("arena2", [P, NCH * D], BF16)
        self.lg_tc = self.sb("lg_tc", [P, NB * NCH * NE], F32)

    def emit_common(self):
        nc = self.nc
        sg, sld = self.s_g, self.s_load
        with ExitStack() as stk, nc.Block("common") as blk:
            iota_row = stk.enter_context(nc.sbuf_tensor("iota_row", [P, P], F32))
            iota_col = stk.enter_context(nc.sbuf_tensor("iota_col", [P, 1], F32))
            pidx8 = stk.enter_context(nc.sbuf_tensor("pidx8", [NE, P], F32))
            g0 = sg.v
            sg.v += 1

            @blk.gpsimd
            def _(g):
                g.iota(iota_row[:], [[1, P]], channel_multiplier=0,
                       allow_small_or_imprecise_dtypes=True)
                g.iota(iota_col[:], [[0, 1]], channel_multiplier=1,
                       allow_small_or_imprecise_dtypes=True)
                g.iota(pidx8[:], [[0, P]], channel_multiplier=1,
                       allow_small_or_imprecise_dtypes=True).then_inc(sg.h, 1)

            @blk.vector
            def _(v):
                v.wait_ge(sg.h, g0 + 1)
                for e in range(NE):
                    v.tensor_scalar(self.selb[:, e * P:(e + 1) * P],
                                    pidx8[:], float(e), None, ALU.is_equal)
                v.tensor_scalar(self.ident[:], iota_row[:],
                                iota_col[:, 0:1], None, ALU.is_equal)
                v.tensor_scalar(self.identf[:], iota_row[:],
                                iota_col[:, 0:1], None, ALU.is_equal)
                v.tensor_scalar(self.LTb[:], iota_row[:],
                                iota_col[:, 0:1], None, ALU.is_gt)
                v.memset(self.onesc_b[:], 1.0)
                v.memset(self.ones_col[:], 1.0)
                v.memset(self.ones_colf[:], 1.0)
                v.memset(self.cst[:, 0:1], LN_EPS)
                v.memset(self.cst[:, 1:2], 0.0)
                v.memset(self.cst[:, 2:3], 1.0).then_inc(self.s_vch.h, 1)
                self.s_vch.v += 1
                v.wait_ge(self.s_vch.h, self.s_vch.v)

        with nc.Block("wcast") as blk:
            sgl = self.s_gld
            l0 = sgl.v
            nd = 12 * DT + DT
            sgl.v = l0 + 16 * nd

            @blk.gpsimd
            def _(g):
                for nm in ["ss_b", "gate_b", "fin_b", "br", "bk", "bv",
                           "bo", "SCB", "ffn_bias", "maa_k", "maa_v", "maa_r"]:
                    for dt in range(DT):
                        g.dma_start(self.vecs[nm][:, dt:dt + 1],
                                    self.w[nm][dt * P:(dt + 1) * P]).then_inc(sgl.h, 16)
                for dt in range(DT):
                    g.dma_start(
                        self.rwT[:, dt * NE:(dt + 1) * NE],
                        self.w["router_w"][:, dt * P:(dt + 1) * P]
                            .rearrange("e p -> p e"),
                        allow_slow_non_contiguous=True).then_inc(sgl.h, 16)

            @blk.sync
            def _(sy):
                sy.wait_ge(sgl.h, sgl.v)

    def emit_load_wT(self, tag, wbf_ap, dst, n_in=D, n_out=D, dst_off=0,
                     need=None):
        nc = self.nc
        ni_t = n_in // P
        l0 = self.s_ld2.v
        self.s_ld2.v = l0 + 16 * ni_t
        if need is None:
            need = (self.s_x, 80)
        with nc.Block("lwT_" + tag) as blk:
            @blk.sync
            def _(sy):
                sy.wait_ge(need[0].h, need[1])
                for it in range(ni_t):
                    sy.dma_start(dst[:, dst_off + it * n_out: dst_off + (it + 1) * n_out],
                                 wbf_ap[:, it * P:(it + 1) * P],
                                 transpose=True).then_inc(self.s_ld2.h, 16)
                sy.wait_ge(self.s_ld2.h, self.s_ld2.v)

    def emit_wcast2(self, part):
        nc = self.nc
        grp = (["ss_W", "gate_W", "fin_W"] if part == 0
               else ["Wr", "Wk", "Wv", "Wo", "CB"])
        sem = self.s_xw if part == 0 else self.s_x
        with nc.Block(f"wcast2_{part}") as blk:
            @blk.gpsimd
            def _(g):
                for nm in grp:
                    g.dma_start(self.wbf[nm], self.w[nm]).then_inc(sem.h, 16)
                    sem.v += 16

    def emit_ln(self, b, src_dram, xn_tc, xn_spill, layer, ct_sbuf=None):
        nc = self.nc
        smm, sev, sac = self.s_mm, self.s_ev, self.s_ac
        slp, ssp = self.s_lp, self.s_sp
        n = NCH
        with ExitStack() as stk:
            def SB(nm, shape, dt_):
                return stk.enter_context(nc.sbuf_tensor(self.nm(nm), shape, dt_))

            def PS(nm, shape, dt_=F32):
                return stk.enter_context(nc.psum_tensor(self.nm(nm), shape, dt_))

            xt = [SB(f"ln_x{i}", [P, D], F32) for i in range(2)]
            xnf = [SB(f"ln_xnf{i}", [P, D], F32) for i in range(2)]
            bst = [SB(f"ln_bst{i}", [P, 12], F32) for i in range(2)]
            sta = [SB(f"ln_st{i}", [P, 6], F32) for i in range(2)]
            ctf = [SB(f"ln_ctf{i}", [P, D], F32) for i in range(2)]
            stg = ([SB(f"ln_stg{i}", [P, D], BF16) for i in range(2)]
                   if ct_sbuf is None else None)
            lgrow = [SB(f"ln_lgrow{i}", [NE, P], F32) for i in range(2)]
            ptr = [PS(f"ln_ptr{i}", [P, D]) for i in range(2)]
            plg = [PS(f"ln_plg{i}", [NE, P]) for i in range(2)]
            plg2 = [PS(f"ln_plg2{i}", [P, NE]) for i in range(2)]
            m0, e0, a0 = smm.v, sev.v, sac.v
            lp0 = [s.v for s in slp]
            sp0 = [s.v for s in ssp]
            lay0 = (layer == 0)
            sev_aggr, sev_nmrs, sev_ctf = {}, {}, {}
            sev_lgr, sev_lgt = {}, {}
            sac_rstd, sac_apply, sac_stage, sac_cast = {}, {}, {}, {}
            smm_tr, smm_rt, smm_lgt = {}, {}, {}
            cv, ca_, cm = e0, a0, m0
            nit = n + 4 if lay0 else n + 2
            for t in range(nit):
                if t < n:
                    cv += 1; sev_aggr[t] = cv
                    ca_ += 1; sac_rstd[t] = ca_
                j, k = t - 1, t - 2
                k3, k4 = t - 3, t - 4
                if 0 <= j < n:
                    cv += 1; sev_nmrs[j] = cv
                    ca_ += 1; sac_apply[j] = ca_
                    cm += 1; smm_tr[j] = cm
                if 0 <= k < n:
                    cv += 1; sev_ctf[k] = cv
                    ca_ += 1; sac_stage[k] = ca_
                    if xn_tc is not None:
                        ca_ += 1; sac_cast[k] = ca_
                    if lay0:
                        cm += 1; smm_rt[k] = cm
                if lay0 and 0 <= k3 < n:
                    cv += 1; sev_lgr[k3] = cv
                    cm += 1; smm_lgt[k3] = cm
                if lay0 and 0 <= k4 < n:
                    cv += 1; sev_lgt[k4] = cv

            with nc.Block(f"lnA_{layer}_{b}") as blk:
                @blk.sync
                def _(sy):
                    for i in range(n):
                        if i >= 2:
                            sy.wait_ge(sev.h, sev_aggr[i - 2])
                            sy.wait_ge(sac.h, sac_apply[i - 2])
                        sy.dma_start(xt[i % 2][:], src_dram[b, i * P:(i + 1) * P, :]
                                     ).then_inc(slp[i % 2].h, 16)
                    if ct_sbuf is None:
                        for par in range(2):
                            sy.wait_ge(ssp[par].h, sp0[par] + 16 * (n // 2))
                    else:
                        sy.wait_ge(sac.h, ca_)

                @blk.gpsimd
                def _(g):
                    if ct_sbuf is not None:
                        return
                    for i in range(n):
                        g.wait_ge(sac.h, sac_stage[i])
                        t0 = b * L + i * P
                        g.dma_start(
                            xn_spill.rearrange("dt p t -> p dt t")[:, :, t0:t0 + P],
                            stg[i % 2].ap().rearrange("p (dt q) -> p dt q", dt=DT)
                        ).then_inc(ssp[i % 2].h, 16)

                @blk.vector
                def _(v):
                    for t in range(nit):
                        if t < n:
                            v.wait_ge(slp[t % 2].h, lp0[t % 2] + 16 * (t // 2 + 1))
                            if t >= 2:
                                v.wait_ge(sac.h, sac_rstd[t - 2])
                            v.bn_stats(bst[t % 2][:, 0:6], xt[t % 2][:, 0:512])
                            v.bn_stats(bst[t % 2][:, 6:12], xt[t % 2][:, 512:1024]
                                       ).then_inc(self.s_vch.h, 1)
                            self.s_vch.v += 1
                            v.wait_ge(self.s_vch.h, self.s_vch.v)
                            v.bn_aggr(sta[t % 2][:, 0:2], bst[t % 2][:, 0:12]
                                      ).then_inc(sev.h, 1)
                        j, k = t - 1, t - 2
                        if 0 <= j < n:
                            v.wait_ge(sac.h, sac_rstd[j])
                            v.scalar_tensor_tensor(sta[j % 2][:, 4:5],
                                                   sta[j % 2][:, 0:1], -1.0,
                                                   sta[j % 2][:, 3:4],
                                                   ALU.mult, ALU.mult
                                                   ).then_inc(sev.h, 1)
                        if 0 <= k < n:
                            v.wait_ge(smm.h, smm_tr[k])
                            if k >= 2:
                                v.wait_ge(sac.h, sac_stage[k - 2])
                                if lay0:
                                    v.wait_ge(smm.h, smm_rt[k - 2])
                            v.tensor_copy(ctf[k % 2][:], ptr[k % 2][:]
                                          ).then_inc(sev.h, 1)
                        k3, k4 = t - 3, t - 4
                        if lay0 and 0 <= k3 < n:
                            v.wait_ge(smm.h, smm_rt[k3])
                            if k3 >= 2:
                                v.wait_ge(smm.h, smm_lgt[k3 - 2])
                            v.tensor_copy(lgrow[k3 % 2][:], plg[k3 % 2][:]
                                          ).then_inc(sev.h, 1)
                        if lay0 and 0 <= k4 < n:
                            v.wait_ge(smm.h, smm_lgt[k4])
                            v.tensor_copy(
                                self.lg_tc[:, (b * NCH + k4) * NE:
                                           (b * NCH + k4 + 1) * NE],
                                plg2[k4 % 2][:]).then_inc(sev.h, 1)

                @blk.scalar
                def _(s):
                    for t in range(nit):
                        if t < n:
                            s.wait_ge(sev.h, sev_aggr[t])
                            s.activation(sta[t % 2][:, 2:3], sta[t % 2][:, 1:2],
                                         AF.Ln, bias=self.cst[:, 0:1]
                                         ).then_inc(self.s_sch.h, 1)
                            self.s_sch.v += 1
                            s.wait_ge(self.s_sch.h, self.s_sch.v)
                            s.activation(sta[t % 2][:, 3:4], sta[t % 2][:, 2:3],
                                         AF.Exp, scale=-0.5).then_inc(sac.h, 1)
                        j, k = t - 1, t - 2
                        if 0 <= j < n:
                            s.wait_ge(sev.h, sev_nmrs[j])
                            s.wait_ge(sac.h, sac_rstd[j])
                            if j >= 2:
                                s.wait_ge(smm.h, smm_tr[j - 2])
                                if xn_tc is not None:
                                    s.wait_ge(sac.h, sac_cast[j - 2])
                            s.activation(xnf[j % 2][:], xt[j % 2][:], AF.Identity,
                                         scale=sta[j % 2][:, 3:4],
                                         bias=sta[j % 2][:, 4:5]).then_inc(sac.h, 1)
                        if 0 <= k < n:
                            s.wait_ge(sev.h, sev_ctf[k])
                            if ct_sbuf is None:
                                if k >= 2:
                                    s.wait_ge(ssp[k % 2].h,
                                              sp0[k % 2] + 16 * ((k - 2) // 2 + 1))
                                tgt = stg[k % 2][:]
                            else:
                                tgt = bass.AP(ct_sbuf, k * P,
                                              [[DT * L, P], [L, DT], [1, P]])
                            s.activation(tgt, ctf[k % 2][:], AF.Copy
                                         ).then_inc(sac.h, 1)
                            if xn_tc is not None:
                                s.wait_ge(sac.h, sac_apply[k])
                                s.activation(xn_tc[:, k * D:(k + 1) * D],
                                             xnf[k % 2][:], AF.Copy
                                             ).then_inc(sac.h, 1)

                @blk.tensor
                def _(t_):
                    for t in range(nit):
                        j, k = t - 1, t - 2
                        k3 = t - 3
                        if 0 <= j < n:
                            t_.wait_ge(sac.h, sac_apply[j])
                            if j >= 2:
                                t_.wait_ge(sev.h, sev_ctf[j - 2])
                            for dt in range(DT):
                                ins = t_.transpose(ptr[j % 2][:, dt * P:(dt + 1) * P],
                                                   xnf[j % 2][:, dt * P:(dt + 1) * P],
                                                   self.identf[:])
                                if dt == DT - 1:
                                    ins.then_inc(smm.h, 1)
                        if 0 <= k < n and lay0:
                            t_.wait_ge(sev.h, sev_ctf[k])
                            if k >= 2:
                                t_.wait_ge(sev.h, sev_lgr[k - 2])
                            for dt in range(DT):
                                ins = t_.matmul(plg[k % 2][:, :],
                                                self.rwT[:, dt * NE:(dt + 1) * NE],
                                                ctf[k % 2][:, dt * P:(dt + 1) * P],
                                                start=(dt == 0), stop=(dt == DT - 1))
                                if dt == DT - 1:
                                    ins.then_inc(smm.h, 1)
                        if lay0 and 0 <= k3 < n:
                            t_.wait_ge(sev.h, sev_lgr[k3])
                            if k3 >= 2:
                                t_.wait_ge(sev.h, sev_lgt[k3 - 2])
                            t_.transpose(plg2[k3 % 2][:], lgrow[k3 % 2][:],
                                         self.identf[0:NE, 0:NE]
                                         ).then_inc(smm.h, 1)

            for par in range(2):
                slp[par].v = lp0[par] + 16 * (n // 2)
                if ct_sbuf is None:
                    ssp[par].v = sp0[par] + 16 * (n // 2)
            sev.v = cv
            sac.v = ca_
            smm.v = cm

    def emit_gate_ss_fin(self, b, xn_src, y_src, wT, s_ct):
        nc = self.nc
        smm, sev, sac = self.s_mm, self.s_ev, self.s_ac
        slp, slp2 = self.s_lp, self.s_lp2
        TC = 512
        nch = L // TC
        WG, WS, WF = 0, DT * D, 2 * DT * D
        with ExitStack() as stk:
            def SB(nm, shape, dt_):
                return stk.enter_context(nc.sbuf_tensor(self.nm(nm), shape, dt_))

            def PS(nm, shape, dt_=F32):
                return stk.enter_context(nc.psum_tensor(self.nm(nm), shape, dt_))

            xcn = [SB(f"c_x{i}", [P, DT * TC], BF16) for i in range(2)]
            ycn = [SB(f"c_y{i}", [P, DT * TC], BF16) for i in range(2)]
            ga = SB("c_ga", [P, DT * TC], BF16)
            sa = SB("c_sa", [P, DT * TC], BF16)
            pr = SB("c_pr", [P, DT * TC], BF16)
            pa = [PS(f"c_pa{i}", [P, TC]) for i in range(2)]
            ps = [PS(f"c_ps{i}", [P, TC]) for i in range(2)]
            pf = [PS(f"c_pf{i}", [P, TC]) for i in range(2)]
            m0, e0, a0 = smm.v, sev.v, sac.v
            lp0 = [s.v for s in slp]
            lq0 = [s.v for s in slp2]
            MV, EV, AV = 24, 8, 24

            with nc.Block(f"gsf_{b}") as blk:
                @blk.sync
                def _(sy):
                    for c in range(nch):
                        t0 = b * L + c * TC
                        if c >= 2:
                            sy.wait_ge(smm.h, m0 + MV * (c - 2) + 8)
                        sy.dma_start(
                            xcn[c % 2].ap().rearrange("p (dt t) -> p dt t", dt=DT),
                            xn_src.rearrange("dt p t -> p dt t")[:, :, t0:t0 + TC]
                        ).then_inc(slp[c % 2].h, 16)
                        if c >= 2:
                            sy.wait_ge(smm.h, m0 + MV * (c - 2) + 16)
                        sy.dma_start(
                            ycn[c % 2].ap().rearrange("p (dt t) -> p dt t", dt=DT),
                            y_src.rearrange("dt p t -> p dt t")[:, :, t0:t0 + TC]
                        ).then_inc(slp2[c % 2].h, 16)
                    sy.wait_ge(sac.h, a0 + AV * nch)

                @blk.tensor
                def _(t):
                    for c in range(nch):
                        cb_m, cb_a, cb_e = m0 + MV * c, a0 + AV * c, e0 + EV * c
                        t.wait_ge(slp[c % 2].h, lp0[c % 2] + 16 * (c // 2 + 1))
                        for ot in range(DT):
                            if (c, ot) > (0, 1):
                                pot = ot - 2 if ot >= 2 else ot + 6
                                pcb = cb_a if ot >= 2 else a0 + AV * (c - 1)
                                t.wait_ge(sac.h, pcb + pot + 1)
                            for it in range(DT):
                                ins = t.matmul(pa[ot % 2][:],
                                         wT[:, WG + it * D + ot * P: WG + it * D + (ot + 1) * P],
                                         xcn[c % 2][:, it * TC:(it + 1) * TC],
                                         start=(it == 0), stop=(it == DT - 1))
                                if it == DT - 1:
                                    ins.then_inc(smm.h, 1)
                        t.wait_ge(slp2[c % 2].h, lq0[c % 2] + 16 * (c // 2 + 1))
                        for ot in range(DT):
                            if (c, ot) > (0, 1):
                                pot = ot - 2 if ot >= 2 else ot + 6
                                pcb = cb_a if ot >= 2 else a0 + AV * (c - 1)
                                t.wait_ge(sac.h, pcb + 8 + pot + 1)
                            for it in range(DT):
                                ins = t.matmul(ps[ot % 2][:],
                                         wT[:, WS + it * D + ot * P: WS + it * D + (ot + 1) * P],
                                         ycn[c % 2][:, it * TC:(it + 1) * TC],
                                         start=(it == 0), stop=(it == DT - 1))
                                if it == DT - 1:
                                    ins.then_inc(smm.h, 1)
                        t.wait_ge(sev.h, cb_e + 8)
                        for ot in range(DT):
                            if (c, ot) > (0, 1):
                                pot = ot - 2 if ot >= 2 else ot + 6
                                pcb = cb_a if ot >= 2 else a0 + AV * (c - 1)
                                t.wait_ge(sac.h, pcb + 16 + pot + 1)
                            for it in range(DT):
                                ins = t.matmul(pf[ot % 2][:],
                                         wT[:, WF + it * D + ot * P: WF + it * D + (ot + 1) * P],
                                         pr[:, it * TC:(it + 1) * TC],
                                         start=(it == 0), stop=(it == DT - 1))
                                if it == DT - 1:
                                    ins.then_inc(smm.h, 1)

                @blk.scalar
                def _(s):
                    for c in range(nch):
                        cb_m, cb_a, cb_e = m0 + MV * c, a0 + AV * c, e0 + EV * c
                        if c >= 1:
                            s.wait_ge(sev.h, e0 + EV * c)
                        for ot in range(DT):
                            s.wait_ge(smm.h, cb_m + ot + 1)
                            s.activation(ga[:, ot * TC:(ot + 1) * TC], pa[ot % 2][:],
                                         AF.Gelu, bias=self.vecs["gate_b"][:, ot:ot + 1]
                                         ).then_inc(sac.h, 1)
                        for ot in range(DT):
                            s.wait_ge(smm.h, cb_m + 8 + ot + 1)
                            s.activation(sa[:, ot * TC:(ot + 1) * TC], ps[ot % 2][:],
                                         AF.Gelu, bias=self.vecs["ss_b"][:, ot:ot + 1]
                                         ).then_inc(sac.h, 1)
                        for ot in range(DT):
                            s.wait_ge(smm.h, cb_m + 16 + ot + 1)
                            s.activation(s_ct[:, ot * L + c * TC: ot * L + c * TC + TC],
                                         pf[ot % 2][:], AF.Identity, scale=1.0,
                                         bias=self.vecs["fin_b"][:, ot:ot + 1]
                                         ).then_inc(sac.h, 1)

                @blk.vector
                def _(v):
                    for c in range(nch):
                        cb_a = a0 + AV * c
                        if c >= 1:
                            v.wait_ge(smm.h, m0 + MV * c)
                        for ot in range(DT):
                            v.wait_ge(sac.h, cb_a + 8 + ot + 1)
                            v.tensor_tensor(pr[:, ot * TC:(ot + 1) * TC],
                                            ga[:, ot * TC:(ot + 1) * TC],
                                            sa[:, ot * TC:(ot + 1) * TC], ALU.mult
                                            ).then_inc(sev.h, 1)

            for par in range(2):
                slp[par].v = lp0[par] + 16 * (nch // 2)
                slp2[par].v = lq0[par] + 16 * (nch // 2)
            smm.v = m0 + MV * nch
            sev.v = e0 + EV * nch
            sac.v = a0 + AV * nch

    def emit_moe(self, b, xn_src, g_ct, wga):
        nc = self.nc
        smm, sev, sac, sgc = self.s_mm, self.s_ev, self.s_ac, self.s_gch
        slp, sgp, sdn = self.s_lp, self.s_gp, self.s_dn
        TC = 512
        nch = L // TC
        moestk = ExitStack()
        wdn = moestk.enter_context(nc.sbuf_tensor(self.nm("wdn"), [P, FT * D], BF16))
        msr = moestk.enter_context(nc.sbuf_tensor(self.nm("msr"), [NE, L], BF16))
        ones_row = moestk.enter_context(nc.sbuf_tensor(self.nm("onesr"), [1, 512], BF16))
        with ExitStack() as stk:
            def SB(nm, shape, dt_):
                return stk.enter_context(nc.sbuf_tensor(self.nm(nm), shape, dt_))

            def PS(nm, shape, dt_=F32):
                return stk.enter_context(nc.psum_tensor(self.nm(nm), shape, dt_))

            mxs = SB("r_mxs", [P, 6], F32)
            mst = [SB(f"r_mst{i}", [P, NE], F32) for i in range(2)]
            pms = PS("r_pms", [NE, P])
            m0, e0, a0 = smm.v, sev.v, sac.v
            with nc.Block(f"route_{b}") as blk:
                @blk.vector
                def _(v):
                    for i in range(NCH):
                        lg = self.lg_tc[:, (b * NCH + i) * NE:
                                        (b * NCH + i + 1) * NE]
                        v.reduce_max(mxs[:, i % 2:i % 2 + 1], lg, axis=AX.X
                                     ).then_inc(sev.h, 1)
                        v.wait_ge(sac.h, a0 + i + 1)
                        if i >= 2:
                            v.wait_ge(smm.h, m0 + i - 1)
                        for e in range(NE):
                            v.tensor_scalar(mst[i % 2][:, e:e + 1], lg[:, e:e + 1],
                                            mxs[:, i % 2:i % 2 + 1],
                                            mxs[:, 2 + i % 2:3 + i % 2],
                                            ALU.is_equal, ALU.mult)
                        v.tensor_copy(mxs[:, 4:5], self.cst[:, 0:1]
                                      ).then_inc(sev.h, 1)
                        if i >= 1:
                            v.wait_ge(smm.h, m0 + i)
                            v.tensor_copy(msr[0:NE, (i - 1) * P: i * P],
                                          pms[:]).then_inc(sev.h, 1)
                        else:
                            v.tensor_copy(mxs[:, 5:6], self.cst[:, 0:1]
                                          ).then_inc(sev.h, 1)
                    v.wait_ge(smm.h, m0 + NCH)
                    v.tensor_copy(msr[0:NE, (NCH - 1) * P: NCH * P],
                                  pms[:]).then_inc(sev.h, 1)
                    v.memset(ones_row[:], 1.0)

                @blk.scalar
                def _(s):
                    for i in range(NCH):
                        s.wait_ge(sev.h, e0 + 3 * i + 1)
                        s.activation(mxs[:, 2 + i % 2:3 + i % 2],
                                     mxs[:, i % 2:i % 2 + 1], AF.Sigmoid
                                     ).then_inc(sac.h, 1)

                @blk.tensor
                def _(t):
                    for i in range(NCH):
                        t.wait_ge(sev.h, e0 + 3 * i + 2)
                        if i >= 1:
                            t.wait_ge(sev.h, e0 + 3 * i + 3)
                        t.transpose(pms[:], mst[i % 2][:], self.identf[:]
                                    ).then_inc(smm.h, 1)
            smm.v = m0 + NCH
            sev.v = e0 + 3 * NCH + 1
            sac.v = a0 + NCH

        with ExitStack() as stk:
            def SB(nm, shape, dt_):
                return stk.enter_context(nc.sbuf_tensor(self.nm(nm), shape, dt_))

            def PS(nm, shape, dt_=F32):
                return stk.enter_context(nc.psum_tensor(self.nm(nm), shape, dt_))

            xcn0 = SB("m_x0", [P, DT * TC], BF16)
            xcn = [xcn0, xcn0]
            xs = xcn
            bc0 = SB("m_bc", [P, TC], BF16)
            bc = [bc0, bc0]
            hs0 = SB("m_hs", [P, TC], BF16)
            hs = [hs0, hs0]
            h_all = SB("m_h", [P, FT * TC], BF16)
            pbc = PS("m_pbc", [P, TC])
            pg = [PS(f"m_pg{i}", [P, TC]) for i in range(2)]
            pu = [PS(f"m_pu{i}", [P, TC]) for i in range(2)]
            pd = [PS(f"m_pd{i}", [P, TC]) for i in range(2)]
            m0, e0, a0 = smm.v, sev.v, sac.v
            lp0 = [s.v for s in slp]
            gp0 = [s.v for s in sgp]
            dn0 = [s.v for s in sdn]
            g0 = sgc.v
            NEX = NE + 1
            MV, EV, AV = 41, 32, 17
            MC, EC, AC = MV * nch, EV * nch, AV * nch

            def src_w(kind, e):
                if e < NE:
                    a = {"g": self.w["expert_gate"], "u": self.w["expert_up"],
                         "d": self.w["expert_down"]}[kind][e]
                else:
                    a = {"g": self.w["shared_gate"], "u": self.w["shared_up"],
                         "d": self.w["shared_down"]}[kind]
                return a

            with nc.Block(f"moe_{b}") as blk:
                @blk.gpsimd
                def _(g):
                    g.memset(g_ct[:], 0.0).then_inc(sgc.h, 1)
                    for e in range(NEX):
                        if e >= 1:
                            g.wait_ge(smm.h, m0 + MC * (e - 1) + MV * (nch - 1) + 33)
                        g.dma_start(wga[:, 0:DT * FF].rearrange(
                                        "p (it f) -> p it f", it=DT),
                                    src_w("g", e).rearrange("(it p) f -> p it f", p=P)
                                    ).then_inc(sgp[e % 2].h, 16)
                        g.dma_start(wga[:, DT * FF:2 * DT * FF].rearrange(
                                        "p (it f) -> p it f", it=DT),
                                    src_w("u", e).rearrange("(it p) f -> p it f", p=P)
                                    ).then_inc(sgp[e % 2].h, 16)
                        if e >= 1:
                            g.wait_ge(smm.h, m0 + MC * e)
                        g.dma_start(wdn.ap().rearrange("p (ft o) -> p ft o", ft=FT),
                                    src_w("d", e).rearrange("(ft p) o -> p ft o", p=P)
                                    ).then_inc(sdn[e % 2].h, 16)
                    self.s_gch.v = g0 + 1

                @blk.sync
                def _(sy):
                    for e in range(NEX):
                        for c in range(nch):
                            t0 = b * L + c * TC
                            i = e * nch + c
                            if i >= 1:
                                e2, c2 = divmod(i - 1, nch)
                                sy.wait_ge(smm.h, m0 + MC * e2 + MV * c2 + 33)
                            sy.dma_start(
                                xcn[i % 2].ap().rearrange("p (dt t) -> p dt t", dt=DT),
                                xn_src.rearrange("dt p t -> p dt t")[:, :, t0:t0 + TC]
                            ).then_inc(slp[i % 2].h, 16)
                    sy.wait_ge(sac.h, a0 + AC * NEX)

                @blk.tensor
                def _(t):
                    for e in range(NEX):
                        eb_m = m0 + MC * e
                        t.wait_ge(sgp[e % 2].h, gp0[e % 2] + 16 * 2 * (e // 2 + 1))
                        for c in range(nch):
                            cb_m = eb_m + MV * c
                            cb_e = e0 + EC * e + EV * c
                            cb_a = a0 + AC * e + AV * c
                            i = e * nch + c
                            if i >= 1:
                                t.wait_ge(sac.h, a0 + AC * e + AV * (c - 1) + 1
                                          if c >= 1 else a0 + AC * (e - 1) + AV * (nch - 1) + 1)
                            if e < NE:
                                t.matmul(pbc[:], self.selb[:, e * P:(e + 1) * P],
                                         msr[0:NE, c * TC: c * TC + TC],
                                         start=True, stop=True).then_inc(smm.h, 1)
                            else:
                                t.matmul(pbc[:], self.ones_col[:],
                                         ones_row[0:1, 0:TC],
                                         start=True, stop=True).then_inc(smm.h, 1)
                            t.wait_ge(sev.h, cb_e + 8)
                            for ft in range(FT):
                                if ft >= 2 or i >= 1:
                                    pft = (cb_a + ft - 2 + 2) if ft >= 2 else \
                                          (a0 + AC * e + AV * (c - 1) + ft + 14 + 2
                                           if c >= 1 else
                                           a0 + AC * (e - 1) + AV * (nch - 1) + ft + 14 + 2)
                                    t.wait_ge(sac.h, pft)
                                for it in range(DT):
                                    ins = t.matmul(
                                        pg[ft % 2][:],
                                        wga[:, it * FF + ft * P: it * FF + (ft + 1) * P],
                                        xs[i % 2][:, it * TC:(it + 1) * TC],
                                        start=(it == 0), stop=(it == DT - 1))
                                    if it == DT - 1:
                                        ins.then_inc(smm.h, 1)
                                if ft >= 2 or i >= 1:
                                    pft = (cb_e + 8 + ft - 2 + 1) if ft >= 2 else \
                                          (e0 + EC * e + EV * (c - 1) + 8 + ft + 14 + 1
                                           if c >= 1 else
                                           e0 + EC * (e - 1) + EV * (nch - 1) + 8 + ft + 14 + 1)
                                    t.wait_ge(sev.h, pft)
                                for it in range(DT):
                                    ins = t.matmul(
                                        pu[ft % 2][:],
                                        wga[:, DT * FF + it * FF + ft * P:
                                            DT * FF + it * FF + (ft + 1) * P],
                                        xs[i % 2][:, it * TC:(it + 1) * TC],
                                        start=(it == 0), stop=(it == DT - 1))
                                    if it == DT - 1:
                                        ins.then_inc(smm.h, 1)
                            if c == 0:
                                t.wait_ge(sdn[e % 2].h, dn0[e % 2] + 16 * (e // 2 + 1))
                            t.wait_ge(sev.h, cb_e + 8 + 16)
                            for ot in range(DT):
                                if ot >= 2 or i >= 1:
                                    pot = (cb_e + 24 + ot - 2 + 1) if ot >= 2 else \
                                          (e0 + EC * e + EV * (c - 1) + 24 + ot + 6 + 1
                                           if c >= 1 else
                                           e0 + EC * (e - 1) + EV * (nch - 1) + 24 + ot + 6 + 1)
                                    t.wait_ge(sev.h, pot)
                                for ft in range(FT):
                                    ins = t.matmul(
                                        pd[ot % 2][:],
                                        wdn[:, ft * D + ot * P: ft * D + (ot + 1) * P],
                                        h_all[:, ft * TC:(ft + 1) * TC],
                                        start=(ft == 0), stop=(ft == FT - 1))
                                    if ft == FT - 1:
                                        ins.then_inc(smm.h, 1)

                @blk.scalar
                def _(s):
                    for e in range(NEX):
                        for c in range(nch):
                            cb_m = m0 + MC * e + MV * c
                            cb_a = a0 + AC * e + AV * c
                            i = e * nch + c
                            s.wait_ge(smm.h, cb_m + 1)
                            if i >= 1:
                                e2, c2 = divmod(i - 1, nch)
                                s.wait_ge(sev.h, e0 + EC * e2 + EV * c2 + 8)
                            s.activation(bc[i % 2][:], pbc[:], AF.Copy
                                         ).then_inc(sac.h, 1)
                            for ft in range(FT):
                                s.wait_ge(smm.h, cb_m + 2 * ft + 2)
                                if ft >= 1:
                                    s.wait_ge(sev.h, e0 + EC * e + EV * c + 8 + ft)
                                elif i >= 1:
                                    pv = (e0 + EC * e + EV * (c - 1) + 8 + 16
                                          if c >= 1 else
                                          e0 + EC * (e - 1) + EV * (nch - 1) + 8 + 16)
                                    s.wait_ge(sev.h, pv)
                                s.activation(hs[ft % 2][:], pg[ft % 2][:], AF.Silu
                                             ).then_inc(sac.h, 1)
                    s.wait_ge(sev.h, e0 + EC * NEX)
                    for dt in range(DT):
                        s.activation(g_ct[:, dt * L:(dt + 1) * L],
                                     g_ct[:, dt * L:(dt + 1) * L], AF.Silu
                                     ).then_inc(sac.h, 1)

                @blk.vector
                def _(v):
                    for e in range(NEX):
                        for c in range(nch):
                            cb_m = m0 + MC * e + MV * c
                            cb_e = e0 + EC * e + EV * c
                            cb_a = a0 + AC * e + AV * c
                            i = e * nch + c
                            v.wait_ge(slp[i % 2].h, lp0[i % 2] + 16 * (i // 2 + 1))
                            v.wait_ge(sac.h, cb_a + 1)
                            for dt in range(DT):
                                v.tensor_tensor(xs[i % 2][:, dt * TC:(dt + 1) * TC],
                                                xs[i % 2][:, dt * TC:(dt + 1) * TC],
                                                bc[i % 2][:], ALU.mult
                                                ).then_inc(sev.h, 1)
                            if i >= 1:
                                v.wait_ge(smm.h, m0 + MC * e + MV * (c - 1) + 41
                                          if c >= 1 else m0 + MC * e)
                            for ft in range(FT):
                                v.wait_ge(sac.h, cb_a + 1 + ft + 1)
                                v.wait_ge(smm.h, cb_m + 2 * ft + 3)
                                v.tensor_tensor(h_all[:, ft * TC:(ft + 1) * TC],
                                                hs[ft % 2][:], pu[ft % 2][:],
                                                ALU.mult).then_inc(sev.h, 1)
                            if i == 0:
                                v.wait_ge(sgc.h, g0 + 1)
                            for ot in range(DT):
                                v.wait_ge(smm.h, cb_m + 33 + ot + 1)
                                sl = g_ct[:, ot * L + c * TC: ot * L + c * TC + TC]
                                v.tensor_tensor(sl, sl, pd[ot % 2][:], ALU.add
                                                ).then_inc(sev.h, 1)

            slp[0].v = lp0[0] + 16 * ((NEX * nch + 1) // 2)
            slp[1].v = lp0[1] + 16 * ((NEX * nch) // 2)
            for par in range(2):
                n_e = (NEX + 1) // 2 if par == 0 else NEX // 2
                sgp[par].v = gp0[par] + 32 * n_e
                sdn[par].v = dn0[par] + 16 * n_e
            smm.v = m0 + MC * NEX
            sev.v = e0 + EC * NEX
            sac.v = a0 + AC * NEX + DT
        moestk.close()

    def emit_combine(self, b, s_src, g_ct, x_src, dst):
        nc = self.nc
        smm, sev, sac = self.s_mm, self.s_ev, self.s_ac
        slp, slp2, ssp = self.s_lp, self.s_lp2, self.s_sp
        n = NCH
        with ExitStack() as stk:
            def SB(nm, shape, dt_):
                return stk.enter_context(nc.sbuf_tensor(self.nm(nm), shape, dt_))

            def PS(nm, shape, dt_=F32):
                return stk.enter_context(nc.psum_tensor(self.nm(nm), shape, dt_))

            scn = [SB(f"e_s{i}", [P, D], BF16) for i in range(2)]
            xtl = [SB(f"e_x{i}", [P, D], F32) for i in range(2)]
            pr = [SB(f"e_pr{i}", [P, D], BF16) for i in range(2)]
            sgt = [SB(f"e_sgt{i}", [P, D], BF16) for i in range(2)]
            x1t = [SB(f"e_x1{i}", [P, D], F32) for i in range(2)]
            ptr = PS("e_ptr", [P, D], BF16)
            m0, e0, a0 = smm.v, sev.v, sac.v
            lp0 = [s.v for s in slp]
            lq0 = [s.v for s in slp2]
            sp0 = [s.v for s in ssp]
            MV, EV, AV = 1, 2, 1

            with nc.Block(f"cmb_{b}") as blk:
                @blk.sync
                def _(sy):
                    for i in range(n):
                        t0 = b * L + i * P
                        if i >= 2:
                            sy.wait_ge(sev.h, e0 + EV * (i - 2) + 1)
                        sy.dma_start(
                            scn[i % 2].ap().rearrange("p (dt t) -> p dt t", dt=DT),
                            s_src.rearrange("dt p t -> p dt t")[:, :, t0:t0 + P]
                        ).then_inc(slp[i % 2].h, 16)
                        if i >= 2:
                            sy.wait_ge(sev.h, e0 + EV * (i - 2) + 2)
                        sy.dma_start(xtl[i % 2][:], x_src[b, i * P:(i + 1) * P, :]
                                     ).then_inc(slp2[i % 2].h, 16)
                    for par in range(2):
                        sy.wait_ge(ssp[par].h, sp0[par] + 16 * (n // 2))

                @blk.vector
                def _(v):
                    for i in range(n):
                        v.wait_ge(slp[i % 2].h, lp0[i % 2] + 16 * (i // 2 + 1))
                        if i >= 2:
                            v.wait_ge(smm.h, m0 + MV * (i - 2) + 1)
                        for dt in range(DT):
                            ins = v.tensor_tensor(
                                pr[i % 2][:, dt * P:(dt + 1) * P],
                                scn[i % 2][:, dt * P:(dt + 1) * P],
                                g_ct[:, dt * L + i * P: dt * L + (i + 1) * P],
                                ALU.mult)
                            if dt == DT - 1:
                                ins.then_inc(sev.h, 1)
                        v.wait_ge(slp2[i % 2].h, lq0[i % 2] + 16 * (i // 2 + 1))
                        v.wait_ge(sac.h, a0 + AV * i + 1)
                        if i >= 2:
                            v.wait_ge(ssp[i % 2].h, sp0[i % 2] + 16 * ((i - 2) // 2 + 1))
                        v.tensor_tensor(x1t[i % 2][:], xtl[i % 2][:], sgt[i % 2][:],
                                        ALU.add).then_inc(sev.h, 1)

                @blk.tensor
                def _(t):
                    for i in range(n):
                        t.wait_ge(sev.h, e0 + EV * i + 1)
                        if i >= 2:
                            pass
                        if i >= 1:
                            t.wait_ge(sac.h, a0 + AV * (i - 1) + 1)
                        for dt in range(DT):
                            ins = t.transpose(ptr[:, dt * P:(dt + 1) * P],
                                              pr[i % 2][:, dt * P:(dt + 1) * P],
                                              self.ident[:])
                            if dt == DT - 1:
                                ins.then_inc(smm.h, 1)

                @blk.scalar
                def _(s):
                    for i in range(n):
                        s.wait_ge(smm.h, m0 + MV * i + 1)
                        if i >= 2:
                            s.wait_ge(sev.h, e0 + EV * (i - 2) + 2)
                        s.activation(sgt[i % 2][:], ptr[:], AF.Copy
                                     ).then_inc(sac.h, 1)

                @blk.gpsimd
                def _(g):
                    for i in range(n):
                        g.wait_ge(sev.h, e0 + EV * i + 2)
                        g.dma_start(dst[b, i * P:(i + 1) * P, :], x1t[i % 2][:]
                                    ).then_inc(ssp[i % 2].h, 16)

            for par in range(2):
                slp[par].v = lp0[par] + 16 * (n // 2)
                slp2[par].v = lq0[par] + 16 * (n // 2)
                ssp[par].v = sp0[par] + 16 * (n // 2)
            smm.v = m0 + MV * n
            sev.v = e0 + EV * n
            sac.v = a0 + AV * n

    def emit_conv_tables(self, lstk):
        nc = self.nc
        sev, sac, smm, sg = self.s_ev, self.s_ac, self.s_mm, self.s_g
        svc, ssc = self.s_vch, self.s_sch
        sgl = self.s_gld
        NM = NMODE

        def LSB(nm, shape, dt_):
            return lstk.enter_context(nc.sbuf_tensor(self.nm(nm), shape, dt_))

        self.Tre = LSB("Tre", [NM, 132], F32)
        self.Tim = LSB("Tim", [NM, 132], F32)
        self.Trev = LSB("Trev", [NM, 2 * P], F32)
        self.Fstack = LSB("Fstack", [P, P], BF16)
        self.Estack = LSB("Estack", [P, P], BF16)
        self.A_bd = LSB("A_bd", [P, P], F32)
        self.cstack = LSB("cstack", [P, D], F32)
        self.k_ext = LSB("k_ext", [P, DT * 256], BF16)
        with ExitStack() as stk:
            def SB(nm, shape, dt_):
                return stk.enter_context(nc.sbuf_tensor(self.nm(nm), shape, dt_))

            def PS(nm, shape, dt_=F32):
                return stk.enter_context(nc.psum_tensor(self.nm(nm), shape, dt_))

            fr = SB("t_fr", [NM, 8], F32)
            tt = SB("t_tt", [NM, 140], F32)
            ipb = SB("t_ip", [P, DT * NM], F32)
            opb = SB("t_op", [NM, D], F32)
            cct = SB("t_cct", [P, DT * NM], F32)
            io_r = SB("t_ior", [P, P], F32)
            io_c = SB("t_ioc", [P, 1], F32)
            oc = SB("t_oc", [P, 1], F32)
            are2 = SB("t_are2", [P, 1], F32)
            aim2 = SB("t_aim2", [P, 1], F32)
            negim = SB("t_negim", [NM, 132], F32)
            pt1 = PS("t_pt1", [P, P])
            pt2 = PS("t_pt2", [P, 256])
            l0 = sgl.v
            g0 = sg.v
            e0, a0, m0 = sev.v, sac.v, smm.v

            with nc.Block("ctab") as blk:
                @blk.gpsimd
                def _(g):
                    g.iota(io_r[:], [[1, P]], channel_multiplier=0,
                           allow_small_or_imprecise_dtypes=True)
                    g.iota(io_c[:], [[0, 1]], channel_multiplier=1,
                           allow_small_or_imprecise_dtypes=True)
                    g.memset(self.A_bd[:], 0.0)
                    g.memset(self.k_ext[:], 0.0)
                    g.memset(self.Tre[:], 0.0)
                    g.memset(self.Tim[:], 0.0).then_inc(sg.h, 1)
                    g.dma_start(fr[:, 0:1], self.w["frequencies"][:]).then_inc(sgl.h, 16)
                    g.dma_start(fr[:, 5:6], self.w["decays"][:]).then_inc(sgl.h, 16)
                    g.dma_start(ipb.ap().rearrange("p (dt n) -> p dt n", dt=DT),
                                self.w["in_proj"].rearrange("(dt p) n -> p dt n", p=P)
                                ).then_inc(sgl.h, 16)
                    g.dma_start(opb[:], self.w["out_proj"]).then_inc(sgl.h, 16)
                    g.wait_ge(sev.h, e0 + 3)
                    g.dma_start(oc[0:NM, :], io_c[NM:2 * NM, :]).then_inc(sgl.h, 16)
                    g.dma_start(oc[NM:2 * NM, :], io_c[0:NM, :]).then_inc(sgl.h, 16)
                    g.wait_ge(sev.h, e0 + 20)
                    g.dma_start(are2[0:NM, :], self.Tre[:, 128:129]).then_inc(sgl.h, 16)
                    g.dma_start(are2[NM:2 * NM, :], self.Tre[:, 128:129]).then_inc(sgl.h, 16)
                    g.dma_start(aim2[0:NM, :], self.Tim[:, 128:129]).then_inc(sgl.h, 16)
                    g.dma_start(aim2[NM:2 * NM, :], negim[:, 128:129]).then_inc(sgl.h, 16)
                    g.dma_start(self.Estack[NM:2 * NM, :], negim[:, 1:129]
                                ).then_inc(sgl.h, 16)
                    g.wait_ge(sev.h, e0 + 37)
                    g.dma_start(self.cstack[NM:2 * NM, :], self.cstack[0:NM, :]
                                ).then_inc(sgl.h, 16)
                    g.wait_ge(sgl.h, l0 + 16 * 12)

                @blk.vector
                def _(v):
                    v.wait_ge(sgl.h, l0 + 64)
                    v.wait_ge(sg.h, g0 + 1)
                    vch = self.s_vch

                    def vsync(ins):
                        ins.then_inc(vch.h, 1)
                        vch.v += 1
                        v.wait_ge(vch.h, vch.v)

                    v.tensor_scalar(fr[:, 1:2], fr[:, 0:1], TWO_PI, None, ALU.mult)
                    vsync(v.tensor_scalar(fr[:, 2:3], fr[:, 0:1], MAGIC, MAGIC,
                                          ALU.add, ALU.subtract))
                    v.tensor_copy(tt[:, 129:130], self.cst[0:NM, 0:1]).then_inc(sev.h, 1)
                    vsync(v.scalar_tensor_tensor(fr[:, 3:4], fr[:, 2:3], -CW1,
                                                 fr[:, 1:2], ALU.mult, ALU.add))
                    vsync(v.scalar_tensor_tensor(fr[:, 3:4], fr[:, 2:3], -CW2,
                                                 fr[:, 3:4], ALU.mult, ALU.add))
                    vsync(v.scalar_tensor_tensor(fr[:, 3:4], fr[:, 2:3], -CW3,
                                                 fr[:, 3:4], ALU.mult, ALU.add))
                    v.tensor_copy(tt[:, 129:130], self.cst[0:NM, 0:1]).then_inc(sev.h, 1)
                    vsync(v.tensor_scalar(fr[:, 4:5], fr[:, 3:4], math.pi / 2.0,
                                          None, ALU.add))
                    v.tensor_scalar(tt[:, 132:133], fr[:, 4:5], -math.pi, None,
                                    ALU.is_lt)
                    vsync(v.tensor_scalar(tt[:, 133:134], fr[:, 4:5], math.pi, None,
                                          ALU.is_gt))
                    vsync(v.tensor_tensor(tt[:, 132:133], tt[:, 132:133],
                                          tt[:, 133:134], ALU.subtract))
                    v.scalar_tensor_tensor(fr[:, 4:5], tt[:, 132:133], TWO_PI,
                                           fr[:, 4:5], ALU.mult, ALU.add
                                           ).then_inc(sev.h, 1)
                    v.memset(self.Tre[:, 0:1], 1.0)
                    v.wait_ge(sac.h, a0 + 4)
                    v.tensor_tensor(self.Tre[:, 1:2], fr[:, 7:8], tt[:, 130:131],
                                    ALU.mult)
                    v.tensor_tensor(self.Tim[:, 1:2], fr[:, 7:8], tt[:, 131:132],
                                    ALU.mult).then_inc(svc.h, 1)
                    self.s_vch.v += 1
                    nsz = 1
                    vch = self.s_vch
                    while nsz < 128:
                        v.wait_ge(vch.h, vch.v)
                        v.tensor_scalar(tt[:, 128:129], self.Tim[:, nsz:nsz + 1],
                                        -1.0, None, ALU.mult)
                        k = min(nsz, 128 - nsz)
                        v.tensor_scalar(tt[:, 0:k], self.Tre[:, 1:k + 1],
                                        self.Tre[:, nsz:nsz + 1], None, ALU.mult
                                        ).then_inc(vch.h, 1)
                        vch.v += 1
                        v.wait_ge(vch.h, vch.v)
                        v.scalar_tensor_tensor(
                            tt[:, 64:64 + k], self.Tim[:, 1:k + 1],
                            self.Tim[:, nsz:nsz + 1], tt[:, 0:k]
                            if False else None, ALU.mult, ALU.bypass) \
                            if False else None
                        v.scalar_tensor_tensor(
                            self.Tre[:, nsz + 1:nsz + k + 1],
                            self.Tim[:, 1:k + 1], tt[:, 128:129], tt[:, 0:k],
                            ALU.mult, ALU.add)
                        v.tensor_scalar(tt[:, 64:64 + k], self.Tre[:, 1:k + 1],
                                        self.Tim[:, nsz:nsz + 1], None, ALU.mult
                                        ).then_inc(vch.h, 1)
                        vch.v += 1
                        v.wait_ge(vch.h, vch.v)
                        v.scalar_tensor_tensor(
                            self.Tim[:, nsz + 1:nsz + k + 1],
                            self.Tim[:, 1:k + 1], self.Tre[:, nsz:nsz + 1],
                            tt[:, 64:64 + k], ALU.mult, ALU.add).then_inc(vch.h, 1)
                        vch.v += 1
                        nsz = nsz + k
                    v.wait_ge(vch.h, vch.v)
                    v.tensor_scalar(negim[:], self.Tim[:, 0:132], -1.0, None,
                                    ALU.mult)
                    v.tensor_copy(self.Trev[:, 0:P],
                                  bass.AP(self.Tre, 127, [[132, NM], [-1, P]]))
                    v.tensor_copy(self.Trev[:, P:2 * P],
                                  bass.AP(self.Tim, 127, [[132, NM], [-1, P]]))
                    v.tensor_copy(self.Estack[0:NM, :], self.Tre[:, 1:129])
                    v.tensor_copy(tt[:, 129:130], self.cst[0:NM, 0:1]
                                  ).then_inc(sev.h, 17)
                    v.wait_ge(sgl.h, l0 + 16 * 11)
                    v.tensor_scalar(self.A_bd[:], io_r[:], io_c[:, 0:1], are2[:, 0:1],
                                    ALU.is_equal, ALU.mult).then_inc(svc.h, 1)
                    vch.v += 1
                    v.wait_ge(vch.h, vch.v)
                    v.tensor_scalar(tt[0:P, 0:P] if False else io_r[:], io_r[:],
                                    oc[:, 0:1], aim2[:, 0:1], ALU.is_equal, ALU.mult)
                    v.tensor_copy(tt[:, 129:130], self.cst[0:NM, 0:1]).then_inc(svc.h, 1)
                    vch.v += 1
                    v.wait_ge(vch.h, vch.v)
                    v.tensor_tensor(self.A_bd[:], self.A_bd[:], io_r[:], ALU.add
                                    ).then_inc(sev.h, 1)
                    for dt in range(DT):
                        v.wait_ge(smm.h, m0 + dt + 1)
                        v.tensor_tensor(cct[:, dt * NM:(dt + 1) * NM],
                                        ipb[:, dt * NM:(dt + 1) * NM],
                                        pt1[:, 0:NM], ALU.mult).then_inc(sev.h, 1)
                    for dt in range(DT):
                        v.wait_ge(smm.h, m0 + DT + dt + 1)
                        v.tensor_copy(self.cstack[0:NM, dt * P:(dt + 1) * P],
                                      pt2[0:NM, 0:P]).then_inc(sev.h, 1)

                @blk.scalar
                def _(s):
                    s.wait_ge(sgl.h, l0 + 64)
                    s.activation(fr[:, 6:7], fr[:, 5:6], AF.Exp).then_inc(ssc.h, 1)
                    self.s_sch.v += 1
                    s.wait_ge(ssc.h, self.s_sch.v)
                    s.activation(fr[:, 7:8], fr[:, 6:7], AF.Exp, scale=-1.0
                                 ).then_inc(sac.h, 1)
                    s.wait_ge(sev.h, e0 + 2)
                    s.activation(tt[:, 131:132], fr[:, 3:4], AF.Sin).then_inc(sac.h, 1)
                    s.wait_ge(sev.h, e0 + 3)
                    s.activation(tt[:, 130:131], fr[:, 4:5], AF.Sin).then_inc(sac.h, 2)

                @blk.tensor
                def _(t):
                    t.wait_ge(sgl.h, l0 + 64)
                    for dt in range(DT):
                        if dt >= 1:
                            t.wait_ge(sev.h, e0 + 20 + 1 + dt)
                        t.transpose(pt1[:, 0:NM], opb[:, dt * P:(dt + 1) * P],
                                    self.identf[0:NM, 0:NM]).then_inc(smm.h, 1)
                    for dt in range(DT):
                        t.wait_ge(sev.h, e0 + 21 + dt + 1)
                        if dt >= 1:
                            t.wait_ge(sev.h, e0 + 29 + dt)
                        t.transpose(pt2[0:NM, 0:P], cct[:, dt * NM:(dt + 1) * NM],
                                    self.identf[:]).then_inc(smm.h, 1)

            sev.v = e0 + 20 + 1 + 2 * DT
            sac.v = a0 + 4
            smm.v = m0 + 2 * DT
            sgl.v = l0 + 16 * 12
            sg.v = g0 + 1

        with ExitStack() as stk:
            pt3 = stk.enter_context(nc.psum_tensor(self.nm("t_pt3"), [P, P], F32))
            pk = stk.enter_context(nc.psum_tensor(self.nm("t_pk"), [P, P], F32))
            krev = stk.enter_context(nc.sbuf_tensor(self.nm("t_krev"), [P, DT * 256], BF16))
            e0, a0, m0 = sev.v, sac.v, smm.v
            with nc.Block("ctab2") as blk:
                @blk.tensor
                def _(t):
                    t.transpose(pt3[:, 0:NM], self.Trev[:, 0:P],
                                self.identf[0:NM, 0:NM]).then_inc(smm.h, 1)
                    t.wait_ge(sev.h, e0 + 1)
                    t.transpose(pt3[:, 0:NM], self.Trev[:, P:2 * P],
                                self.identf[0:NM, 0:NM]).then_inc(smm.h, 1)
                    for dt in range(DT):
                        if dt >= 1:
                            t.wait_ge(sac.h, a0 + dt)
                        t.matmul(pk[:, 0:P], self.cstack[0:NM, dt * P:(dt + 1) * P],
                                 self.Tre[:, 0:P], start=True, stop=True
                                 ).then_inc(smm.h, 1)

                @blk.vector
                def _(v):
                    v.memset(krev[:], 0.0)
                    v.wait_ge(smm.h, m0 + 1)
                    v.tensor_copy(self.Fstack[:, 0:NM], pt3[:, 0:NM]).then_inc(sev.h, 1)
                    v.wait_ge(smm.h, m0 + 2)
                    v.tensor_copy(self.Fstack[:, NM:2 * NM], pt3[:, 0:NM]
                                  ).then_inc(sev.h, 1)
                    v.wait_ge(sac.h, a0 + DT)
                    v.tensor_copy(
                        bass.AP(krev, 0, [[DT * 256, P], [256, DT], [1, 255]]),
                        bass.AP(self.k_ext, 254, [[DT * 256, P], [256, DT], [-1, 255]])
                    ).then_inc(sev.h, 1)

                @blk.scalar
                def _(s):
                    for dt in range(DT):
                        s.wait_ge(smm.h, m0 + 2 + dt + 1)
                        s.activation(self.k_ext[:, dt * 256 + 127: dt * 256 + 255],
                                     pk[:, 0:P], AF.Copy).then_inc(sac.h, 1)

                @blk.gpsimd
                def _(g):
                    g.wait_ge(sev.h, e0 + 3)
                    g.dma_start(
                        self.k_dram.rearrange("(dt p) j -> p dt j", p=P),
                        krev.ap().rearrange("p (dt j) -> p dt j", dt=DT)
                    ).then_inc(self.s_gld.h, 16)
                    g.wait_ge(self.s_gld.h, self.s_gld.v + 16)
            self.s_gld.v += 16
            sev.v = e0 + 3
            sac.v = a0 + DT
            smm.v = m0 + 2 + DT

    def emit_conv(self, b, xn_tc, y_ct, y_spill):
        nc = self.nc
        smm, sev, sac = self.s_mm, self.s_ev, self.s_ac
        sgp, ssp = self.s_gp, self.s_sp
        with ExitStack() as stk:
            def SB(nm, shape, dt_):
                return stk.enter_context(nc.sbuf_tensor(self.nm(nm), shape, dt_))

            def PS(nm, shape, dt_=F32):
                return stk.enter_context(nc.psum_tensor(self.nm(nm), shape, dt_))

            S_sb = SB("cv_S", [P, D], F32)
            PM = SB("cv_PM", [P, D], BF16)
            Ibuf = SB("cv_I", [P, P * NCH], BF16)
            Kb8 = [SB(f"cv_K{j}", [P, 16 * P], BF16) for j in range(8)]
            pstate = PS("cv_ps", [P, D])
            py = PS("cv_py", [P, D])
            pI = [PS(f"cv_pI{i}", [P, 512]) for i in range(2)]
            ptb = [PS(f"cv_pt{i}", [P, P], BF16) for i in range(2)]
            m0, e0, a0 = smm.v, sev.v, sac.v
            g0 = sgp[0].v, sgp[1].v

            with nc.Block(f"cvB1_{b}") as blk:
                @blk.gpsimd
                def _(g):
                    g.memset(S_sb[:], 0.0)
                    g.memset(PM[:], 0.0).then_inc(self.s_gch.h, 1)
                    self.s_gch.v += 1

                gch0 = self.s_gch.v

                @blk.tensor
                def _(t):
                    t.wait_ge(self.s_gch.h, gch0)
                    for c in range(NCH):
                        if c >= 1:
                            t.wait_ge(sev.h, e0 + 3 * c)
                            t.wait_ge(sac.h, a0 + c)
                        for ht in range(DT):
                            ins = t.matmul(py[:, ht * P:(ht + 1) * P],
                                           PM[:, ht * P:(ht + 1) * P],
                                           self.Estack[:], start=True, stop=True)
                            if ht == DT - 1:
                                ins.then_inc(smm.h, 1)
                        if c >= 1:
                            t.wait_ge(sev.h, e0 + 3 * c)
                        for half in range(2):
                            t.matmul(pstate[:, half * 512:(half + 1) * 512],
                                     self.A_bd[:], S_sb[:, half * 512:(half + 1) * 512],
                                     start=True, stop=False)
                            ins = t.matmul(pstate[:, half * 512:(half + 1) * 512],
                                           self.Fstack[:],
                                           xn_tc[:, c * D + half * 512:
                                                 c * D + (half + 1) * 512],
                                           start=False, stop=True)
                            if half == 1:
                                ins.then_inc(smm.h, 1)

                @blk.scalar
                def _(s):
                    for c in range(NCH):
                        s.wait_ge(smm.h, m0 + 2 * c + 1)
                        s.activation(
                            y_ct.ap().rearrange("p (dt t) -> p dt t", dt=DT)
                                [:, :, c * P:(c + 1) * P],
                            py.ap().rearrange("p (dt q) -> p dt q", dt=DT),
                            AF.Copy).then_inc(sac.h, 1)

                @blk.vector
                def _(v):
                    for c in range(NCH):
                        v.wait_ge(smm.h, m0 + 2 * c + 2)
                        v.tensor_tensor(PM[:], self.cstack[:], pstate[:], ALU.mult
                                        ).then_inc(sev.h, 1)
                        v.tensor_copy(S_sb[:], pstate[:]).then_inc(sev.h, 2)
            smm.v = m0 + 2 * NCH
            sac.v = a0 + NCH
            sev.v = e0 + 3 * NCH

            m0, e0, a0 = smm.v, sev.v, smm.v
            gp0 = [s.v for s in sgp]
            with nc.Block(f"cvB2_{b}") as blk:
                NG = DT
                NB16 = P // 16
                MVg = NB16 + NCH
                EVg = NB16 + NCH

                @blk.gpsimd
                def _(g):
                    for gi in range(NG):
                        if gi >= 1:
                            g.wait_ge(smm.h, m0 + MVg * (gi - 1) + NB16)
                        for hh in range(8):
                            src = bass.AP(
                                self.k_dram_h,
                                (gi * P + hh * 16) * 256,
                                [[1, P], [256, 16], [1, P]])
                            dst = bass.AP(
                                Kb8[hh], 0,
                                [[16 * P, P], [P, 16], [1, P]])
                            g.dma_start(dst, src).then_inc(sgp[gi % 2].h, 16)

                @blk.tensor
                def _(t):
                    for gi in range(NG):
                        gb_m = m0 + MVg * gi
                        gb_e = e0 + EVg * gi
                        t.wait_ge(sgp[gi % 2].h, gp0[gi % 2] + 16 * 8 * (gi // 2 + 1))
                        for b16 in range(NB16):
                            if b16 >= 2 or gi >= 1:
                                pv = (gb_e + b16 - 2 + 1) if b16 >= 2 else \
                                     (e0 + EVg * (gi - 1) + b16 + 6 + 1)
                                t.wait_ge(sev.h, pv)
                            for hh in range(16):
                                h0 = b16 * 16 + hh
                                rhs = bass.AP(xn_tc, gi * P + h0,
                                              [[NCH * D, P], [D, NCH]])
                                ins = t.matmul(
                                    pI[b16 % 2][:, hh * NCH:(hh + 1) * NCH],
                                    Kb8[b16][:, hh * P:(hh + 1) * P], rhs,
                                    start=True, stop=True)
                                if hh == 15:
                                    ins.then_inc(smm.h, 1)
                        t.wait_ge(sev.h, gb_e + NB16)
                        for c in range(NCH):
                            if c >= 2 or gi >= 1:
                                pv = (gb_e + NB16 + c - 2 + 1) if c >= 2 else \
                                     (e0 + EVg * (gi - 1) + NB16 + c + 30 + 1)
                                t.wait_ge(sev.h, pv)
                            ins = t.transpose(
                                ptb[c % 2][:],
                                bass.AP(Ibuf, c, [[P * NCH, P], [NCH, P]]),
                                self.ident[:])
                            ins.then_inc(smm.h, 1)

                @blk.vector
                def _(v):
                    for gi in range(NG):
                        gb_m = m0 + MVg * gi
                        gb_e = e0 + EVg * gi
                        for b16 in range(NB16):
                            v.wait_ge(smm.h, gb_m + b16 + 1)
                            if gi >= 1 and b16 == 0:
                                v.wait_ge(smm.h, m0 + MVg * (gi - 1) + MVg)
                            v.tensor_copy(Ibuf[:, b16 * 16 * NCH:(b16 + 1) * 16 * NCH],
                                          pI[b16 % 2][:]).then_inc(sev.h, 1)
                        for c in range(NCH):
                            v.wait_ge(smm.h, gb_m + NB16 + c + 1)
                            sl = y_ct[:, gi * L + c * P: gi * L + (c + 1) * P]
                            v.tensor_tensor(
                                sl, sl,
                                bass.AP(ptb[c % 2], 127, [[P, P], [-1, P]]),
                                ALU.add).then_inc(sev.h, 1)
            smm.v = m0 + MVg * NG
            sev.v = e0 + EVg * NG
            for par in range(2):
                sgp[par].v = gp0[par] + 16 * 8 * (NG // 2)

            sp0 = ssp[b % 2].v
            with nc.Block(f"cvB3_{b}") as blk:
                @blk.gpsimd
                def _(g):
                    g.dma_start(
                        y_spill.rearrange("dt p t -> dt p t")[:, :, b * L:(b + 1) * L]
                            .rearrange("dt p t -> p dt t"),
                        y_ct.ap().rearrange("p (dt t) -> p dt t", dt=DT)
                    ).then_inc(ssp[b % 2].h, 16)

                @blk.sync
                def _(sy):
                    sy.wait_ge(ssp[b % 2].h, sp0 + 16)
            ssp[b % 2].v = sp0 + 16

    def emit_wkv_tables(self):
        nc = self.nc
        sev, sac, smm, sgl = self.s_ev, self.s_ac, self.s_mm, self.s_gld
        vch, sch = self.s_vch, self.s_sch
        self.WmT = self.sb("WmT", [P, NH * 2 * RQ], BF16)
        self.wbrow = self.sb("wbrow", [P, NH * RQ], BF16)
        self.wbcol = self.sb("wbcol", [P, 2 * NH], F32)
        self.wkcol = self.sb("wkcol", [P, 2 * NH], F32)
        self.wscol = self.sb("wscol", [P, NH], F32)
        with ExitStack() as stk:
            def SB(nm, shape, dt_):
                return stk.enter_context(nc.sbuf_tensor(self.nm(nm), shape, dt_))

            tdr = SB("w_tdr", [1, NH], F32)
            tfr = SB("w_tfr", [1, NH], F32)
            lnw = SB("w_lnw", [P, NH], F32)
            nlw = SB("w_nlw", [P, NH], F32)
            ubc = SB("w_ubc", [P, NH], F32)
            io2 = SB("w_io2", [P, P], F32)
            iq = SB("w_iq", [P, RQ], F32)
            pcol = SB("w_pcol", [P, 2], F32)
            Mt = SB("w_M", [P, RQ], F32)
            Mc = SB("w_Mc", [P, RQ], F32)
            W1 = SB("w_W1", [P, RQ], F32)
            m1 = SB("w_m1", [P, RQ], F32)
            pb = stk.enter_context(nc.psum_tensor(self.nm("w_pb"), [P, NH], F32))
            l0, e0, a0, m0 = sgl.v, sev.v, sac.v, smm.v
            g0 = self.s_g.v

            with nc.Block("wtab") as blk:
                @blk.gpsimd
                def _(g):
                    g.dma_start(tdr[:], self.w["t_decay"][:]).then_inc(sgl.h, 16)
                    g.dma_start(tfr[:], self.w["t_faaaa"][:]).then_inc(sgl.h, 16)
                    g.iota(iq[:], [[1, RQ]], channel_multiplier=0,
                           allow_small_or_imprecise_dtypes=True)
                    g.iota(pcol[:, 0:1], [[0, 1]], channel_multiplier=1,
                           allow_small_or_imprecise_dtypes=True)
                    g.iota(pcol[:, 1:2], [[0, 1]], base=P, channel_multiplier=1,
                           allow_small_or_imprecise_dtypes=True).then_inc(self.s_g.h, 1)

                @blk.scalar
                def _(s):
                    s.wait_ge(sgl.h, l0 + 32)
                    s.activation(tdr[:], tdr[:], AF.Exp, scale=1.0
                                 ).then_inc(sch.h, 1)
                    self.s_sch.v += 1
                    s.wait_ge(sch.h, self.s_sch.v)
                    s.activation(tdr[:], tdr[:], AF.Copy, scale=-1.0
                                 ).then_inc(sac.h, 1) if False else None
                    s.mul(tdr[:], tdr[:], -1.0).then_inc(sac.h, 1)

                @blk.tensor
                def _(t):
                    t.wait_ge(sac.h, a0 + 1)
                    t.matmul(pb[:], self.ones_colf[:], tdr[:], start=True, stop=True
                             ).then_inc(smm.h, 1)
                    t.wait_ge(sev.h, e0 + 1)
                    t.matmul(pb[:], self.ones_colf[:], tfr[:], start=True, stop=True
                             ).then_inc(smm.h, 1)

                @blk.vector
                def _(v):
                    v.wait_ge(smm.h, m0 + 1)
                    v.tensor_copy(lnw[:], pb[:])
                    v.tensor_scalar(nlw[:], pb[:], -1.0, None, ALU.mult
                                    ).then_inc(sev.h, 1)
                    v.wait_ge(smm.h, m0 + 2)
                    v.tensor_copy(ubc[:], pb[:]).then_inc(sev.h, 1)
            sgl.v = l0 + 32
            sac.v = a0 + 1
            smm.v = m0 + 2
            sev.v = e0 + 2
            self.s_g.v = g0 + 1

            e0, a0 = sev.v, sac.v
            with nc.Block("wtab2") as blk:
                @blk.vector
                def _(v):
                    v.wait_ge(self.s_g.h, g0 + 1)
                    i = 0
                    for h in range(NH):
                        for pt in range(2):
                            if i >= 1:
                                v.wait_ge(sac.h, a0 + 2 * i)
                            v.tensor_scalar(Mt[:], iq[:], pcol[:, pt:pt + 1], None,
                                            ALU.subtract).then_inc(sev.h, 1)
                            v.wait_ge(sev.h, e0 + 3 * i + 1)
                            v.tensor_scalar(Mc[:], Mt[:], 0.0, None, ALU.max
                                            ).then_inc(sev.h, 1)
                            v.tensor_scalar(m1[:], Mt[:], 0.5, None, ALU.is_gt
                                            ).then_inc(self.s_vch.h, 1)
                            self.s_vch.v += 1
                            v.wait_ge(self.s_vch.h, self.s_vch.v)
                            v.wait_ge(sac.h, a0 + 2 * i + 2)
                            v.tensor_tensor(W1[:], W1[:], m1[:], ALU.mult
                                            ).then_inc(self.s_vch.h, 1)
                            self.s_vch.v += 1
                            v.wait_ge(self.s_vch.h, self.s_vch.v)
                            v.tensor_scalar(m1[:], Mt[:], 0.0, ubc[:, h:h + 1],
                                            ALU.is_equal, ALU.mult
                                            ).then_inc(self.s_vch.h, 1)
                            self.s_vch.v += 1
                            v.wait_ge(self.s_vch.h, self.s_vch.v)
                            v.tensor_tensor(
                                self.WmT[:, (h * 2 + pt) * RQ:(h * 2 + pt + 1) * RQ],
                                W1[:], m1[:], ALU.add).then_inc(sev.h, 1)
                            i += 1
                    v.wait_ge(sac.h, a0 + 2 * NH * 2)
                    v.wait_ge(sev.h, e0 + 3 * NH * 2)
                    v.tensor_scalar(W1[:, 0:NH], lnw[:], 255.0, None, ALU.mult
                                    ).then_inc(sev.h, 1)

                @blk.scalar
                def _(s):
                    i = 0
                    for h in range(NH):
                        for pt in range(2):
                            s.wait_ge(sev.h, e0 + 3 * i + 2)
                            s.activation(W1[:], Mc[:], AF.Exp,
                                         scale=lnw[:, h:h + 1], bias=nlw[:, h:h + 1]
                                         ).then_inc(sac.h, 2)
                            i += 1
                    s.wait_ge(sev.h, e0 + 3 * NH * 2 + 1)
                    for h in range(NH):
                        for pt in range(2):
                            s.activation(self.wbcol[:, h * 2 + pt:h * 2 + pt + 1],
                                         pcol[:, pt:pt + 1], AF.Exp,
                                         scale=lnw[:, h:h + 1])
                            s.activation(self.wkcol[:, h * 2 + pt:h * 2 + pt + 1],
                                         pcol[:, pt:pt + 1], AF.Exp,
                                         scale=nlw[:, h:h + 1],
                                         bias=W1[:, h:h + 1])
                        s.activation(self.wscol[:, h:h + 1], lnw[:, h:h + 1],
                                     AF.Exp, scale=256.0)
                        ins = s.activation(self.wbrow[:, h * RQ:(h + 1) * RQ],
                                           iq[:], AF.Exp, scale=lnw[:, h:h + 1])
                        if h == NH - 1:
                            ins.then_inc(sac.h, 1)
            sev.v = e0 + 3 * NH * 2 + 1
            sac.v = a0 + 2 * NH * 2 + 1

    def emit_lerp_proj(self, b, xn1_ct, wT, rkv):
        nc = self.nc
        smm, sev, sac = self.s_mm, self.s_ev, self.s_ac
        ssp = self.s_sp
        TC = 512
        nch = L // TC
        with ExitStack() as stk:
            def SB(nm, shape, dt_):
                return stk.enter_context(nc.sbuf_tensor(self.nm(nm), shape, dt_))

            def PS(nm, shape, dt_=F32):
                return stk.enter_context(nc.psum_tensor(self.nm(nm), shape, dt_))

            om = SB("g1_om", [P, 3 * DT], F32)
            xl = [SB(f"g1_x{i}", [P, DT * TC], BF16) for i in range(3)]
            tsc = SB("g1_ts", [P, 2 * TC], F32)
            stg = [SB(f"g1_s{i}", [P, DT * TC], BF16) for i in range(2)]
            pp = [PS(f"g1_p{i}", [P, TC]) for i in range(2)]
            m0, e0, a0 = smm.v, sev.v, sac.v
            sp0 = [s.v for s in ssp]
            names = ["maa_k", "maa_v", "maa_r"]
            bn = ["bk", "bv", "br"]
            MV, EV, AV = 24, 48, 24

            with nc.Block(f"g1_{b}") as blk:
                @blk.vector
                def _(v):
                    for j in range(3):
                        v.tensor_scalar(om[:, j * DT:(j + 1) * DT],
                                        self.vecs[names[j]][:], -1.0, 1.0,
                                        ALU.mult, ALU.add)
                    v.tensor_copy(tsc[:, 0:1], self.cst[:, 0:1]).then_inc(sev.h, 1)
                    for c in range(nch):
                        ce = e0 + 1 + EV * c
                        if c >= 1:
                            v.wait_ge(smm.h, m0 + MV * c)
                        for j in range(3):
                            for dt in range(DT):
                                ii = j * DT + dt
                                ts2 = tsc[:, (ii % 2) * TC:(ii % 2) * TC + TC]
                                if ii >= 2:
                                    v.wait_ge(sev.h, ce + 2 * (ii - 2) + 1)
                                elif c >= 1:
                                    v.wait_ge(sev.h, e0 + 1 + EV * (c - 1)
                                              + 2 * (ii + 22) + 1)
                                base = c * TC
                                xsl = xn1_ct[:, dt * L + base: dt * L + base + TC]
                                mc = self.vecs[names[j]][:, dt:dt + 1]
                                oc = om[:, j * DT + dt: j * DT + dt + 1]
                                dst = xl[j][:, dt * TC:(dt + 1) * TC]
                                if c == 0:
                                    sh = xn1_ct[:, dt * L: dt * L + TC - 1]
                                    v.tensor_scalar(ts2[:, 1:TC], sh, mc, None,
                                                    ALU.mult).then_inc(self.s_vch.h, 1)
                                    self.s_vch.v += 1
                                    v.wait_ge(self.s_vch.h, self.s_vch.v)
                                    v.scalar_tensor_tensor(
                                        dst[:, 1:TC], xsl[:, 1:TC], oc,
                                        ts2[:, 1:TC], ALU.mult, ALU.add
                                        ).then_inc(sev.h, 1)
                                    v.tensor_scalar(dst[:, 0:1], xsl[:, 0:1], oc,
                                                    None, ALU.mult).then_inc(sev.h, 1)
                                else:
                                    sh = xn1_ct[:, dt * L + base - 1:
                                                dt * L + base + TC - 1]
                                    v.tensor_scalar(ts2, sh, mc, None,
                                                    ALU.mult).then_inc(self.s_vch.h, 1)
                                    self.s_vch.v += 1
                                    v.wait_ge(self.s_vch.h, self.s_vch.v)
                                    v.scalar_tensor_tensor(
                                        dst, xsl, oc, ts2, ALU.mult, ALU.add
                                        ).then_inc(sev.h, 2)

                @blk.tensor
                def _(t):
                    for c in range(nch):
                        cb_e = e0 + 1 + EV * c
                        for j in range(3):
                            t.wait_ge(sev.h, cb_e + (j + 1) * 16)
                            for ot in range(DT):
                                i = c * MV + j * DT + ot
                                if i >= 2:
                                    c2, r2 = divmod(i - 2, MV)
                                    t.wait_ge(sac.h, a0 + AV * c2 + r2 + 1)
                                for it in range(DT):
                                    ins = t.matmul(
                                        pp[ot % 2][:],
                                        wT[:, j * DT * D + it * D + ot * P:
                                           j * DT * D + it * D + (ot + 1) * P],
                                        xl[j][:, it * TC:(it + 1) * TC],
                                        start=(it == 0), stop=(it == DT - 1))
                                    if it == DT - 1:
                                        ins.then_inc(smm.h, 1)

                @blk.scalar
                def _(s):
                    for c in range(nch):
                        cb_m, cb_a = m0 + MV * c, a0 + AV * c
                        for j in range(3):
                            cj = c * 3 + j
                            for ot in range(DT):
                                s.wait_ge(smm.h, cb_m + j * DT + ot + 1)
                                if cj >= 2 and ot == 0:
                                    s.wait_ge(ssp[cj % 2].h,
                                              sp0[cj % 2] + 16 * ((cj - 2) // 2 + 1))
                                s.activation(stg[cj % 2][:, ot * TC:(ot + 1) * TC],
                                             pp[ot % 2][:], AF.Identity, scale=1.0,
                                             bias=self.vecs[bn[j]][:, ot:ot + 1]
                                             ).then_inc(sac.h, 1)

                @blk.gpsimd
                def _(g):
                    for c in range(nch):
                        for j in range(3):
                            cj = c * 3 + j
                            g.wait_ge(sac.h, a0 + AV * c + (j + 1) * DT)
                            t0 = b * L + c * TC
                            g.dma_start(
                                rkv[j].rearrange("dt p t -> p dt t")[:, :, t0:t0 + TC],
                                stg[cj % 2].ap().rearrange(
                                    "p (dt t) -> p dt t", dt=DT)
                            ).then_inc(ssp[cj % 2].h, 16)

                @blk.sync
                def _(sy):
                    for par in range(2):
                        sy.wait_ge(ssp[par].h, sp0[par] + 16 * (nch * 3 // 2))

            for par in range(2):
                ssp[par].v = sp0[par] + 16 * (nch * 3 // 2)
            smm.v = m0 + MV * nch
            sev.v = e0 + 1 + EV * nch
            sac.v = a0 + AV * nch

    def emit_wkv(self, b, rkv, yt_dram):
        nc = self.nc
        smm, sev, sac, sgc = self.s_mm, self.s_ev, self.s_ac, self.s_gch
        slp, slp2, slp3, slp4, ssp = (self.s_lp, self.s_lp2, self.s_lp3,
                                      self.s_lp4, self.s_sp)
        NLT = NB * L
        NHT = NRC * NH
        with ExitStack() as stk:
            def SB(nm, shape, dt_):
                return stk.enter_context(nc.sbuf_tensor(self.nm(nm), shape, dt_))

            def PS(nm, shape, dt_=F32):
                return stk.enter_context(nc.psum_tensor(self.nm(nm), shape, dt_))

            kc = [SB(f"g2_k{i}", [P, DT * RQ], BF16) for i in range(2)]
            rc = [SB(f"g2_r{i}", [P, DT * RQ], BF16) for i in range(2)]
            kTb = [SB(f"g2_kT{i}", [P, 2 * D], BF16) for i in range(2)]
            vTb = [SB(f"g2_vT{i}", [P, 2 * D], BF16) for i in range(2)]
            at2 = [SB(f"g2_at{i}", [P, 2 * RQ], BF16) for i in range(2)]
            kw2 = [SB(f"g2_kw{i}", [P, 2 * HD], BF16) for i in range(2)]
            rb2 = [SB(f"g2_rb{i}", [P, RQ], BF16) for i in range(2)]
            ytg = [SB(f"g2_yt{i}", [P, 2 * D], BF16) for i in range(2)]
            S_sb = SB("g2_S", [P, (NH // 2) * HD], F32)
            S_bf = SB("g2_Sb", [P, (NH // 2) * HD], BF16)
            pAT2 = [PS(f"g2_pAT{i}", [P, 2 * RQ]) for i in range(2)]
            py1_ = [PS(f"g2_py1{i}", [P, 2 * HD]) for i in range(2)]
            pS_ = [PS(f"g2_pS{i}", [P, HD]) for i in range(2)]
            m0, e0, a0 = smm.v, sev.v, sac.v
            lp0 = [s.v for s in slp]
            lq0 = [s.v for s in slp2]
            lr0 = [s.v for s in slp3]
            ls0 = [s.v for s in slp4]
            sp0 = [s.v for s in ssp]
            g0 = sgc.v
            at_p, y2_p, sd_p = {}, {}, {}
            mctr = [m0]
            sev_m, sev_r, sev_s = {}, {}, {}
            cv = e0
            sev_m[0] = cv + 1
            sev_r[0] = cv + 2
            cv += 2
            for G_ in range(NHT):
                if G_ + 1 < NHT:
                    sev_m[G_ + 1] = cv + 1
                    sev_r[G_ + 1] = cv + 2
                    cv += 2
                sev_s[G_] = cv + 1
                cv += 1

            def minc(ins, tab, G):
                ins.then_inc(smm.h, 1)
                mctr[0] += 1
                tab[G] = mctr[0]

            with nc.Block(f"g2_{b}") as blk:
                @blk.tensor
                def _(t):
                    t.wait_ge(sgc.h, g0 + 1)

                    def at_op(G2):
                        c2, h2 = divmod(G2, NH)
                        if h2 == 0:
                            t.wait_ge(slp[c2 % 2].h, lp0[c2 % 2] + 16 * (c2 // 2 + 1))
                            t.wait_ge(slp3[c2 % 2].h, lr0[c2 % 2] + 16 * (c2 // 2 + 1))
                            t.wait_ge(slp4[c2 % 2].h, ls0[c2 % 2] + 256 * (c2 // 2 + 1))
                        if G2 >= 2:
                            t.wait_ge(sev.h, sev_m[G2 - 2])
                        hb2 = (h2 % 2) * HD
                        hc2 = (h2 // 2) * RQ
                        ksl2 = kc[c2 % 2][hb2:hb2 + HD, hc2:hc2 + RQ]
                        rsl2 = rc[c2 % 2][hb2:hb2 + HD, hc2:hc2 + RQ]
                        for mt in range(2):
                            ins = t.matmul(pAT2[h2 % 2][:, mt * RQ:(mt + 1) * RQ],
                                           ksl2[:, mt * P:(mt + 1) * P], rsl2,
                                           start=True, stop=True)
                            if mt == 1:
                                minc(ins, at_p, G2)

                    at_op(0)
                    at_op(1)
                    for G in range(NHT):
                        c, h = divmod(G, NH)
                        hb = (h % 2) * HD
                        hcol = (h // 2) * HD
                        t.wait_ge(sev.h, sev_r[G])
                        if G >= 2:
                            t.wait_ge(sac.h, a0 + 2 * (G - 2) + 2)
                        if c >= 1:
                            t.wait_ge(sgc.h, g0 + (c - 1) * NH + h + 2)
                        for qt in range(2):
                            for kt in range(2):
                                t.matmul(py1_[h % 2][:, qt * HD:(qt + 1) * HD],
                                         at2[h % 2][:, kt * RQ + qt * P:
                                                    kt * RQ + (qt + 1) * P],
                                         vTb[c % 2][:, kt * D + h * HD:
                                                    kt * D + (h + 1) * HD],
                                         start=(kt == 0), stop=False)
                            ins = t.matmul(py1_[h % 2][:, qt * HD:(qt + 1) * HD],
                                           rb2[h % 2][hb:hb + HD, qt * P:(qt + 1) * P],
                                           S_bf[hb:hb + HD, hcol:hcol + HD],
                                           start=False, stop=True)
                            if qt == 1:
                                minc(ins, y2_p, G)
                        t.wait_ge(sac.h, a0 + 2 * G + 1)
                        if G >= 2:
                            t.wait_ge(sev.h, sev_s[G - 2])
                        for qt in range(2):
                            ins = t.matmul(pS_[h % 2][hb:hb + HD, :],
                                           kw2[h % 2][:, qt * HD:(qt + 1) * HD],
                                           vTb[c % 2][:, qt * D + h * HD:
                                                      qt * D + (h + 1) * HD],
                                           start=(qt == 0), stop=(qt == 1),
                                           tile_position=(0, hb))
                            if qt == 1:
                                minc(ins, sd_p, G)
                        if G + 2 < NHT:
                            at_op(G + 2)

                @blk.sync
                def _(sy):
                    for c in range(NRC):
                        t0 = b * L + c * RQ
                        if c >= 2:
                            sy.wait_ge(smm.h, sd_p[(c - 1) * NH - 1])
                            sy.wait_ge(sac.h, a0 + 2 * ((c - 1) * NH - 1) + 1)
                            sy.wait_ge(sev.h, sev_r[(c - 1) * NH - 1])
                        for j, buf, sem in [(0, kc, slp), (2, rc, slp3)]:
                            sy.dma_start(
                                buf[c % 2].ap().rearrange("p (dt t) -> p dt t", dt=DT),
                                rkv[j].rearrange("dt p t -> p dt t")[:, :, t0:t0 + RQ]
                            ).then_inc(sem[c % 2].h, 16)
                        for j, dstb, sem in [(0, kTb, slp2), (1, vTb, slp4)]:
                            for qt in range(2):
                                for dt in range(DT):
                                    o0 = (j * DT + dt) * P * NLT + t0 + qt * P
                                    sy.dma_start(
                                        dstb[c % 2][:, qt * D + dt * P:
                                                    qt * D + (dt + 1) * P],
                                        bass.AP(rkv.tensor, o0, [[NLT, P], [1, P]]),
                                        transpose=True).then_inc(sem[c % 2].h, 16)

                @blk.scalar
                def _(s):
                    s.wait_ge(sgc.h, g0 + 1)
                    for G in range(NHT):
                        c, h = divmod(G, NH)
                        if h == 0:
                            s.wait_ge(slp2[c % 2].h, lq0[c % 2] + 256 * (c // 2 + 1))
                        if G >= 2:
                            s.wait_ge(smm.h, sd_p[G - 2])
                        for qt in range(2):
                            ins = s.activation(
                                kw2[h % 2][:, qt * HD:(qt + 1) * HD],
                                kTb[c % 2][:, qt * D + h * HD: qt * D + (h + 1) * HD],
                                AF.Identity,
                                scale=self.wkcol[:, h * 2 + qt: h * 2 + qt + 1])
                            if qt == 1:
                                ins.then_inc(sac.h, 1)
                        s.wait_ge(smm.h, y2_p[G])
                        if h == 0 and c >= 2:
                            s.wait_ge(ssp[c % 2].h,
                                      sp0[c % 2] + 16 * ((c - 2) // 2 + 1))
                        s.activation(
                            ytg[c % 2].ap().rearrange("p (qt d) -> p qt d", qt=2
                                                      )[:, :, h * HD:(h + 1) * HD],
                            py1_[h % 2][:], AF.Copy).then_inc(sac.h, 1)

                @blk.vector
                def _(v):
                    v.wait_ge(sgc.h, g0 + 1)

                    def vm(G2):
                        c2, h2 = divmod(G2, NH)
                        hb3 = (h2 % 2) * HD
                        hcq = (h2 // 2) * RQ
                        if h2 == 0:
                            v.wait_ge(slp3[c2 % 2].h, lr0[c2 % 2] + 16 * (c2 // 2 + 1))
                        v.wait_ge(smm.h, at_p[G2])
                        if G2 >= 2:
                            v.wait_ge(smm.h, y2_p[G2 - 2])
                        v.tensor_tensor(
                            at2[G2 % 2][:, 0:2 * RQ],
                            pAT2[G2 % 2][:, 0:2 * RQ],
                            self.WmT[:, (h2 * 2) * RQ:(h2 * 2 + 2) * RQ], ALU.mult
                            ).then_inc(sev.h, 1)
                        v.tensor_tensor(
                            rb2[G2 % 2][hb3:hb3 + HD, :],
                            rc[c2 % 2][hb3:hb3 + HD, hcq:hcq + RQ],
                            self.wbrow[hb3:hb3 + HD, h2 * RQ:(h2 + 1) * RQ],
                            ALU.mult).then_inc(sev.h, 1)

                    vm(0)
                    for G in range(NHT):
                        c, h = divmod(G, NH)
                        hb2 = (h % 2) * HD
                        hc2 = (h // 2) * HD
                        if G + 1 < NHT:
                            vm(G + 1)
                        v.wait_ge(smm.h, sd_p[G])
                        v.scalar_tensor_tensor(
                            S_sb[hb2:hb2 + HD, hc2:hc2 + HD],
                            S_sb[hb2:hb2 + HD, hc2:hc2 + HD],
                            self.wscol[hb2:hb2 + HD, h:h + 1], pS_[h % 2][hb2:hb2 + HD, :],
                            ALU.mult, ALU.add).then_inc(sev.h, 1)

                @blk.gpsimd
                def _(g):
                    g.memset(S_sb[:], 0.0)
                    g.memset(S_bf[:], 0.0).then_inc(sgc.h, 1)
                    for c in range(NRC):
                        for h in range(NH):
                            G = c * NH + h
                            hb2 = (h % 2) * HD
                            hc2 = (h // 2) * HD
                            g.wait_ge(sev.h, sev_s[G])
                            g.tensor_copy(S_bf[hb2:hb2 + HD, hc2:hc2 + HD],
                                          S_sb[hb2:hb2 + HD, hc2:hc2 + HD]
                                          ).then_inc(sgc.h, 1)
                        g.wait_ge(sac.h, a0 + 2 * (c + 1) * NH)
                        t0 = b * L + c * RQ
                        g.dma_start(
                            yt_dram[t0:t0 + RQ, :].rearrange("(qt p) d -> p qt d", p=P),
                            ytg[c % 2].ap().rearrange("p (qt d) -> p qt d", qt=2)
                        ).then_inc(ssp[c % 2].h, 16)
                    for par in range(2):
                        g.wait_ge(ssp[par].h, sp0[par] + 16 * (NRC // 2))

            for par in range(2):
                slp[par].v = lp0[par] + 16 * ((NRC + 1) // 2 if par == 0 else NRC // 2)
                slp2[par].v = lq0[par] + 256 * ((NRC + 1) // 2 if par == 0 else NRC // 2)
                slp3[par].v = lr0[par] + 16 * ((NRC + 1) // 2 if par == 0 else NRC // 2)
                slp4[par].v = ls0[par] + 256 * ((NRC + 1) // 2 if par == 0 else NRC // 2)
                ssp[par].v = sp0[par] + 16 * (NRC // 2)
            smm.v = mctr[0]
            sev.v = cv
            sac.v = a0 + 2 * NHT
            sgc.v = g0 + 1 + NHT

    def emit_wo(self, b, yt_dram, woT, s1_spill):
        nc = self.nc
        smm, sev, sac = self.s_mm, self.s_ev, self.s_ac
        slp, ssp = self.s_lp, self.s_sp
        TC = 512
        nch = L // TC
        with ExitStack() as stk:
            def SB(nm, shape, dt_):
                return stk.enter_context(nc.sbuf_tensor(self.nm(nm), shape, dt_))

            def PS(nm, shape, dt_=F32):
                return stk.enter_context(nc.psum_tensor(self.nm(nm), shape, dt_))

            ycc = [SB(f"g3_yc{i}", [P, DT * TC], BF16) for i in range(2)]
            stg = [SB(f"g3_s{i}", [P, DT * TC], BF16) for i in range(2)]
            pp = [PS(f"g3_p{i}", [P, TC]) for i in range(2)]
            m0, e0, a0 = smm.v, sev.v, sac.v
            lp0 = [s.v for s in slp]
            sp0 = [s.v for s in ssp]
            MV, EV, AV = DT, 1, DT

            with nc.Block(f"g3_{b}") as blk:
                @blk.sync
                def _(sy):
                    for c in range(nch):
                        t0 = b * L + c * TC
                        if c >= 2:
                            sy.wait_ge(smm.h, m0 + MV * (c - 2) + DT)
                        for dt in range(DT):
                            sy.dma_start(
                                ycc[c % 2][:, dt * TC:(dt + 1) * TC],
                                bass.AP(yt_dram.tensor, t0 * D + dt * P,
                                        [[D, TC], [1, P]]),
                                transpose=True).then_inc(slp[c % 2].h, 16)

                @blk.tensor
                def _(t):
                    for c in range(nch):
                        cm, ca = m0 + MV * c, a0 + AV * c
                        t.wait_ge(slp[c % 2].h, lp0[c % 2] + 16 * DT * (c // 2 + 1))
                        for ot in range(DT):
                            if ot >= 2 or c >= 1:
                                pv = (ca + ot - 2 + 1) if ot >= 2 else \
                                     (a0 + AV * (c - 1) + ot + 6 + 1)
                                t.wait_ge(sac.h, pv)
                            for it in range(DT):
                                ins = t.matmul(pp[ot % 2][:],
                                               woT[:, it * D + ot * P:
                                                   it * D + (ot + 1) * P],
                                               ycc[c % 2][:, it * TC:(it + 1) * TC],
                                               start=(it == 0), stop=(it == DT - 1))
                                if it == DT - 1:
                                    ins.then_inc(smm.h, 1)

                @blk.scalar
                def _(s):
                    for c in range(nch):
                        cm, ca = m0 + MV * c, a0 + AV * c
                        cj = c
                        for ot in range(DT):
                            s.wait_ge(smm.h, cm + ot + 1)
                            if ot == 0 and cj >= 2:
                                s.wait_ge(ssp[cj % 2].h,
                                          sp0[cj % 2] + 16 * ((cj - 2) // 2 + 1))
                            s.activation(stg[cj % 2][:, ot * TC:(ot + 1) * TC],
                                         pp[ot % 2][:], AF.Identity, scale=1.0,
                                         bias=self.vecs["bo"][:, ot:ot + 1]
                                         ).then_inc(sac.h, 1)

                @blk.gpsimd
                def _(g):
                    for c in range(nch):
                        g.wait_ge(sac.h, a0 + AV * c + AV)
                        t0 = b * L + c * TC
                        g.dma_start(
                            s1_spill.rearrange("dt p t -> p dt t")[:, :, t0:t0 + TC],
                            stg[c % 2].ap().rearrange("p (dt t) -> p dt t", dt=DT)
                        ).then_inc(ssp[c % 2].h, 16)
                    for par in range(2):
                        g.wait_ge(ssp[par].h, sp0[par] + 16 * (nch // 2))

            for par in range(2):
                slp[par].v = lp0[par] + 16 * DT * (nch // 2)
                ssp[par].v = sp0[par] + 16 * (nch // 2)
            smm.v = m0 + MV * nch
            sev.v = e0
            sac.v = a0 + AV * nch

    def emit_ffn1_combine(self, b, xn1_ct, cbT, s1_src, x1_src, dst):
        nc = self.nc
        smm, sev, sac = self.s_mm, self.s_ev, self.s_ac
        slp, slp2, ssp = self.s_lp, self.s_lp2, self.s_sp
        TC = 512
        nch = L // TC
        with ExitStack() as stk:
            def SB(nm, shape, dt_):
                return stk.enter_context(nc.sbuf_tensor(self.nm(nm), shape, dt_))

            def PS(nm, shape, dt_=F32):
                return stk.enter_context(nc.psum_tensor(self.nm(nm), shape, dt_))

            g1c = SB("h_g1", [P, DT * TC], BF16)
            s1c = [SB(f"h_s1{i}", [P, DT * TC], BF16) for i in range(2)]
            x1f0 = SB("h_x10", [P, 4 * D], F32)
            x1f = [x1f0, x1f0]
            sgt = SB("h_sgt", [P, 4 * D], BF16)
            scbc = SB("h_scb", [P, DT], F32)
            pp = [PS(f"h_p{i}", [P, TC]) for i in range(2)]
            ptr = [PS(f"h_pt{i}", [P, D], BF16) for i in range(2)]
            m0, e0, a0 = smm.v, sev.v, sac.v
            lp0 = [s.v for s in slp]
            lq0 = [s.v for s in slp2]
            sp0 = [s.v for s in ssp]
            MV, EV, AV = DT + 4, DT + 1, DT + 4

            with nc.Block(f"h_{b}") as blk:
                @blk.vector
                def _(v):
                    v.tensor_scalar(scbc[:], self.vecs["SCB"][:], 1.0 / 127.0,
                                    None, ALU.mult).then_inc(sev.h, 1)
                    for c in range(nch):
                        ce = e0 + 1 + EV * c
                        v.wait_ge(slp[c % 2].h, lp0[c % 2] + 16 * (c // 2 + 1))
                        if c >= 1:
                            v.wait_ge(smm.h, m0 + MV * (c - 1) + MV)
                        for dt in range(DT):
                            v.wait_ge(sac.h, a0 + AV * c + dt + 1)
                            ins = v.tensor_tensor(s1c[c % 2][:, dt * TC:(dt + 1) * TC],
                                                  s1c[c % 2][:, dt * TC:(dt + 1) * TC],
                                                  g1c[:, dt * TC:(dt + 1) * TC],
                                                  ALU.mult)
                            ins.then_inc(sev.h, 1)
                        v.wait_ge(slp2[c % 2].h, lq0[c % 2] + 16 * (c // 2 + 1))
                        v.wait_ge(sac.h, a0 + AV * c + DT + 4)
                        v.tensor_tensor(x1f[c % 2][:], x1f[c % 2][:], sgt[:],
                                        ALU.add).then_inc(sev.h, 1)

                @blk.tensor
                def _(t):
                    for c in range(nch):
                        cm, ce, ca = m0 + MV * c, e0 + 1 + EV * c, a0 + AV * c
                        for ot in range(DT):
                            if ot >= 2 or c >= 1:
                                pv = (ca + ot - 2 + 1) if ot >= 2 else \
                                     (a0 + AV * (c - 1) + ot + 6 + 1)
                                t.wait_ge(sac.h, pv)
                            for it in range(DT):
                                ins = t.matmul(pp[ot % 2][:],
                                               cbT[:, it * D + ot * P:
                                                   it * D + (ot + 1) * P],
                                               xn1_ct.ap().rearrange(
                                                   "p (dt t) -> p dt t", dt=DT)
                                                   [:, it, c * TC:(c + 1) * TC]
                                               if False else
                                               xn1_ct[:, it * L + c * TC:
                                                      it * L + c * TC + TC],
                                               start=(it == 0), stop=(it == DT - 1))
                                if it == DT - 1:
                                    ins.then_inc(smm.h, 1)
                        for tt in range(4):
                            t.wait_ge(sev.h, ce + DT)
                            if tt >= 2 or c >= 1:
                                pv = (ca + DT + tt - 2 + 1) if tt >= 2 else \
                                     (a0 + AV * (c - 1) + DT + tt + 2 + 1)
                                t.wait_ge(sac.h, pv)
                            for dt in range(DT):
                                ins = t.transpose(ptr[tt % 2][:, dt * P:(dt + 1) * P],
                                                  s1c[c % 2][:, dt * TC + tt * P:
                                                              dt * TC + (tt + 1) * P],
                                                  self.ident[:])
                                if dt == DT - 1:
                                    ins.then_inc(smm.h, 1)

                @blk.scalar
                def _(s):
                    for c in range(nch):
                        cm, ca = m0 + MV * c, a0 + AV * c
                        for ot in range(DT):
                            s.wait_ge(smm.h, cm + ot + 1)
                            if ot == 0 and c >= 1:
                                s.wait_ge(sev.h, e0 + 1 + EV * (c - 1) + DT)
                            s.wait_ge(sev.h, e0 + 1)
                            s.activation(g1c[:, ot * TC:(ot + 1) * TC],
                                         pp[ot % 2][:], AF.Silu,
                                         scale=scbc[:, ot:ot + 1],
                                         bias=self.vecs["ffn_bias"][:, ot:ot + 1]
                                         ).then_inc(sac.h, 1)
                        for tt in range(4):
                            s.wait_ge(smm.h, cm + DT + tt + 1)
                            if tt == 0 and c >= 1:
                                s.wait_ge(sev.h, e0 + 1 + EV * (c - 1) + DT + 1)
                            s.activation(sgt[:, tt * D:(tt + 1) * D],
                                         ptr[tt % 2][:], AF.Copy).then_inc(sac.h, 1)

                @blk.sync
                def _(sy):
                    for c in range(nch):
                        t0 = b * L + c * TC
                        if c >= 2:
                            sy.wait_ge(smm.h, m0 + MV * (c - 2) + MV)
                        sy.dma_start(
                            s1c[c % 2].ap().rearrange("p (dt t) -> p dt t", dt=DT),
                            s1_src.rearrange("dt p t -> p dt t")[:, :, t0:t0 + TC]
                        ).then_inc(slp[c % 2].h, 16)
                        if c >= 1:
                            sy.wait_ge(ssp[(c - 1) % 2].h,
                                       sp0[(c - 1) % 2] + 16 * ((c - 1) // 2 + 1))
                        sy.dma_start(
                            x1f[c % 2].ap().rearrange("p (tt d) -> p tt d", tt=4),
                            x1_src[b, c * TC:(c + 1) * TC, :]
                                .rearrange("(tt p) d -> p tt d", p=P)
                        ).then_inc(slp2[c % 2].h, 16)

                @blk.gpsimd
                def _(g):
                    for c in range(nch):
                        g.wait_ge(sev.h, e0 + 1 + EV * (c + 1))
                        g.dma_start(
                            dst[b, c * TC:(c + 1) * TC, :]
                                .rearrange("(tt p) d -> p tt d", p=P),
                            x1f[c % 2].ap().rearrange("p (tt d) -> p tt d", tt=4)
                        ).then_inc(ssp[c % 2].h, 16)
                    for par in range(2):
                        g.wait_ge(ssp[par].h, sp0[par] + 16 * (nch // 2))

            for par in range(2):
                slp[par].v = lp0[par] + 16 * (nch // 2)
                slp2[par].v = lq0[par] + 16 * (nch // 2)
                ssp[par].v = sp0[par] + 16 * (nch // 2)
            smm.v = m0 + MV * nch
            sev.v = e0 + 1 + EV * nch
            sac.v = a0 + AV * nch

    def emit_fence(self):
        nc = self.nc
        with nc.Block(self.nm("fence")) as blk:
            @blk.tensor
            def _(t):
                t.wait_ge(self.s_mm.h, self.s_mm.v)

            @blk.vector
            def _(v):
                v.wait_ge(self.s_ev.h, self.s_ev.v)
                v.wait_ge(self.s_vch.h, self.s_vch.v)

            @blk.scalar
            def _(s):
                s.wait_ge(self.s_ac.h, self.s_ac.v)
                s.wait_ge(self.s_sch.h, self.s_sch.v)

            @blk.gpsimd
            def _(g):
                g.wait_ge(self.s_g.h, self.s_g.v)
                g.wait_ge(self.s_gch.h, self.s_gch.v)

    def emit_spill_rows(self, sb, dram, b):
        nc = self.nc
        sp = self.s_sp[b % 2]
        v0 = sp.v
        with nc.Block(self.nm("sprow")) as blk:
            @blk.gpsimd
            def _(g):
                g.dma_start(
                    bass.AP(dram if not hasattr(dram, 'tensor') else dram.tensor,
                            b * L * D, [[D, P], [P * D, NCH], [1, D]]),
                    bass.AP(sb, 0, [[NCH * D, P], [D, NCH], [1, D]])
                ).then_inc(sp.h, 16)
                g.wait_ge(sp.h, v0 + 16)
        sp.v = v0 + 16

    def emit_copy_sb2dram(self, sb, dram, b):
        nc = self.nc
        sp = self.s_sp[b % 2]
        v0 = sp.v
        with nc.Block(self.nm("sb2d")) as blk:
            @blk.gpsimd
            def _(g):
                g.dma_start(
                    dram.rearrange("dt p t -> p dt t")[:, :, b * L:(b + 1) * L],
                    sb.ap().rearrange("p (dt t) -> p dt t", dt=DT)
                ).then_inc(sp.h, 16)
                g.wait_ge(sp.h, v0 + 16)
        sp.v = v0 + 16

    def emit_copy_dram2sb(self, dram, sb, b):
        nc = self.nc
        lp = self.s_lp[b % 2]
        v0 = lp.v
        with nc.Block(self.nm("d2sb")) as blk:
            @blk.sync
            def _(sy):
                sy.dma_start(
                    sb.ap().rearrange("p (dt t) -> p dt t", dt=DT),
                    dram.rearrange("dt p t -> p dt t")[:, :, b * L:(b + 1) * L]
                ).then_inc(lp.h, 16)
                sy.wait_ge(lp.h, v0 + 16)
        lp.v = v0 + 16


    def emit_moe_routed(self, b, xn_rows, g_ct, wga):
        nc = self.nc
        smm, sev, sac, sgc = self.s_mm, self.s_ev, self.s_ac, self.s_gch
        slp, slp2, sgp, sdn, ssp = self.s_lp, self.s_lp2, self.s_gp, self.s_dn, self.s_sp
        NCHK = ECAP // GCH
        with ExitStack() as stk:
            def SB(nm, shape, dt_):
                return stk.enter_context(nc.sbuf_tensor(self.nm(nm), shape, dt_))

            def PS(nm, shape, dt_=F32):
                return stk.enter_context(nc.psum_tensor(self.nm(nm), shape, dt_))

            mxs = self.r_mxs
            score_tc = self.r_sc
            m01 = self.r_m01
            didx_f = self.r_didxf
            didx_u = self.r_didxu
            jnk8 = self.r_jnk
            offrow = self.r_off
            offrow_f = self.r_offf
            rstk = ExitStack()
            pcomb = rstk.enter_context(nc.psum_tensor(self.nm("r_comb"), [P, NE], F32))
            pcnt = rstk.enter_context(nc.psum_tensor(self.nm("r_cnt"), [1, NE], F32))
            m0, e0, a0 = smm.v, sev.v, sac.v
            g0 = self.s_g.v
            with nc.Block(f"rroute_{b}") as blk:
                @blk.gpsimd
                def _(g):
                    g.iota(offrow_f[:], [[ECAP, NE]], channel_multiplier=0,
                           allow_small_or_imprecise_dtypes=True).then_inc(self.s_g.h, 1)
                self.s_g.v = g0 + 1

                @blk.vector
                def _(v):
                    v.wait_ge(self.s_g.h, g0 + 1)
                    v.tensor_copy(offrow[:], offrow_f[:]).then_inc(self.s_vch.h, 1)
                    self.s_vch.v += 1
                    v.wait_ge(self.s_vch.h, self.s_vch.v)
                    for i in range(NCH):
                        lg = self.lg_tc[:, (b * NCH + i) * NE:(b * NCH + i + 1) * NE]
                        v.reduce_max(mxs[:, i % 2:i % 2 + 1], lg, axis=AX.X
                                     ).then_inc(sev.h, 1)
                        v.wait_ge(sac.h, a0 + i + 1)
                        v.tensor_copy(score_tc[:, i:i + 1],
                                      mxs[:, 2 + i % 2:3 + i % 2])
                        if i >= 2:
                            v.wait_ge(smm.h, m0 + 3 * (i - 1))
                        for e in range(NE):
                            v.tensor_scalar(m01[i % 2][:, e:e + 1], lg[:, e:e + 1],
                                            mxs[:, i % 2:i % 2 + 1], None,
                                            ALU.is_equal)
                        v.tensor_copy(mxs[:, 4:5], self.cst[:, 0:1]
                                      ).then_inc(sev.h, 1)
                        v.wait_ge(smm.h, m0 + 3 * i + 3)
                        v.scalar_tensor_tensor(jnk8[:], m01[i % 2][:], 1.0,
                                               pcomb[:], ALU.mult, ALU.mult,
                                               accum_out=didx_f[:, i:i + 1]
                                               ).then_inc(sev.h, 1)
                        v.tensor_tensor(offrow[:], offrow[:], pcnt[:], ALU.add
                                        ).then_inc(sev.h, 1)
                    v.wait_ge(sev.h, e0 + 4 * NCH)
                    v.tensor_copy(didx_u[:], didx_f[:]).then_inc(sev.h, 1)

                @blk.scalar
                def _(s):
                    for i in range(NCH):
                        s.wait_ge(sev.h, e0 + 4 * i + 1)
                        s.activation(mxs[:, 2 + i % 2:3 + i % 2],
                                     mxs[:, i % 2:i % 2 + 1], AF.Sigmoid
                                     ).then_inc(sac.h, 1)

                @blk.tensor
                def _(t):
                    for i in range(NCH):
                        t.wait_ge(sev.h, e0 + 4 * i + 2)
                        if i >= 1:
                            t.wait_ge(sev.h, e0 + 4 * i)
                        t.matmul(pcomb[:], self.LTb[:], m01[i % 2][:],
                                 start=True, stop=False).then_inc(smm.h, 1)
                        t.matmul(pcomb[:], self.ones_colf[:],
                                 offrow[:], start=False, stop=True
                                 ).then_inc(smm.h, 1)
                        t.matmul(pcnt[:], self.onesc_b[:], m01[i % 2][:],
                                 start=True, stop=True).then_inc(smm.h, 1)
            smm.v = m0 + 3 * NCH
            sev.v = e0 + 4 * NCH + 1
            sac.v = a0 + NCH

            xst = [SB(f"s_x{i}", [P, D], BF16) for i in range(2)]
            xss = [SB(f"s_xs{i}", [P, D], BF16) for i in range(2)]
            zrow = SB("s_z", [P, D], BF16)
            lp0 = [s.v for s in slp]
            gp0 = [s.v for s in sgp]
            e0, g0 = sev.v, self.s_gch.v
            with nc.Block(f"rscat_{b}") as blk:
                @blk.sync
                def _(sy):
                    for i in range(NCH):
                        if i >= 2:
                            sy.wait_ge(sev.h, e0 + (i - 2) + 1)
                        sy.dma_start(
                            xst[i % 2][:],
                            bass.AP(xn_rows.tensor, (b * L + i * P) * D,
                                    [[D, P], [1, D]])).then_inc(slp[i % 2].h, 16)

                @blk.vector
                def _(v):
                    v.memset(zrow[:], 0.0).then_inc(self.s_vch.h, 1)
                    self.s_vch.v += 1
                    v.wait_ge(self.s_vch.h, self.s_vch.v)
                    for i in range(NCH):
                        v.wait_ge(slp[i % 2].h, lp0[i % 2] + 16 * (i // 2 + 1))
                        if i >= 2:
                            v.wait_ge(sgp[0].h, gp0[0] + 16 * (i - 1))
                        v.tensor_scalar(xss[i % 2][:], xst[i % 2][:],
                                        score_tc[:, i:i + 1], None, ALU.mult
                                        ).then_inc(sev.h, 1)

                @blk.gpsimd
                def _(g):
                    if b == 0:
                        g.wait_ge(sev.h, e0)
                        for z in range(NE * ECAP // P):
                            g.wait_ge(self.s_vch.h, self.s_vch.v)
                            g.dma_start(
                                bass.AP(self.xg_dram.tensor, z * P * D,
                                        [[D, P], [1, D]]),
                                zrow[:]).then_inc(self.s_gst.h, 16)
                            self.s_gst.v += 16
                        g.wait_ge(self.s_gst.h, self.s_gst.v)
                    for i in range(NCH):
                        g.wait_ge(sev.h, e0 + i + 1)
                        if i >= 1:
                            g.wait_ge(sgp[0].h, gp0[0] + 16 * i)
                        g.indirect_dma_start(
                            bass.AP(self.xg_dram.tensor, 0, [[D, NE * ECAP], [1, D]]),
                            bass.IndirectOffsetOnAxis(didx_u[:, i:i + 1], 0),
                            xss[i % 2][:], None).then_inc(sgp[0].h, 16)
                    g.wait_ge(sgp[0].h, gp0[0] + 16 * NCH)
            for par in range(2):
                slp[par].v = lp0[par] + 16 * (NCH // 2)
            sgp[0].v = gp0[0] + 16 * NCH
            sev.v = e0 + NCH
            rstk.close()

            shg = self.shg_dram
            estk = ExitStack()

            def ESB(nm, shape, dt_):
                return estk.enter_context(nc.sbuf_tensor(self.nm(nm), shape, dt_))

            wdn = ESB("rm_wdn", [P, FT * D], BF16)
            xs_ct = ESB("rm_xs", [P, DT * GCH], BF16)
            h_all = ESB("rm_h", [P, FT * GCH], BF16)
            hs0 = ESB("rm_hs", [P, GCH], BF16)
            eo_ct = ESB("rm_eo", [P, DT * GCH], BF16)
            eoT = [ESB(f"rm_eoT{i}", [P, D], BF16) for i in range(2)]
            ptbA = PS("rm_ptA", [P, P], BF16)
            ptbB = PS("rm_ptB", [P, P], BF16)
            ptbp = [ptbA, ptbB]
            pg = estk.enter_context(nc.psum_tensor(self.nm("rm_pg"), [P, GCH], F32))
            pu = estk.enter_context(nc.psum_tensor(self.nm("rm_pu"), [P, GCH], F32))
            pd = estk.enter_context(nc.psum_tensor(self.nm("rm_pd"), [P, DT * GCH], F32))
            m0, e0, a0 = smm.v, sev.v, sac.v
            lq0 = [s.v for s in slp2]
            gp0 = [s.v for s in sgp]
            dn0 = [s.v for s in sdn]
            sp0 = [s.v for s in ssp]
            NCK = [3] * NE + [16]
            QTOT = sum(NCK)
            MV, EV, AV = 56, 16, 40
            qof = [sum(NCK[:e]) for e in range(NE + 1)]

            def srcrow(e, ch):
                if e < NE:
                    return (e * ECAP + ch * GCH) * D
                return (b * L + ch * GCH) * D

            with nc.Block(f"rexp_{b}") as blk:
                @blk.gpsimd
                def _(g):
                    for e in range(NE + 1):
                        q0 = qof[e]
                        eg = self.w["expert_gate"][e] if e < NE else self.w["shared_gate"]
                        eu = self.w["expert_up"][e] if e < NE else self.w["shared_up"]
                        ed = self.w["expert_down"][e] if e < NE else self.w["shared_down"]
                        if e >= 1:
                            g.wait_ge(smm.h, m0 + MV * q0 - MV + 16)
                        for mat, co in ((eg, 0), (eu, DT * FF)):
                            g.dma_start(
                                wga[:, co:co + DT * FF].rearrange(
                                    "p (it f) -> p it f", it=DT)[:, :, 0:FF // 2],
                                mat.rearrange("(it p) f -> p it f", p=P
                                              )[:, :, 0:FF // 2]
                            ).then_inc(sgp[e % 2].h, 16)
                        g.wait_ge(sgp[e % 2].h, gp0[e % 2] + 64 * (e // 2) + 32)
                        if e >= 1:
                            g.wait_ge(smm.h, m0 + MV * q0 - MV + 32)
                        for mat, co in ((eg, 0), (eu, DT * FF)):
                            g.dma_start(
                                wga[:, co:co + DT * FF].rearrange(
                                    "p (it f) -> p it f", it=DT)[:, :, FF // 2:FF],
                                mat.rearrange("(it p) f -> p it f", p=P
                                              )[:, :, FF // 2:FF]
                            ).then_inc(sgp[e % 2].h, 16)
                        if e >= 1:
                            g.wait_ge(smm.h, m0 + MV * q0 - MV + 56)
                        g.dma_start(wdn.ap().rearrange("p (ft o) -> p ft o", ft=FT),
                                    ed.rearrange("(ft p) o -> p ft o", p=P)
                                    ).then_inc(sdn[e % 2].h, 16)
                        if e == NE and b == 0:
                            for nm in ["Wr", "Wk", "Wv", "Wo", "CB"]:
                                g.dma_start(self.wbf[nm], self.w[nm]
                                            ).then_inc(self.s_x.h, 16)
                                self.s_x.v += 16
                        def st(e, ch):
                            q = qof[e] + ch
                            g.wait_ge(sac.h, a0 + AV * q + 40)
                            dstb = self.eog_dram.tensor if e < NE else shg.tensor
                            for qt in range(2):
                                g.dma_start(
                                    bass.AP(dstb, srcrow(e, ch) - (b * L * D if e == NE else 0)
                                            + qt * P * D, [[D, P], [1, D]]),
                                    eoT[qt][:]).then_inc(ssp[q % 2].h, 16)
                        if e >= 1:
                            st(e - 1, NCK[e - 1] - 1)
                        for ch in range(NCK[e] - 1):
                            st(e, ch)
                    st(NE, NCK[NE] - 1)
                    for par in range(2):
                        g.wait_ge(ssp[par].h,
                                  sp0[par] + 32 * ((QTOT + 1 - par) // 2))

                @blk.sync
                def _(sy):
                    for e in range(NE + 1):
                        for ch in range(NCK[e]):
                            q = qof[e] + ch
                            if q >= 1:
                                sy.wait_ge(smm.h, m0 + MV * (q - 1) + 32)
                            src = (self.xg_dram.tensor if e < NE
                                   else xn_rows.tensor)
                            for dt in range(DT):
                                sy.dma_start(
                                    xs_ct[:, dt * GCH:(dt + 1) * GCH],
                                    bass.AP(src, srcrow(e, ch) + dt * P,
                                            [[D, GCH], [1, P]]),
                                    transpose=True).then_inc(slp2[q % 2].h, 16)

                @blk.tensor
                def _(t):
                    for e in range(NE + 1):
                        t.wait_ge(sgp[e % 2].h, gp0[e % 2] + 64 * (e // 2) + 32)
                        for ch in range(NCK[e]):
                            q = qof[e] + ch
                            cb_m, cb_e, cb_a = m0 + MV * q, e0 + EV * q, a0 + AV * q
                            pb_a = a0 + AV * (q - 1)
                            pb_e = e0 + EV * (q - 1)
                            t.wait_ge(slp2[q % 2].h,
                                      lq0[q % 2] + 16 * DT * (q // 2 + 1))
                            for ft in range(FT):
                                if ch == 0 and ft == FT // 2:
                                    t.wait_ge(sgp[e % 2].h,
                                              gp0[e % 2] + 64 * (e // 2) + 64)
                                if ft >= 1 or q >= 1:
                                    pv = (cb_a + ft) if ft >= 1 else \
                                         (pb_a + 16)
                                    t.wait_ge(sac.h, pv)
                                for it in range(DT):
                                    ins = t.matmul(
                                        pg[:],
                                        wga[:, it * FF + ft * P: it * FF + (ft + 1) * P],
                                        xs_ct[:, it * GCH:(it + 1) * GCH],
                                        start=(it == 0), stop=(it == DT - 1))
                                    if it == DT - 1:
                                        ins.then_inc(smm.h, 1)
                                if ft >= 1 or q >= 1:
                                    pv = (cb_e + ft) if ft >= 1 else \
                                         (pb_e + 16)
                                    t.wait_ge(sev.h, pv)
                                for it in range(DT):
                                    ins = t.matmul(
                                        pu[:],
                                        wga[:, DT * FF + it * FF + ft * P:
                                            DT * FF + it * FF + (ft + 1) * P],
                                        xs_ct[:, it * GCH:(it + 1) * GCH],
                                        start=(it == 0), stop=(it == DT - 1))
                                    if it == DT - 1:
                                        ins.then_inc(smm.h, 1)
                            if ch == 0:
                                t.wait_ge(sdn[e % 2].h, dn0[e % 2] + 16 * (e // 2 + 1))
                            t.wait_ge(sev.h, cb_e + 16)
                            if q >= 1:
                                t.wait_ge(sac.h, pb_a + 24)
                            for ot in range(DT):
                                for ft in range(FT):
                                    ins = t.matmul(
                                        pd[:, ot * GCH:(ot + 1) * GCH],
                                        wdn[:, ft * D + ot * P: ft * D + (ot + 1) * P],
                                        h_all[:, ft * GCH:(ft + 1) * GCH],
                                        start=(ft == 0), stop=(ft == FT - 1))
                                    if ft == FT - 1:
                                        ins.then_inc(smm.h, 1)
                            t.wait_ge(sac.h, cb_a + 24)
                            for j in range(16):
                                qt, dt = divmod(j, DT)
                                if j >= 2 or q >= 1:
                                    pv = (cb_a + 24 + j - 2 + 1) if j >= 2 else \
                                         (pb_a + 24 + j + 14 + 1)
                                    t.wait_ge(sac.h, pv)
                                t.transpose(ptbp[j % 2][:, 0:P],
                                            eo_ct[:, dt * GCH + qt * P:
                                                  dt * GCH + (qt + 1) * P],
                                            self.ident[:]).then_inc(smm.h, 1)

                @blk.scalar
                def _(s):
                    for e in range(NE + 1):
                        for ch in range(NCK[e]):
                            q = qof[e] + ch
                            cb_m, cb_e, cb_a = m0 + MV * q, e0 + EV * q, a0 + AV * q
                            for ft in range(FT):
                                s.wait_ge(smm.h, cb_m + 2 * ft + 1)
                                if ft >= 1:
                                    s.wait_ge(sev.h, cb_e + ft)
                                elif q >= 1:
                                    s.wait_ge(sev.h, e0 + EV * (q - 1) + 16)
                                s.activation(hs0[:], pg[:],
                                             AF.Silu).then_inc(sac.h, 1)
                            for ot in range(DT):
                                s.wait_ge(smm.h, cb_m + 40)
                                if ot == 0 and q >= 1:
                                    s.wait_ge(smm.h, m0 + MV * (q - 1) + 56)
                                s.activation(eo_ct[:, ot * GCH:(ot + 1) * GCH],
                                             pd[:, ot * GCH:(ot + 1) * GCH],
                                             AF.Copy).then_inc(sac.h, 1)
                            for j in range(16):
                                qt, dt = divmod(j, DT)
                                s.wait_ge(smm.h, cb_m + 40 + j + 1)
                                if j == 0 and q >= 1:
                                    s.wait_ge(ssp[(q - 1) % 2].h,
                                              sp0[(q - 1) % 2]
                                              + 32 * ((q - 1) // 2 + 1))
                                s.activation(eoT[qt][:, dt * P:(dt + 1) * P],
                                             ptbp[j % 2][:, 0:P], AF.Copy
                                             ).then_inc(sac.h, 1)

                @blk.vector
                def _(v):
                    for e in range(NE + 1):
                        for ch in range(NCK[e]):
                            q = qof[e] + ch
                            cb_m, cb_e, cb_a = m0 + MV * q, e0 + EV * q, a0 + AV * q
                            for ft in range(FT):
                                v.wait_ge(sac.h, cb_a + ft + 1)
                                v.wait_ge(smm.h, cb_m + 2 * ft + 2)
                                if ft == 0 and q >= 1:
                                    v.wait_ge(smm.h, m0 + MV * (q - 1) + 40)
                                v.tensor_tensor(h_all[:, ft * GCH:(ft + 1) * GCH],
                                                hs0[:], pu[:],
                                                ALU.mult).then_inc(sev.h, 1)

            smm.v = m0 + MV * QTOT
            sev.v = e0 + EV * QTOT
            sac.v = a0 + AV * QTOT
            for par in range(2):
                npar = (QTOT + 1) // 2 if par == 0 else QTOT // 2
                slp2[par].v = lq0[par] + 16 * DT * npar
                ssp[par].v = sp0[par] + 32 * npar
            ne_par = [(NE + 1 + 1) // 2, (NE + 1) // 2]
            for par in range(2):
                sgp[par].v = gp0[par] + 64 * ne_par[par]
                sdn[par].v = dn0[par] + 16 * ne_par[par]

            estk.close()
            ptg = [PS(f"rm_ptg{i}", [P, P], BF16) for i in range(2)]
            mgt = [SB(f"rm_mg{i}", [P, D], BF16) for i in range(2)]
            sht = [SB(f"rm_sh{i}", [P, D], BF16) for i in range(2)]
            srow = SB("rm_srow", [P, D], BF16)
            m0, e0, a0 = smm.v, sev.v, sac.v
            gp0 = [s.v for s in sgp]
            lp0 = [s.v for s in slp]
            with nc.Block(f"rgath_{b}") as blk:
                @blk.gpsimd
                def _(g):
                    for i in range(NCH):
                        if i >= 2:
                            g.wait_ge(sev.h, e0 + (i - 2) + 1)
                        g.indirect_dma_start(
                            mgt[i % 2][:],
                            None,
                            bass.AP(self.eog_dram.tensor, 0,
                                    [[D, NE * ECAP], [1, D]]),
                            bass.IndirectOffsetOnAxis(didx_u[:, i:i + 1], 0)
                        ).then_inc(sgp[i % 2].h, 16)

                @blk.sync
                def _(sy):
                    for i in range(NCH):
                        if i >= 2:
                            sy.wait_ge(sev.h, e0 + (i - 2) + 1)
                        sy.dma_start(
                            sht[i % 2][:],
                            bass.AP(shg.tensor, i * P * D, [[D, P], [1, D]])
                        ).then_inc(slp[i % 2].h, 16)
                    sy.wait_ge(sac.h, a0 + DT * NCH + DT)

                @blk.vector
                def _(v):
                    for i in range(NCH):
                        v.wait_ge(sgp[i % 2].h, gp0[i % 2] + 16 * (i // 2 + 1))
                        v.wait_ge(slp[i % 2].h, lp0[i % 2] + 16 * (i // 2 + 1))
                        if i >= 1:
                            v.wait_ge(smm.h, m0 + DT * i)
                        v.tensor_tensor(srow[:], mgt[i % 2][:], sht[i % 2][:],
                                        ALU.add).then_inc(sev.h, 1)

                @blk.tensor
                def _(t):
                    for i in range(NCH):
                        t.wait_ge(sev.h, e0 + i + 1)
                        for dt in range(DT):
                            j = i * DT + dt
                            if j >= 2:
                                t.wait_ge(sac.h, a0 + j - 2 + 1)
                            t.transpose(ptbp[j % 2][:, 0:P],
                                        srow[:, dt * P:(dt + 1) * P],
                                        self.ident[:]).then_inc(smm.h, 1)

                @blk.scalar
                def _(s):
                    for i in range(NCH):
                        for dt in range(DT):
                            j = i * DT + dt
                            s.wait_ge(smm.h, m0 + j + 1)
                            s.activation(g_ct[:, dt * L + i * P:
                                              dt * L + (i + 1) * P],
                                         ptbp[j % 2][:, 0:P], AF.Copy
                                         ).then_inc(sac.h, 1)
                    s.wait_ge(sac.h, a0 + DT * NCH)
                    for dt in range(DT):
                        s.activation(g_ct[:, dt * L:(dt + 1) * L],
                                     g_ct[:, dt * L:(dt + 1) * L], AF.Silu
                                     ).then_inc(sac.h, 1)
            smm.v = m0 + NCH * DT
            sev.v = e0 + NCH
            sac.v = a0 + DT * NCH + DT
            for par in range(2):
                sgp[par].v = gp0[par] + 16 * (NCH // 2)
                slp[par].v = lp0[par] + 16 * (NCH // 2)


def build_program():
    nc = bass.Bass("TRN2", target_bir_lowering=False)
    mk = MK(nc)
    mk.emit_setup()
    s0_ct = nc.dram_tensor("s0_ct", [DT, P, NB * L], BF16, kind="Internal").ap()
    mk.emit_common()
    mk.emit_fence()
    with ExitStack() as l0stk:
        mk.emit_conv_tables(l0stk)
        mk.emit_fence()
        for b in range(NB):
            mk.emit_ln(b, mk.x_in, mk.arena1, mk.xn_ct_dram, layer=0)
            mk.emit_fence()
            mk.emit_spill_rows(mk.arena1, mk.xn_tc2_dram, b)
            mk.emit_conv(b, mk.arena1, mk.arena2, mk.y_ct_dram)
            mk.emit_fence()
            if b == 0:
                mk.emit_wcast2(0)
    for b in range(NB):
        for nmw, off in [("gate_W", 0), ("ss_W", 1), ("fin_W", 2)]:
            mk.emit_load_wT(mk.nm(nmw), mk.wbf[nmw], mk.arena2[:],
                            dst_off=off * DT * D, need=(mk.s_xw, 48))
        mk.emit_gate_ss_fin(b, mk.xn_ct_dram, mk.y_ct_dram, mk.arena2[:],
                            mk.arena1[:])
        mk.emit_fence()
        mk.emit_copy_sb2dram(mk.arena1, s0_ct, b)
        mk.emit_moe_routed(b, mk.xn_tc2_dram, mk.arena2[:], mk.arena1[:])
        mk.emit_fence()
        mk.emit_combine(b, s0_ct, mk.arena2[:], mk.x_in, mk.x1_dram)
        mk.emit_fence()
    with ExitStack() as l1stk:
        mk.emit_wkv_tables()
        mk.emit_fence()
        for b in range(NB):
            for nmw, off in [("Wk", 0), ("Wv", 1), ("Wr", 2), ("Wo", 3)]:
                mk.emit_load_wT(mk.nm(nmw), mk.wbf[nmw], mk.arena2[:],
                                dst_off=off * DT * D)
            mk.emit_ln(b, mk.x1_dram, None, None, layer=1, ct_sbuf=mk.arena1)
            mk.emit_fence()
            mk.emit_lerp_proj(b, mk.arena1, mk.arena2[:], mk.rkv_dram)
            mk.emit_fence()
            mk.emit_wkv(b, mk.rkv_dram, mk.yt_dram)
            mk.emit_fence()
            mk.emit_wo(b, mk.yt_dram, mk.arena2[:, 3 * DT * D:], mk.s1_ct_dram)
            mk.emit_fence()
            mk.emit_load_wT(mk.nm("CB"), mk.wbf["CB"], mk.arena2[:], dst_off=0)
            mk.emit_ffn1_combine(b, mk.arena1, mk.arena2[:, 0:DT * D],
                                 mk.s1_ct_dram, mk.x1_dram, mk.out)
            mk.emit_fence()
    return nc


_REN = {
    "ss_W": "L0_attn_ss_W", "ss_b": "L0_attn_ss_b", "gate_W": "L0_attn_gate_W",
    "gate_b": "L0_attn_gate_b", "fin_W": "L0_attn_fin_W", "fin_b": "L0_attn_fin_b",
    "in_proj": "L0_attn_in_proj", "out_proj": "L0_attn_out_proj",
    "frequencies": "L0_attn_frequencies", "decays": "L0_attn_decays",
    "router_w": "L0_ffn_router_w", "expert_gate": "L0_ffn_expert_gate",
    "expert_up": "L0_ffn_expert_up", "expert_down": "L0_ffn_expert_down",
    "shared_gate": "L0_ffn_shared_gate", "shared_up": "L0_ffn_shared_up",
    "shared_down": "L0_ffn_shared_down", "maa_k": "L1_attn_time_maa_k",
    "maa_v": "L1_attn_time_maa_v", "maa_r": "L1_attn_time_maa_r",
    "t_decay": "L1_attn_time_decay", "t_faaaa": "L1_attn_time_faaaa",
    "Wr": "L1_attn_Wr", "br": "L1_attn_br", "Wk": "L1_attn_Wk", "bk": "L1_attn_bk",
    "Wv": "L1_attn_Wv", "bv": "L1_attn_bv", "Wo": "L1_attn_Wo", "bo": "L1_attn_bo",
    "CB": "L1_ffn_CB", "SCB": "L1_ffn_SCB", "ffn_bias": "L1_ffn_bias",
}

_NCORES = 8


def kernel(**inputs):
    nc = build_program()
    x = np.ascontiguousarray(inputs["L0_attn_x"])
    in_maps = []
    for core in range(_NCORES):
        im = {"x": x[core * NB:(core + 1) * NB]}
        for knm, src in _REN.items():
            im[knm] = inputs[src]
        in_maps.append(im)
    res = run_bass_kernel_spmd(nc, in_maps, core_ids=list(range(_NCORES)))
    return np.concatenate([r["out"] for r in res.results], axis=0)



# version 16 (final)
# speedup vs baseline: 1.0292x; 1.0292x over previous
import math
from contextlib import ExitStack
import numpy as np
import concourse.bass as bass
import concourse.mybir as mybir
from concourse.bass_utils import run_bass_kernel_spmd

F32 = mybir.dt.float32
BF16 = mybir.dt.bfloat16
AF = mybir.ActivationFunctionType
ALU = mybir.AluOpType
AX = mybir.AxisListType

P = 128
D = 1024
DT = D // P
L = 4096
NB = 2
NMODE = 64
QC = 128
NCH = L // QC
FF = 2048
FT = FF // P
NE = 8
RQ = 256
NRC = L // RQ
NH = 16
HD = 64
ECAP = 640
GCH = 256
U32 = mybir.dt.uint32

TWO_PI = 2.0 * math.pi
LN_EPS = 1e-5


def _round_hi(v, bits):
    if v == 0.0:
        return 0.0
    e = math.floor(math.log2(abs(v)))
    q = 2.0 ** (e - bits + 1)
    return round(v / q) * q


CW1 = float(np.float32(_round_hi(TWO_PI, 10)))
CW2 = float(np.float32(_round_hi(TWO_PI - CW1, 10)))
CW3 = float(np.float32(TWO_PI - CW1 - CW2))
MAGIC = float(2 ** 23)


class Sem:
    def __init__(self, nc, name):
        self.h = nc.semaphore(name).__enter__()
        self.v = 0


class MK:
    def __init__(self, nc):
        self.nc = nc
        self._uid = 0

    def nm(self, base):
        self._uid += 1
        return f"{base}_{self._uid}"

    def sb(self, name, shape, dtype):
        return self.nc.sbuf_tensor(name, shape, dtype).__enter__()

    def emit_setup(self):
        nc = self.nc
        self.x_in = nc.dram_tensor("x", [NB, L, D], F32, kind="ExternalInput").ap()
        w = {}
        for nm, shp in [
            ("ss_W", [D, D]), ("ss_b", [D]), ("gate_W", [D, D]), ("gate_b", [D]),
            ("fin_W", [D, D]), ("fin_b", [D]),
            ("in_proj", [D, NMODE]), ("out_proj", [NMODE, D]),
            ("frequencies", [NMODE]), ("decays", [NMODE]),
            ("router_w", [NE, D]),
            ("expert_gate", [NE, D, FF]), ("expert_up", [NE, D, FF]),
            ("expert_down", [NE, FF, D]),
            ("shared_gate", [D, FF]), ("shared_up", [D, FF]), ("shared_down", [FF, D]),
            ("maa_k", [D]), ("maa_v", [D]), ("maa_r", [D]),
            ("t_decay", [NH]), ("t_faaaa", [NH]),
            ("Wr", [D, D]), ("br", [D]), ("Wk", [D, D]), ("bk", [D]),
            ("Wv", [D, D]), ("bv", [D]), ("Wo", [D, D]), ("bo", [D]),
            ("CB", [D, D]), ("SCB", [D]), ("ffn_bias", [D]),
        ]:
            w[nm] = nc.dram_tensor(nm, shp, F32, kind="ExternalInput").ap()
        self.w = w
        self.out = nc.dram_tensor("out", [NB, L, D], F32, kind="ExternalOutput").ap()

        self.wbf = {}
        for nm in ["ss_W", "gate_W", "fin_W", "Wr", "Wk", "Wv", "Wo", "CB"]:
            self.wbf[nm] = nc.dram_tensor("bf_" + nm, [D, D], BF16, kind="Internal").ap()
        self.xn_ct_dram = nc.dram_tensor("xn_ct", [DT, P, NB * L], BF16, kind="Internal").ap()
        self.y_ct_dram = nc.dram_tensor("y_ctd", [DT, P, NB * L], BF16, kind="Internal").ap()
        self.x1_dram = nc.dram_tensor("x1", [NB, L, D], F32, kind="Internal").ap()
        self.xn1_ct_dram = nc.dram_tensor("xn1_ct", [DT, P, NB * L], BF16, kind="Internal").ap()
        self.rkv_dram = nc.dram_tensor("rkv", [3, DT, P, NB * L], BF16, kind="Internal").ap()
        self.s1_ct_dram = nc.dram_tensor("s1_ct", [DT, P, NB * L], BF16, kind="Internal").ap()
        self.k_dram_h = nc.dram_tensor("k_dram", [D, 256], BF16, kind="Internal")
        self.k_dram = self.k_dram_h.ap()
        self.yt_dram = nc.dram_tensor("yt_dram", [NB * L, D], BF16, kind="Internal").ap()
        self.xn_tc2_dram = nc.dram_tensor("xn_tc2", [NB * L, D], BF16, kind="Internal").ap()
        ECAP = 640
        self.xg_dram = nc.dram_tensor("xg", [NE * ECAP, D], BF16, kind="Internal").ap()
        self.eog_dram = nc.dram_tensor("eog", [NE * ECAP, D], BF16, kind="Internal").ap()
        self.shg_dram = nc.dram_tensor("shg", [L, D], BF16, kind="Internal").ap()

        self.s_load = Sem(nc, "s_load")
        self.s_ld2 = Sem(nc, "s_ld2")
        self.s_st = Sem(nc, "s_st")
        self.s_gld = Sem(nc, "s_gld")
        self.s_gst = Sem(nc, "s_gst")
        self.s_mm = Sem(nc, "s_mm")
        self.s_ev = Sem(nc, "s_ev")
        self.s_ac = Sem(nc, "s_ac")
        self.s_g = Sem(nc, "s_g")
        self.s_x = Sem(nc, "s_x")
        self.s_xw = Sem(nc, "s_xw")
        self.s_lp = [Sem(nc, "s_lp0"), Sem(nc, "s_lp1")]
        self.s_sp = [Sem(nc, "s_sp0"), Sem(nc, "s_sp1")]
        self.s_lp2 = [Sem(nc, "s_lp2a"), Sem(nc, "s_lp2b")]
        self.s_lp3 = [Sem(nc, "s_lp3a"), Sem(nc, "s_lp3b")]
        self.s_lp4 = [Sem(nc, "s_lp4a"), Sem(nc, "s_lp4b")]
        self.s_gp = [Sem(nc, "s_gp0"), Sem(nc, "s_gp1")]
        self.s_dn = [Sem(nc, "s_dn0"), Sem(nc, "s_dn1")]
        self.s_vch = Sem(nc, "s_vch")
        self.s_sch = Sem(nc, "s_sch")
        self.s_gch = Sem(nc, "s_gch")

        self.ident = self.sb("ident", [P, P], BF16)
        self.identf = self.sb("identf", [P, P], F32)
        self.ones_col = self.sb("ones_col", [1, P], BF16)
        self.ones_colf = self.sb("ones_colf", [1, P], F32)
        self.onesc_b = self.sb("onesc_b", [P, 1], BF16)
        self.LTb = self.sb("LTb", [P, P], BF16)
        self.selb = self.sb("selb", [NE, NE * P], BF16)
        self.vecs = {}
        for nm in ["ss_b", "gate_b", "fin_b", "br", "bk", "bv", "bo",
                   "SCB", "ffn_bias", "maa_k", "maa_v", "maa_r"]:
            self.vecs[nm] = self.sb("v_" + nm, [P, DT], F32)
        self.rwT = self.sb("rwT", [P, DT * NE], F32)
        self.cst = self.sb("cst", [P, 4], F32)
        self.r_mxs = self.sb("rp_mxs", [P, 6], F32)
        self.r_sc = self.sb("rp_sc", [P, NCH], F32)
        self.r_m01 = [self.sb(f"rp_m{i}", [P, NE], BF16) for i in range(2)]
        self.r_didxf = self.sb("rp_didx", [P, NCH], F32)
        self.r_didxu = self.sb("rp_didxu", [P, NCH], U32)
        self.r_jnk = self.sb("rp_jnk", [P, NE], F32)
        self.r_off = self.sb("rp_off", [1, NE], F32)
        self.r_offf = self.sb("rp_offf", [1, NE], F32)
        self.arena1 = self.sb("arena1", [P, NCH * D], BF16)
        self.arena2 = self.sb("arena2", [P, NCH * D], BF16)
        self.lg_tc = self.sb("lg_tc", [P, NB * NCH * NE], F32)

    def emit_common(self):
        nc = self.nc
        sg, sld = self.s_g, self.s_load
        with ExitStack() as stk, nc.Block("common") as blk:
            iota_row = stk.enter_context(nc.sbuf_tensor("iota_row", [P, P], F32))
            iota_col = stk.enter_context(nc.sbuf_tensor("iota_col", [P, 1], F32))
            pidx8 = stk.enter_context(nc.sbuf_tensor("pidx8", [NE, P], F32))
            g0 = sg.v
            sg.v += 1

            @blk.gpsimd
            def _(g):
                g.iota(iota_row[:], [[1, P]], channel_multiplier=0,
                       allow_small_or_imprecise_dtypes=True)
                g.iota(iota_col[:], [[0, 1]], channel_multiplier=1,
                       allow_small_or_imprecise_dtypes=True)
                g.iota(pidx8[:], [[0, P]], channel_multiplier=1,
                       allow_small_or_imprecise_dtypes=True).then_inc(sg.h, 1)

            @blk.vector
            def _(v):
                v.wait_ge(sg.h, g0 + 1)
                for e in range(NE):
                    v.tensor_scalar(self.selb[:, e * P:(e + 1) * P],
                                    pidx8[:], float(e), None, ALU.is_equal)
                v.tensor_scalar(self.ident[:], iota_row[:],
                                iota_col[:, 0:1], None, ALU.is_equal)
                v.tensor_scalar(self.identf[:], iota_row[:],
                                iota_col[:, 0:1], None, ALU.is_equal)
                v.tensor_scalar(self.LTb[:], iota_row[:],
                                iota_col[:, 0:1], None, ALU.is_gt)
                v.memset(self.onesc_b[:], 1.0)
                v.memset(self.ones_col[:], 1.0)
                v.memset(self.ones_colf[:], 1.0)
                v.memset(self.cst[:, 0:1], LN_EPS)
                v.memset(self.cst[:, 1:2], 0.0)
                v.memset(self.cst[:, 2:3], 1.0).then_inc(self.s_vch.h, 1)
                self.s_vch.v += 1
                v.wait_ge(self.s_vch.h, self.s_vch.v)

        with nc.Block("wcast") as blk:
            sgl = self.s_gld
            l0 = sgl.v
            nd = 12 * DT + DT
            sgl.v = l0 + 16 * nd

            @blk.gpsimd
            def _(g):
                for nm in ["ss_b", "gate_b", "fin_b", "br", "bk", "bv",
                           "bo", "SCB", "ffn_bias", "maa_k", "maa_v", "maa_r"]:
                    for dt in range(DT):
                        g.dma_start(self.vecs[nm][:, dt:dt + 1],
                                    self.w[nm][dt * P:(dt + 1) * P]).then_inc(sgl.h, 16)
                for dt in range(DT):
                    g.dma_start(
                        self.rwT[:, dt * NE:(dt + 1) * NE],
                        self.w["router_w"][:, dt * P:(dt + 1) * P]
                            .rearrange("e p -> p e"),
                        allow_slow_non_contiguous=True).then_inc(sgl.h, 16)

            @blk.sync
            def _(sy):
                sy.wait_ge(sgl.h, sgl.v)

    def emit_load_wT(self, tag, wbf_ap, dst, n_in=D, n_out=D, dst_off=0,
                     need=None):
        nc = self.nc
        ni_t = n_in // P
        l0 = self.s_ld2.v
        self.s_ld2.v = l0 + 16 * ni_t
        if need is None:
            need = (self.s_x, 80)
        with nc.Block("lwT_" + tag) as blk:
            @blk.sync
            def _(sy):
                sy.wait_ge(need[0].h, need[1])
                for it in range(ni_t):
                    sy.dma_start(dst[:, dst_off + it * n_out: dst_off + (it + 1) * n_out],
                                 wbf_ap[:, it * P:(it + 1) * P],
                                 transpose=True).then_inc(self.s_ld2.h, 16)
                sy.wait_ge(self.s_ld2.h, self.s_ld2.v)

    def emit_wcast2(self, part):
        nc = self.nc
        grp = (["ss_W", "gate_W", "fin_W"] if part == 0
               else ["Wr", "Wk", "Wv", "Wo", "CB"])
        sem = self.s_xw if part == 0 else self.s_x
        with nc.Block(f"wcast2_{part}") as blk:
            @blk.gpsimd
            def _(g):
                for nm in grp:
                    g.dma_start(self.wbf[nm], self.w[nm]).then_inc(sem.h, 16)
                    sem.v += 16

    def emit_ln(self, b, src_dram, xn_tc, xn_spill, layer, ct_sbuf=None):
        nc = self.nc
        smm, sev, sac = self.s_mm, self.s_ev, self.s_ac
        slp, ssp = self.s_lp, self.s_sp
        n = NCH
        with ExitStack() as stk:
            def SB(nm, shape, dt_):
                return stk.enter_context(nc.sbuf_tensor(self.nm(nm), shape, dt_))

            def PS(nm, shape, dt_=F32):
                return stk.enter_context(nc.psum_tensor(self.nm(nm), shape, dt_))

            xt = [SB(f"ln_x{i}", [P, D], F32) for i in range(2)]
            xnf = [SB(f"ln_xnf{i}", [P, D], F32) for i in range(2)]
            bst = [SB(f"ln_bst{i}", [P, 12], F32) for i in range(2)]
            sta = [SB(f"ln_st{i}", [P, 6], F32) for i in range(2)]
            ctf = [SB(f"ln_ctf{i}", [P, D], F32) for i in range(2)]
            stg = ([SB(f"ln_stg{i}", [P, D], BF16) for i in range(2)]
                   if ct_sbuf is None else None)
            lgrow = [SB(f"ln_lgrow{i}", [NE, P], F32) for i in range(2)]
            ptr = [PS(f"ln_ptr{i}", [P, D]) for i in range(2)]
            plg = [PS(f"ln_plg{i}", [NE, P]) for i in range(2)]
            plg2 = [PS(f"ln_plg2{i}", [P, NE]) for i in range(2)]
            m0, e0, a0 = smm.v, sev.v, sac.v
            lp0 = [s.v for s in slp]
            sp0 = [s.v for s in ssp]
            lay0 = (layer == 0)
            sev_aggr, sev_nmrs, sev_ctf = {}, {}, {}
            sev_lgr, sev_lgt = {}, {}
            sac_rstd, sac_apply, sac_stage, sac_cast = {}, {}, {}, {}
            smm_tr, smm_rt, smm_lgt = {}, {}, {}
            cv, ca_, cm = e0, a0, m0
            nit = n + 4 if lay0 else n + 2
            for t in range(nit):
                if t < n:
                    cv += 1; sev_aggr[t] = cv
                    ca_ += 1; sac_rstd[t] = ca_
                j, k = t - 1, t - 2
                k3, k4 = t - 3, t - 4
                if 0 <= j < n:
                    cv += 1; sev_nmrs[j] = cv
                    ca_ += 1; sac_apply[j] = ca_
                    cm += 1; smm_tr[j] = cm
                if 0 <= k < n:
                    cv += 1; sev_ctf[k] = cv
                    ca_ += 1; sac_stage[k] = ca_
                    if xn_tc is not None:
                        ca_ += 1; sac_cast[k] = ca_
                    if lay0:
                        cm += 1; smm_rt[k] = cm
                if lay0 and 0 <= k3 < n:
                    cv += 1; sev_lgr[k3] = cv
                    cm += 1; smm_lgt[k3] = cm
                if lay0 and 0 <= k4 < n:
                    cv += 1; sev_lgt[k4] = cv

            with nc.Block(f"lnA_{layer}_{b}") as blk:
                @blk.sync
                def _(sy):
                    for i in range(n):
                        if i >= 2:
                            sy.wait_ge(sev.h, sev_aggr[i - 2])
                            sy.wait_ge(sac.h, sac_apply[i - 2])
                        sy.dma_start(xt[i % 2][:], src_dram[b, i * P:(i + 1) * P, :]
                                     ).then_inc(slp[i % 2].h, 16)
                    if ct_sbuf is None:
                        for par in range(2):
                            sy.wait_ge(ssp[par].h, sp0[par] + 16 * (n // 2))
                    else:
                        sy.wait_ge(sac.h, ca_)

                @blk.gpsimd
                def _(g):
                    if ct_sbuf is not None:
                        return
                    for i in range(n):
                        g.wait_ge(sac.h, sac_stage[i])
                        t0 = b * L + i * P
                        g.dma_start(
                            xn_spill.rearrange("dt p t -> p dt t")[:, :, t0:t0 + P],
                            stg[i % 2].ap().rearrange("p (dt q) -> p dt q", dt=DT)
                        ).then_inc(ssp[i % 2].h, 16)

                @blk.vector
                def _(v):
                    for t in range(nit):
                        if t < n:
                            v.wait_ge(slp[t % 2].h, lp0[t % 2] + 16 * (t // 2 + 1))
                            if t >= 2:
                                v.wait_ge(sac.h, sac_rstd[t - 2])
                            v.bn_stats(bst[t % 2][:, 0:6], xt[t % 2][:, 0:512])
                            v.bn_stats(bst[t % 2][:, 6:12], xt[t % 2][:, 512:1024]
                                       ).then_inc(self.s_vch.h, 1)
                            self.s_vch.v += 1
                            v.wait_ge(self.s_vch.h, self.s_vch.v)
                            v.bn_aggr(sta[t % 2][:, 0:2], bst[t % 2][:, 0:12]
                                      ).then_inc(sev.h, 1)
                        j, k = t - 1, t - 2
                        if 0 <= j < n:
                            v.wait_ge(sac.h, sac_rstd[j])
                            v.scalar_tensor_tensor(sta[j % 2][:, 4:5],
                                                   sta[j % 2][:, 0:1], -1.0,
                                                   sta[j % 2][:, 3:4],
                                                   ALU.mult, ALU.mult
                                                   ).then_inc(sev.h, 1)
                        if 0 <= k < n:
                            v.wait_ge(smm.h, smm_tr[k])
                            if k >= 2:
                                v.wait_ge(sac.h, sac_stage[k - 2])
                                if lay0:
                                    v.wait_ge(smm.h, smm_rt[k - 2])
                            v.tensor_copy(ctf[k % 2][:], ptr[k % 2][:]
                                          ).then_inc(sev.h, 1)
                        k3, k4 = t - 3, t - 4
                        if lay0 and 0 <= k3 < n:
                            v.wait_ge(smm.h, smm_rt[k3])
                            if k3 >= 2:
                                v.wait_ge(smm.h, smm_lgt[k3 - 2])
                            v.tensor_copy(lgrow[k3 % 2][:], plg[k3 % 2][:]
                                          ).then_inc(sev.h, 1)
                        if lay0 and 0 <= k4 < n:
                            v.wait_ge(smm.h, smm_lgt[k4])
                            v.tensor_copy(
                                self.lg_tc[:, (b * NCH + k4) * NE:
                                           (b * NCH + k4 + 1) * NE],
                                plg2[k4 % 2][:]).then_inc(sev.h, 1)

                @blk.scalar
                def _(s):
                    for t in range(nit):
                        if t < n:
                            s.wait_ge(sev.h, sev_aggr[t])
                            s.activation(sta[t % 2][:, 2:3], sta[t % 2][:, 1:2],
                                         AF.Ln, bias=self.cst[:, 0:1]
                                         ).then_inc(self.s_sch.h, 1)
                            self.s_sch.v += 1
                            s.wait_ge(self.s_sch.h, self.s_sch.v)
                            s.activation(sta[t % 2][:, 3:4], sta[t % 2][:, 2:3],
                                         AF.Exp, scale=-0.5).then_inc(sac.h, 1)
                        j, k = t - 1, t - 2
                        if 0 <= j < n:
                            s.wait_ge(sev.h, sev_nmrs[j])
                            s.wait_ge(sac.h, sac_rstd[j])
                            if j >= 2:
                                s.wait_ge(smm.h, smm_tr[j - 2])
                                if xn_tc is not None:
                                    s.wait_ge(sac.h, sac_cast[j - 2])
                            s.activation(xnf[j % 2][:], xt[j % 2][:], AF.Identity,
                                         scale=sta[j % 2][:, 3:4],
                                         bias=sta[j % 2][:, 4:5]).then_inc(sac.h, 1)
                        if 0 <= k < n:
                            s.wait_ge(sev.h, sev_ctf[k])
                            if ct_sbuf is None:
                                if k >= 2:
                                    s.wait_ge(ssp[k % 2].h,
                                              sp0[k % 2] + 16 * ((k - 2) // 2 + 1))
                                tgt = stg[k % 2][:]
                            else:
                                tgt = bass.AP(ct_sbuf, k * P,
                                              [[DT * L, P], [L, DT], [1, P]])
                            s.activation(tgt, ctf[k % 2][:], AF.Copy
                                         ).then_inc(sac.h, 1)
                            if xn_tc is not None:
                                s.wait_ge(sac.h, sac_apply[k])
                                s.activation(xn_tc[:, k * D:(k + 1) * D],
                                             xnf[k % 2][:], AF.Copy
                                             ).then_inc(sac.h, 1)

                @blk.tensor
                def _(t_):
                    for t in range(nit):
                        j, k = t - 1, t - 2
                        k3 = t - 3
                        if 0 <= j < n:
                            t_.wait_ge(sac.h, sac_apply[j])
                            if j >= 2:
                                t_.wait_ge(sev.h, sev_ctf[j - 2])
                            for dt in range(DT):
                                ins = t_.transpose(ptr[j % 2][:, dt * P:(dt + 1) * P],
                                                   xnf[j % 2][:, dt * P:(dt + 1) * P],
                                                   self.identf[:])
                                if dt == DT - 1:
                                    ins.then_inc(smm.h, 1)
                        if 0 <= k < n and lay0:
                            t_.wait_ge(sev.h, sev_ctf[k])
                            if k >= 2:
                                t_.wait_ge(sev.h, sev_lgr[k - 2])
                            for dt in range(DT):
                                ins = t_.matmul(plg[k % 2][:, :],
                                                self.rwT[:, dt * NE:(dt + 1) * NE],
                                                ctf[k % 2][:, dt * P:(dt + 1) * P],
                                                start=(dt == 0), stop=(dt == DT - 1))
                                if dt == DT - 1:
                                    ins.then_inc(smm.h, 1)
                        if lay0 and 0 <= k3 < n:
                            t_.wait_ge(sev.h, sev_lgr[k3])
                            if k3 >= 2:
                                t_.wait_ge(sev.h, sev_lgt[k3 - 2])
                            t_.transpose(plg2[k3 % 2][:], lgrow[k3 % 2][:],
                                         self.identf[0:NE, 0:NE]
                                         ).then_inc(smm.h, 1)

            for par in range(2):
                slp[par].v = lp0[par] + 16 * (n // 2)
                if ct_sbuf is None:
                    ssp[par].v = sp0[par] + 16 * (n // 2)
            sev.v = cv
            sac.v = ca_
            smm.v = cm

    def emit_gate_ss_fin(self, b, xn_src, y_src, wT, s_ct):
        nc = self.nc
        smm, sev, sac = self.s_mm, self.s_ev, self.s_ac
        slp, slp2 = self.s_lp, self.s_lp2
        TC = 512
        nch = L // TC
        WG, WS, WF = 0, DT * D, 2 * DT * D
        with ExitStack() as stk:
            def SB(nm, shape, dt_):
                return stk.enter_context(nc.sbuf_tensor(self.nm(nm), shape, dt_))

            def PS(nm, shape, dt_=F32):
                return stk.enter_context(nc.psum_tensor(self.nm(nm), shape, dt_))

            xcn = [SB(f"c_x{i}", [P, DT * TC], BF16) for i in range(2)]
            ycn = [SB(f"c_y{i}", [P, DT * TC], BF16) for i in range(2)]
            ga = SB("c_ga", [P, DT * TC], BF16)
            sa = SB("c_sa", [P, DT * TC], BF16)
            pr = SB("c_pr", [P, DT * TC], BF16)
            pa = [PS(f"c_pa{i}", [P, TC]) for i in range(2)]
            ps = [PS(f"c_ps{i}", [P, TC]) for i in range(2)]
            pf = [PS(f"c_pf{i}", [P, TC]) for i in range(2)]
            m0, e0, a0 = smm.v, sev.v, sac.v
            lp0 = [s.v for s in slp]
            lq0 = [s.v for s in slp2]
            MV, EV, AV = 24, 8, 24

            with nc.Block(f"gsf_{b}") as blk:
                @blk.sync
                def _(sy):
                    for c in range(nch):
                        t0 = b * L + c * TC
                        if c >= 2:
                            sy.wait_ge(smm.h, m0 + MV * (c - 2) + 8)
                        sy.dma_start(
                            xcn[c % 2].ap().rearrange("p (dt t) -> p dt t", dt=DT),
                            xn_src.rearrange("dt p t -> p dt t")[:, :, t0:t0 + TC]
                        ).then_inc(slp[c % 2].h, 16)
                        if c >= 2:
                            sy.wait_ge(smm.h, m0 + MV * (c - 2) + 16)
                        sy.dma_start(
                            ycn[c % 2].ap().rearrange("p (dt t) -> p dt t", dt=DT),
                            y_src.rearrange("dt p t -> p dt t")[:, :, t0:t0 + TC]
                        ).then_inc(slp2[c % 2].h, 16)
                    sy.wait_ge(sac.h, a0 + AV * nch)

                @blk.tensor
                def _(t):
                    for c in range(nch):
                        cb_m, cb_a, cb_e = m0 + MV * c, a0 + AV * c, e0 + EV * c
                        t.wait_ge(slp[c % 2].h, lp0[c % 2] + 16 * (c // 2 + 1))
                        for ot in range(DT):
                            if (c, ot) > (0, 1):
                                pot = ot - 2 if ot >= 2 else ot + 6
                                pcb = cb_a if ot >= 2 else a0 + AV * (c - 1)
                                t.wait_ge(sac.h, pcb + pot + 1)
                            for it in range(DT):
                                ins = t.matmul(pa[ot % 2][:],
                                         wT[:, WG + it * D + ot * P: WG + it * D + (ot + 1) * P],
                                         xcn[c % 2][:, it * TC:(it + 1) * TC],
                                         start=(it == 0), stop=(it == DT - 1))
                                if it == DT - 1:
                                    ins.then_inc(smm.h, 1)
                        t.wait_ge(slp2[c % 2].h, lq0[c % 2] + 16 * (c // 2 + 1))
                        for ot in range(DT):
                            if (c, ot) > (0, 1):
                                pot = ot - 2 if ot >= 2 else ot + 6
                                pcb = cb_a if ot >= 2 else a0 + AV * (c - 1)
                                t.wait_ge(sac.h, pcb + 8 + pot + 1)
                            for it in range(DT):
                                ins = t.matmul(ps[ot % 2][:],
                                         wT[:, WS + it * D + ot * P: WS + it * D + (ot + 1) * P],
                                         ycn[c % 2][:, it * TC:(it + 1) * TC],
                                         start=(it == 0), stop=(it == DT - 1))
                                if it == DT - 1:
                                    ins.then_inc(smm.h, 1)
                        t.wait_ge(sev.h, cb_e + 8)
                        for ot in range(DT):
                            if (c, ot) > (0, 1):
                                pot = ot - 2 if ot >= 2 else ot + 6
                                pcb = cb_a if ot >= 2 else a0 + AV * (c - 1)
                                t.wait_ge(sac.h, pcb + 16 + pot + 1)
                            for it in range(DT):
                                ins = t.matmul(pf[ot % 2][:],
                                         wT[:, WF + it * D + ot * P: WF + it * D + (ot + 1) * P],
                                         pr[:, it * TC:(it + 1) * TC],
                                         start=(it == 0), stop=(it == DT - 1))
                                if it == DT - 1:
                                    ins.then_inc(smm.h, 1)

                @blk.scalar
                def _(s):
                    for c in range(nch):
                        cb_m, cb_a, cb_e = m0 + MV * c, a0 + AV * c, e0 + EV * c
                        if c >= 1:
                            s.wait_ge(sev.h, e0 + EV * c)
                        for ot in range(DT):
                            s.wait_ge(smm.h, cb_m + ot + 1)
                            s.activation(ga[:, ot * TC:(ot + 1) * TC], pa[ot % 2][:],
                                         AF.Gelu, bias=self.vecs["gate_b"][:, ot:ot + 1]
                                         ).then_inc(sac.h, 1)
                        for ot in range(DT):
                            s.wait_ge(smm.h, cb_m + 8 + ot + 1)
                            s.activation(sa[:, ot * TC:(ot + 1) * TC], ps[ot % 2][:],
                                         AF.Gelu, bias=self.vecs["ss_b"][:, ot:ot + 1]
                                         ).then_inc(sac.h, 1)
                        for ot in range(DT):
                            s.wait_ge(smm.h, cb_m + 16 + ot + 1)
                            s.activation(s_ct[:, ot * L + c * TC: ot * L + c * TC + TC],
                                         pf[ot % 2][:], AF.Identity, scale=1.0,
                                         bias=self.vecs["fin_b"][:, ot:ot + 1]
                                         ).then_inc(sac.h, 1)

                @blk.vector
                def _(v):
                    for c in range(nch):
                        cb_a = a0 + AV * c
                        if c >= 1:
                            v.wait_ge(smm.h, m0 + MV * c)
                        for ot in range(DT):
                            v.wait_ge(sac.h, cb_a + 8 + ot + 1)
                            v.tensor_tensor(pr[:, ot * TC:(ot + 1) * TC],
                                            ga[:, ot * TC:(ot + 1) * TC],
                                            sa[:, ot * TC:(ot + 1) * TC], ALU.mult
                                            ).then_inc(sev.h, 1)

            for par in range(2):
                slp[par].v = lp0[par] + 16 * (nch // 2)
                slp2[par].v = lq0[par] + 16 * (nch // 2)
            smm.v = m0 + MV * nch
            sev.v = e0 + EV * nch
            sac.v = a0 + AV * nch

    def emit_moe(self, b, xn_src, g_ct, wga):
        nc = self.nc
        smm, sev, sac, sgc = self.s_mm, self.s_ev, self.s_ac, self.s_gch
        slp, sgp, sdn = self.s_lp, self.s_gp, self.s_dn
        TC = 512
        nch = L // TC
        moestk = ExitStack()
        wdn = moestk.enter_context(nc.sbuf_tensor(self.nm("wdn"), [P, FT * D], BF16))
        msr = moestk.enter_context(nc.sbuf_tensor(self.nm("msr"), [NE, L], BF16))
        ones_row = moestk.enter_context(nc.sbuf_tensor(self.nm("onesr"), [1, 512], BF16))
        with ExitStack() as stk:
            def SB(nm, shape, dt_):
                return stk.enter_context(nc.sbuf_tensor(self.nm(nm), shape, dt_))

            def PS(nm, shape, dt_=F32):
                return stk.enter_context(nc.psum_tensor(self.nm(nm), shape, dt_))

            mxs = SB("r_mxs", [P, 6], F32)
            mst = [SB(f"r_mst{i}", [P, NE], F32) for i in range(2)]
            pms = PS("r_pms", [NE, P])
            m0, e0, a0 = smm.v, sev.v, sac.v
            with nc.Block(f"route_{b}") as blk:
                @blk.vector
                def _(v):
                    for i in range(NCH):
                        lg = self.lg_tc[:, (b * NCH + i) * NE:
                                        (b * NCH + i + 1) * NE]
                        v.reduce_max(mxs[:, i % 2:i % 2 + 1], lg, axis=AX.X
                                     ).then_inc(sev.h, 1)
                        v.wait_ge(sac.h, a0 + i + 1)
                        if i >= 2:
                            v.wait_ge(smm.h, m0 + i - 1)
                        for e in range(NE):
                            v.tensor_scalar(mst[i % 2][:, e:e + 1], lg[:, e:e + 1],
                                            mxs[:, i % 2:i % 2 + 1],
                                            mxs[:, 2 + i % 2:3 + i % 2],
                                            ALU.is_equal, ALU.mult)
                        v.tensor_copy(mxs[:, 4:5], self.cst[:, 0:1]
                                      ).then_inc(sev.h, 1)
                        if i >= 1:
                            v.wait_ge(smm.h, m0 + i)
                            v.tensor_copy(msr[0:NE, (i - 1) * P: i * P],
                                          pms[:]).then_inc(sev.h, 1)
                        else:
                            v.tensor_copy(mxs[:, 5:6], self.cst[:, 0:1]
                                          ).then_inc(sev.h, 1)
                    v.wait_ge(smm.h, m0 + NCH)
                    v.tensor_copy(msr[0:NE, (NCH - 1) * P: NCH * P],
                                  pms[:]).then_inc(sev.h, 1)
                    v.memset(ones_row[:], 1.0)

                @blk.scalar
                def _(s):
                    for i in range(NCH):
                        s.wait_ge(sev.h, e0 + 3 * i + 1)
                        s.activation(mxs[:, 2 + i % 2:3 + i % 2],
                                     mxs[:, i % 2:i % 2 + 1], AF.Sigmoid
                                     ).then_inc(sac.h, 1)

                @blk.tensor
                def _(t):
                    for i in range(NCH):
                        t.wait_ge(sev.h, e0 + 3 * i + 2)
                        if i >= 1:
                            t.wait_ge(sev.h, e0 + 3 * i + 3)
                        t.transpose(pms[:], mst[i % 2][:], self.identf[:]
                                    ).then_inc(smm.h, 1)
            smm.v = m0 + NCH
            sev.v = e0 + 3 * NCH + 1
            sac.v = a0 + NCH

        with ExitStack() as stk:
            def SB(nm, shape, dt_):
                return stk.enter_context(nc.sbuf_tensor(self.nm(nm), shape, dt_))

            def PS(nm, shape, dt_=F32):
                return stk.enter_context(nc.psum_tensor(self.nm(nm), shape, dt_))

            xcn0 = SB("m_x0", [P, DT * TC], BF16)
            xcn = [xcn0, xcn0]
            xs = xcn
            bc0 = SB("m_bc", [P, TC], BF16)
            bc = [bc0, bc0]
            hs0 = SB("m_hs", [P, TC], BF16)
            hs = [hs0, hs0]
            h_all = SB("m_h", [P, FT * TC], BF16)
            pbc = PS("m_pbc", [P, TC])
            pg = [PS(f"m_pg{i}", [P, TC]) for i in range(2)]
            pu = [PS(f"m_pu{i}", [P, TC]) for i in range(2)]
            pd = [PS(f"m_pd{i}", [P, TC]) for i in range(2)]
            m0, e0, a0 = smm.v, sev.v, sac.v
            lp0 = [s.v for s in slp]
            gp0 = [s.v for s in sgp]
            dn0 = [s.v for s in sdn]
            g0 = sgc.v
            NEX = NE + 1
            MV, EV, AV = 41, 32, 17
            MC, EC, AC = MV * nch, EV * nch, AV * nch

            def src_w(kind, e):
                if e < NE:
                    a = {"g": self.w["expert_gate"], "u": self.w["expert_up"],
                         "d": self.w["expert_down"]}[kind][e]
                else:
                    a = {"g": self.w["shared_gate"], "u": self.w["shared_up"],
                         "d": self.w["shared_down"]}[kind]
                return a

            with nc.Block(f"moe_{b}") as blk:
                @blk.gpsimd
                def _(g):
                    g.memset(g_ct[:], 0.0).then_inc(sgc.h, 1)
                    for e in range(NEX):
                        if e >= 1:
                            g.wait_ge(smm.h, m0 + MC * (e - 1) + MV * (nch - 1) + 33)
                        g.dma_start(wga[:, 0:DT * FF].rearrange(
                                        "p (it f) -> p it f", it=DT),
                                    src_w("g", e).rearrange("(it p) f -> p it f", p=P)
                                    ).then_inc(sgp[e % 2].h, 16)
                        g.dma_start(wga[:, DT * FF:2 * DT * FF].rearrange(
                                        "p (it f) -> p it f", it=DT),
                                    src_w("u", e).rearrange("(it p) f -> p it f", p=P)
                                    ).then_inc(sgp[e % 2].h, 16)
                        if e >= 1:
                            g.wait_ge(smm.h, m0 + MC * e)
                        g.dma_start(wdn.ap().rearrange("p (ft o) -> p ft o", ft=FT),
                                    src_w("d", e).rearrange("(ft p) o -> p ft o", p=P)
                                    ).then_inc(sdn[e % 2].h, 16)
                    self.s_gch.v = g0 + 1

                @blk.sync
                def _(sy):
                    for e in range(NEX):
                        for c in range(nch):
                            t0 = b * L + c * TC
                            i = e * nch + c
                            if i >= 1:
                                e2, c2 = divmod(i - 1, nch)
                                sy.wait_ge(smm.h, m0 + MC * e2 + MV * c2 + 33)
                            sy.dma_start(
                                xcn[i % 2].ap().rearrange("p (dt t) -> p dt t", dt=DT),
                                xn_src.rearrange("dt p t -> p dt t")[:, :, t0:t0 + TC]
                            ).then_inc(slp[i % 2].h, 16)
                    sy.wait_ge(sac.h, a0 + AC * NEX)

                @blk.tensor
                def _(t):
                    for e in range(NEX):
                        eb_m = m0 + MC * e
                        t.wait_ge(sgp[e % 2].h, gp0[e % 2] + 16 * 2 * (e // 2 + 1))
                        for c in range(nch):
                            cb_m = eb_m + MV * c
                            cb_e = e0 + EC * e + EV * c
                            cb_a = a0 + AC * e + AV * c
                            i = e * nch + c
                            if i >= 1:
                                t.wait_ge(sac.h, a0 + AC * e + AV * (c - 1) + 1
                                          if c >= 1 else a0 + AC * (e - 1) + AV * (nch - 1) + 1)
                            if e < NE:
                                t.matmul(pbc[:], self.selb[:, e * P:(e + 1) * P],
                                         msr[0:NE, c * TC: c * TC + TC],
                                         start=True, stop=True).then_inc(smm.h, 1)
                            else:
                                t.matmul(pbc[:], self.ones_col[:],
                                         ones_row[0:1, 0:TC],
                                         start=True, stop=True).then_inc(smm.h, 1)
                            t.wait_ge(sev.h, cb_e + 8)
                            for ft in range(FT):
                                if ft >= 2 or i >= 1:
                                    pft = (cb_a + ft - 2 + 2) if ft >= 2 else \
                                          (a0 + AC * e + AV * (c - 1) + ft + 14 + 2
                                           if c >= 1 else
                                           a0 + AC * (e - 1) + AV * (nch - 1) + ft + 14 + 2)
                                    t.wait_ge(sac.h, pft)
                                for it in range(DT):
                                    ins = t.matmul(
                                        pg[ft % 2][:],
                                        wga[:, it * FF + ft * P: it * FF + (ft + 1) * P],
                                        xs[i % 2][:, it * TC:(it + 1) * TC],
                                        start=(it == 0), stop=(it == DT - 1))
                                    if it == DT - 1:
                                        ins.then_inc(smm.h, 1)
                                if ft >= 2 or i >= 1:
                                    pft = (cb_e + 8 + ft - 2 + 1) if ft >= 2 else \
                                          (e0 + EC * e + EV * (c - 1) + 8 + ft + 14 + 1
                                           if c >= 1 else
                                           e0 + EC * (e - 1) + EV * (nch - 1) + 8 + ft + 14 + 1)
                                    t.wait_ge(sev.h, pft)
                                for it in range(DT):
                                    ins = t.matmul(
                                        pu[ft % 2][:],
                                        wga[:, DT * FF + it * FF + ft * P:
                                            DT * FF + it * FF + (ft + 1) * P],
                                        xs[i % 2][:, it * TC:(it + 1) * TC],
                                        start=(it == 0), stop=(it == DT - 1))
                                    if it == DT - 1:
                                        ins.then_inc(smm.h, 1)
                            if c == 0:
                                t.wait_ge(sdn[e % 2].h, dn0[e % 2] + 16 * (e // 2 + 1))
                            t.wait_ge(sev.h, cb_e + 8 + 16)
                            for ot in range(DT):
                                if ot >= 2 or i >= 1:
                                    pot = (cb_e + 24 + ot - 2 + 1) if ot >= 2 else \
                                          (e0 + EC * e + EV * (c - 1) + 24 + ot + 6 + 1
                                           if c >= 1 else
                                           e0 + EC * (e - 1) + EV * (nch - 1) + 24 + ot + 6 + 1)
                                    t.wait_ge(sev.h, pot)
                                for ft in range(FT):
                                    ins = t.matmul(
                                        pd[ot % 2][:],
                                        wdn[:, ft * D + ot * P: ft * D + (ot + 1) * P],
                                        h_all[:, ft * TC:(ft + 1) * TC],
                                        start=(ft == 0), stop=(ft == FT - 1))
                                    if ft == FT - 1:
                                        ins.then_inc(smm.h, 1)

                @blk.scalar
                def _(s):
                    for e in range(NEX):
                        for c in range(nch):
                            cb_m = m0 + MC * e + MV * c
                            cb_a = a0 + AC * e + AV * c
                            i = e * nch + c
                            s.wait_ge(smm.h, cb_m + 1)
                            if i >= 1:
                                e2, c2 = divmod(i - 1, nch)
                                s.wait_ge(sev.h, e0 + EC * e2 + EV * c2 + 8)
                            s.activation(bc[i % 2][:], pbc[:], AF.Copy
                                         ).then_inc(sac.h, 1)
                            for ft in range(FT):
                                s.wait_ge(smm.h, cb_m + 2 * ft + 2)
                                if ft >= 1:
                                    s.wait_ge(sev.h, e0 + EC * e + EV * c + 8 + ft)
                                elif i >= 1:
                                    pv = (e0 + EC * e + EV * (c - 1) + 8 + 16
                                          if c >= 1 else
                                          e0 + EC * (e - 1) + EV * (nch - 1) + 8 + 16)
                                    s.wait_ge(sev.h, pv)
                                s.activation(hs[ft % 2][:], pg[ft % 2][:], AF.Silu
                                             ).then_inc(sac.h, 1)
                    s.wait_ge(sev.h, e0 + EC * NEX)
                    for dt in range(DT):
                        s.activation(g_ct[:, dt * L:(dt + 1) * L],
                                     g_ct[:, dt * L:(dt + 1) * L], AF.Silu
                                     ).then_inc(sac.h, 1)

                @blk.vector
                def _(v):
                    for e in range(NEX):
                        for c in range(nch):
                            cb_m = m0 + MC * e + MV * c
                            cb_e = e0 + EC * e + EV * c
                            cb_a = a0 + AC * e + AV * c
                            i = e * nch + c
                            v.wait_ge(slp[i % 2].h, lp0[i % 2] + 16 * (i // 2 + 1))
                            v.wait_ge(sac.h, cb_a + 1)
                            for dt in range(DT):
                                v.tensor_tensor(xs[i % 2][:, dt * TC:(dt + 1) * TC],
                                                xs[i % 2][:, dt * TC:(dt + 1) * TC],
                                                bc[i % 2][:], ALU.mult
                                                ).then_inc(sev.h, 1)
                            if i >= 1:
                                v.wait_ge(smm.h, m0 + MC * e + MV * (c - 1) + 41
                                          if c >= 1 else m0 + MC * e)
                            for ft in range(FT):
                                v.wait_ge(sac.h, cb_a + 1 + ft + 1)
                                v.wait_ge(smm.h, cb_m + 2 * ft + 3)
                                v.tensor_tensor(h_all[:, ft * TC:(ft + 1) * TC],
                                                hs[ft % 2][:], pu[ft % 2][:],
                                                ALU.mult).then_inc(sev.h, 1)
                            if i == 0:
                                v.wait_ge(sgc.h, g0 + 1)
                            for ot in range(DT):
                                v.wait_ge(smm.h, cb_m + 33 + ot + 1)
                                sl = g_ct[:, ot * L + c * TC: ot * L + c * TC + TC]
                                v.tensor_tensor(sl, sl, pd[ot % 2][:], ALU.add
                                                ).then_inc(sev.h, 1)

            slp[0].v = lp0[0] + 16 * ((NEX * nch + 1) // 2)
            slp[1].v = lp0[1] + 16 * ((NEX * nch) // 2)
            for par in range(2):
                n_e = (NEX + 1) // 2 if par == 0 else NEX // 2
                sgp[par].v = gp0[par] + 32 * n_e
                sdn[par].v = dn0[par] + 16 * n_e
            smm.v = m0 + MC * NEX
            sev.v = e0 + EC * NEX
            sac.v = a0 + AC * NEX + DT
        moestk.close()

    def emit_combine(self, b, s_src, g_ct, x_src, dst):
        nc = self.nc
        smm, sev, sac = self.s_mm, self.s_ev, self.s_ac
        slp, slp2, ssp = self.s_lp, self.s_lp2, self.s_sp
        n = NCH
        with ExitStack() as stk:
            def SB(nm, shape, dt_):
                return stk.enter_context(nc.sbuf_tensor(self.nm(nm), shape, dt_))

            def PS(nm, shape, dt_=F32):
                return stk.enter_context(nc.psum_tensor(self.nm(nm), shape, dt_))

            scn = [SB(f"e_s{i}", [P, D], BF16) for i in range(2)]
            xtl = [SB(f"e_x{i}", [P, D], F32) for i in range(2)]
            pr = [SB(f"e_pr{i}", [P, D], BF16) for i in range(2)]
            sgt = [SB(f"e_sgt{i}", [P, D], BF16) for i in range(2)]
            x1t = [SB(f"e_x1{i}", [P, D], F32) for i in range(2)]
            ptr = PS("e_ptr", [P, D], BF16)
            m0, e0, a0 = smm.v, sev.v, sac.v
            lp0 = [s.v for s in slp]
            lq0 = [s.v for s in slp2]
            sp0 = [s.v for s in ssp]
            MV, EV, AV = 1, 2, 1

            with nc.Block(f"cmb_{b}") as blk:
                @blk.sync
                def _(sy):
                    for i in range(n):
                        t0 = b * L + i * P
                        if i >= 2:
                            sy.wait_ge(sev.h, e0 + EV * (i - 2) + 1)
                        sy.dma_start(
                            scn[i % 2].ap().rearrange("p (dt t) -> p dt t", dt=DT),
                            s_src.rearrange("dt p t -> p dt t")[:, :, t0:t0 + P]
                        ).then_inc(slp[i % 2].h, 16)
                        if i >= 2:
                            sy.wait_ge(sev.h, e0 + EV * (i - 2) + 2)
                        sy.dma_start(xtl[i % 2][:], x_src[b, i * P:(i + 1) * P, :]
                                     ).then_inc(slp2[i % 2].h, 16)
                    for par in range(2):
                        sy.wait_ge(ssp[par].h, sp0[par] + 16 * (n // 2))

                @blk.vector
                def _(v):
                    for i in range(n):
                        v.wait_ge(slp[i % 2].h, lp0[i % 2] + 16 * (i // 2 + 1))
                        if i >= 2:
                            v.wait_ge(smm.h, m0 + MV * (i - 2) + 1)
                        for dt in range(DT):
                            ins = v.tensor_tensor(
                                pr[i % 2][:, dt * P:(dt + 1) * P],
                                scn[i % 2][:, dt * P:(dt + 1) * P],
                                g_ct[:, dt * L + i * P: dt * L + (i + 1) * P],
                                ALU.mult)
                            if dt == DT - 1:
                                ins.then_inc(sev.h, 1)
                        v.wait_ge(slp2[i % 2].h, lq0[i % 2] + 16 * (i // 2 + 1))
                        v.wait_ge(sac.h, a0 + AV * i + 1)
                        if i >= 2:
                            v.wait_ge(ssp[i % 2].h, sp0[i % 2] + 16 * ((i - 2) // 2 + 1))
                        v.tensor_tensor(x1t[i % 2][:], xtl[i % 2][:], sgt[i % 2][:],
                                        ALU.add).then_inc(sev.h, 1)

                @blk.tensor
                def _(t):
                    for i in range(n):
                        t.wait_ge(sev.h, e0 + EV * i + 1)
                        if i >= 2:
                            pass
                        if i >= 1:
                            t.wait_ge(sac.h, a0 + AV * (i - 1) + 1)
                        for dt in range(DT):
                            ins = t.transpose(ptr[:, dt * P:(dt + 1) * P],
                                              pr[i % 2][:, dt * P:(dt + 1) * P],
                                              self.ident[:])
                            if dt == DT - 1:
                                ins.then_inc(smm.h, 1)

                @blk.scalar
                def _(s):
                    for i in range(n):
                        s.wait_ge(smm.h, m0 + MV * i + 1)
                        if i >= 2:
                            s.wait_ge(sev.h, e0 + EV * (i - 2) + 2)
                        s.activation(sgt[i % 2][:], ptr[:], AF.Copy
                                     ).then_inc(sac.h, 1)

                @blk.gpsimd
                def _(g):
                    for i in range(n):
                        g.wait_ge(sev.h, e0 + EV * i + 2)
                        g.dma_start(dst[b, i * P:(i + 1) * P, :], x1t[i % 2][:]
                                    ).then_inc(ssp[i % 2].h, 16)

            for par in range(2):
                slp[par].v = lp0[par] + 16 * (n // 2)
                slp2[par].v = lq0[par] + 16 * (n // 2)
                ssp[par].v = sp0[par] + 16 * (n // 2)
            smm.v = m0 + MV * n
            sev.v = e0 + EV * n
            sac.v = a0 + AV * n

    def emit_conv_tables(self, lstk):
        nc = self.nc
        sev, sac, smm, sg = self.s_ev, self.s_ac, self.s_mm, self.s_g
        svc, ssc = self.s_vch, self.s_sch
        sgl = self.s_gld
        NM = NMODE

        def LSB(nm, shape, dt_):
            return lstk.enter_context(nc.sbuf_tensor(self.nm(nm), shape, dt_))

        self.Tre = LSB("Tre", [NM, 132], F32)
        self.Tim = LSB("Tim", [NM, 132], F32)
        self.Trev = LSB("Trev", [NM, 2 * P], F32)
        self.Fstack = LSB("Fstack", [P, P], BF16)
        self.Estack = LSB("Estack", [P, P], BF16)
        self.A_bd = LSB("A_bd", [P, P], F32)
        self.cstack = LSB("cstack", [P, D], F32)
        self.k_ext = LSB("k_ext", [P, DT * 256], BF16)
        with ExitStack() as stk:
            def SB(nm, shape, dt_):
                return stk.enter_context(nc.sbuf_tensor(self.nm(nm), shape, dt_))

            def PS(nm, shape, dt_=F32):
                return stk.enter_context(nc.psum_tensor(self.nm(nm), shape, dt_))

            fr = SB("t_fr", [NM, 8], F32)
            tt = SB("t_tt", [NM, 140], F32)
            ipb = SB("t_ip", [P, DT * NM], F32)
            opb = SB("t_op", [NM, D], F32)
            cct = SB("t_cct", [P, DT * NM], F32)
            io_r = SB("t_ior", [P, P], F32)
            io_c = SB("t_ioc", [P, 1], F32)
            oc = SB("t_oc", [P, 1], F32)
            are2 = SB("t_are2", [P, 1], F32)
            aim2 = SB("t_aim2", [P, 1], F32)
            negim = SB("t_negim", [NM, 132], F32)
            pt1 = PS("t_pt1", [P, P])
            pt2 = PS("t_pt2", [P, 256])
            l0 = sgl.v
            g0 = sg.v
            e0, a0, m0 = sev.v, sac.v, smm.v

            with nc.Block("ctab") as blk:
                @blk.gpsimd
                def _(g):
                    g.iota(io_r[:], [[1, P]], channel_multiplier=0,
                           allow_small_or_imprecise_dtypes=True)
                    g.iota(io_c[:], [[0, 1]], channel_multiplier=1,
                           allow_small_or_imprecise_dtypes=True)
                    g.memset(self.A_bd[:], 0.0)
                    g.memset(self.k_ext[:], 0.0)
                    g.memset(self.Tre[:], 0.0)
                    g.memset(self.Tim[:], 0.0).then_inc(sg.h, 1)
                    g.dma_start(fr[:, 0:1], self.w["frequencies"][:]).then_inc(sgl.h, 16)
                    g.dma_start(fr[:, 5:6], self.w["decays"][:]).then_inc(sgl.h, 16)
                    g.dma_start(ipb.ap().rearrange("p (dt n) -> p dt n", dt=DT),
                                self.w["in_proj"].rearrange("(dt p) n -> p dt n", p=P)
                                ).then_inc(sgl.h, 16)
                    g.dma_start(opb[:], self.w["out_proj"]).then_inc(sgl.h, 16)
                    g.wait_ge(sev.h, e0 + 3)
                    g.dma_start(oc[0:NM, :], io_c[NM:2 * NM, :]).then_inc(sgl.h, 16)
                    g.dma_start(oc[NM:2 * NM, :], io_c[0:NM, :]).then_inc(sgl.h, 16)
                    g.wait_ge(sev.h, e0 + 20)
                    g.dma_start(are2[0:NM, :], self.Tre[:, 128:129]).then_inc(sgl.h, 16)
                    g.dma_start(are2[NM:2 * NM, :], self.Tre[:, 128:129]).then_inc(sgl.h, 16)
                    g.dma_start(aim2[0:NM, :], self.Tim[:, 128:129]).then_inc(sgl.h, 16)
                    g.dma_start(aim2[NM:2 * NM, :], negim[:, 128:129]).then_inc(sgl.h, 16)
                    g.dma_start(self.Estack[NM:2 * NM, :], negim[:, 1:129]
                                ).then_inc(sgl.h, 16)
                    g.wait_ge(sev.h, e0 + 37)
                    g.dma_start(self.cstack[NM:2 * NM, :], self.cstack[0:NM, :]
                                ).then_inc(sgl.h, 16)
                    g.wait_ge(sgl.h, l0 + 16 * 12)

                @blk.vector
                def _(v):
                    v.wait_ge(sgl.h, l0 + 64)
                    v.wait_ge(sg.h, g0 + 1)
                    vch = self.s_vch

                    def vsync(ins):
                        ins.then_inc(vch.h, 1)
                        vch.v += 1
                        v.wait_ge(vch.h, vch.v)

                    v.tensor_scalar(fr[:, 1:2], fr[:, 0:1], TWO_PI, None, ALU.mult)
                    vsync(v.tensor_scalar(fr[:, 2:3], fr[:, 0:1], MAGIC, MAGIC,
                                          ALU.add, ALU.subtract))
                    v.tensor_copy(tt[:, 129:130], self.cst[0:NM, 0:1]).then_inc(sev.h, 1)
                    vsync(v.scalar_tensor_tensor(fr[:, 3:4], fr[:, 2:3], -CW1,
                                                 fr[:, 1:2], ALU.mult, ALU.add))
                    vsync(v.scalar_tensor_tensor(fr[:, 3:4], fr[:, 2:3], -CW2,
                                                 fr[:, 3:4], ALU.mult, ALU.add))
                    vsync(v.scalar_tensor_tensor(fr[:, 3:4], fr[:, 2:3], -CW3,
                                                 fr[:, 3:4], ALU.mult, ALU.add))
                    v.tensor_copy(tt[:, 129:130], self.cst[0:NM, 0:1]).then_inc(sev.h, 1)
                    vsync(v.tensor_scalar(fr[:, 4:5], fr[:, 3:4], math.pi / 2.0,
                                          None, ALU.add))
                    v.tensor_scalar(tt[:, 132:133], fr[:, 4:5], -math.pi, None,
                                    ALU.is_lt)
                    vsync(v.tensor_scalar(tt[:, 133:134], fr[:, 4:5], math.pi, None,
                                          ALU.is_gt))
                    vsync(v.tensor_tensor(tt[:, 132:133], tt[:, 132:133],
                                          tt[:, 133:134], ALU.subtract))
                    v.scalar_tensor_tensor(fr[:, 4:5], tt[:, 132:133], TWO_PI,
                                           fr[:, 4:5], ALU.mult, ALU.add
                                           ).then_inc(sev.h, 1)
                    v.memset(self.Tre[:, 0:1], 1.0)
                    v.wait_ge(sac.h, a0 + 4)
                    v.tensor_tensor(self.Tre[:, 1:2], fr[:, 7:8], tt[:, 130:131],
                                    ALU.mult)
                    v.tensor_tensor(self.Tim[:, 1:2], fr[:, 7:8], tt[:, 131:132],
                                    ALU.mult).then_inc(svc.h, 1)
                    self.s_vch.v += 1
                    nsz = 1
                    vch = self.s_vch
                    while nsz < 128:
                        v.wait_ge(vch.h, vch.v)
                        v.tensor_scalar(tt[:, 128:129], self.Tim[:, nsz:nsz + 1],
                                        -1.0, None, ALU.mult)
                        k = min(nsz, 128 - nsz)
                        v.tensor_scalar(tt[:, 0:k], self.Tre[:, 1:k + 1],
                                        self.Tre[:, nsz:nsz + 1], None, ALU.mult
                                        ).then_inc(vch.h, 1)
                        vch.v += 1
                        v.wait_ge(vch.h, vch.v)
                        v.scalar_tensor_tensor(
                            tt[:, 64:64 + k], self.Tim[:, 1:k + 1],
                            self.Tim[:, nsz:nsz + 1], tt[:, 0:k]
                            if False else None, ALU.mult, ALU.bypass) \
                            if False else None
                        v.scalar_tensor_tensor(
                            self.Tre[:, nsz + 1:nsz + k + 1],
                            self.Tim[:, 1:k + 1], tt[:, 128:129], tt[:, 0:k],
                            ALU.mult, ALU.add)
                        v.tensor_scalar(tt[:, 64:64 + k], self.Tre[:, 1:k + 1],
                                        self.Tim[:, nsz:nsz + 1], None, ALU.mult
                                        ).then_inc(vch.h, 1)
                        vch.v += 1
                        v.wait_ge(vch.h, vch.v)
                        v.scalar_tensor_tensor(
                            self.Tim[:, nsz + 1:nsz + k + 1],
                            self.Tim[:, 1:k + 1], self.Tre[:, nsz:nsz + 1],
                            tt[:, 64:64 + k], ALU.mult, ALU.add).then_inc(vch.h, 1)
                        vch.v += 1
                        nsz = nsz + k
                    v.wait_ge(vch.h, vch.v)
                    v.tensor_scalar(negim[:], self.Tim[:, 0:132], -1.0, None,
                                    ALU.mult)
                    v.tensor_copy(self.Trev[:, 0:P],
                                  bass.AP(self.Tre, 127, [[132, NM], [-1, P]]))
                    v.tensor_copy(self.Trev[:, P:2 * P],
                                  bass.AP(self.Tim, 127, [[132, NM], [-1, P]]))
                    v.tensor_copy(self.Estack[0:NM, :], self.Tre[:, 1:129])
                    v.tensor_copy(tt[:, 129:130], self.cst[0:NM, 0:1]
                                  ).then_inc(sev.h, 17)
                    v.wait_ge(sgl.h, l0 + 16 * 11)
                    v.tensor_scalar(self.A_bd[:], io_r[:], io_c[:, 0:1], are2[:, 0:1],
                                    ALU.is_equal, ALU.mult).then_inc(svc.h, 1)
                    vch.v += 1
                    v.wait_ge(vch.h, vch.v)
                    v.tensor_scalar(tt[0:P, 0:P] if False else io_r[:], io_r[:],
                                    oc[:, 0:1], aim2[:, 0:1], ALU.is_equal, ALU.mult)
                    v.tensor_copy(tt[:, 129:130], self.cst[0:NM, 0:1]).then_inc(svc.h, 1)
                    vch.v += 1
                    v.wait_ge(vch.h, vch.v)
                    v.tensor_tensor(self.A_bd[:], self.A_bd[:], io_r[:], ALU.add
                                    ).then_inc(sev.h, 1)
                    for dt in range(DT):
                        v.wait_ge(smm.h, m0 + dt + 1)
                        v.tensor_tensor(cct[:, dt * NM:(dt + 1) * NM],
                                        ipb[:, dt * NM:(dt + 1) * NM],
                                        pt1[:, 0:NM], ALU.mult).then_inc(sev.h, 1)
                    for dt in range(DT):
                        v.wait_ge(smm.h, m0 + DT + dt + 1)
                        v.tensor_copy(self.cstack[0:NM, dt * P:(dt + 1) * P],
                                      pt2[0:NM, 0:P]).then_inc(sev.h, 1)

                @blk.scalar
                def _(s):
                    s.wait_ge(sgl.h, l0 + 64)
                    s.activation(fr[:, 6:7], fr[:, 5:6], AF.Exp).then_inc(ssc.h, 1)
                    self.s_sch.v += 1
                    s.wait_ge(ssc.h, self.s_sch.v)
                    s.activation(fr[:, 7:8], fr[:, 6:7], AF.Exp, scale=-1.0
                                 ).then_inc(sac.h, 1)
                    s.wait_ge(sev.h, e0 + 2)
                    s.activation(tt[:, 131:132], fr[:, 3:4], AF.Sin).then_inc(sac.h, 1)
                    s.wait_ge(sev.h, e0 + 3)
                    s.activation(tt[:, 130:131], fr[:, 4:5], AF.Sin).then_inc(sac.h, 2)

                @blk.tensor
                def _(t):
                    t.wait_ge(sgl.h, l0 + 64)
                    for dt in range(DT):
                        if dt >= 1:
                            t.wait_ge(sev.h, e0 + 20 + 1 + dt)
                        t.transpose(pt1[:, 0:NM], opb[:, dt * P:(dt + 1) * P],
                                    self.identf[0:NM, 0:NM]).then_inc(smm.h, 1)
                    for dt in range(DT):
                        t.wait_ge(sev.h, e0 + 21 + dt + 1)
                        if dt >= 1:
                            t.wait_ge(sev.h, e0 + 29 + dt)
                        t.transpose(pt2[0:NM, 0:P], cct[:, dt * NM:(dt + 1) * NM],
                                    self.identf[:]).then_inc(smm.h, 1)

            sev.v = e0 + 20 + 1 + 2 * DT
            sac.v = a0 + 4
            smm.v = m0 + 2 * DT
            sgl.v = l0 + 16 * 12
            sg.v = g0 + 1

        with ExitStack() as stk:
            pt3 = stk.enter_context(nc.psum_tensor(self.nm("t_pt3"), [P, P], F32))
            pk = stk.enter_context(nc.psum_tensor(self.nm("t_pk"), [P, P], F32))
            krev = stk.enter_context(nc.sbuf_tensor(self.nm("t_krev"), [P, DT * 256], BF16))
            e0, a0, m0 = sev.v, sac.v, smm.v
            with nc.Block("ctab2") as blk:
                @blk.tensor
                def _(t):
                    t.transpose(pt3[:, 0:NM], self.Trev[:, 0:P],
                                self.identf[0:NM, 0:NM]).then_inc(smm.h, 1)
                    t.wait_ge(sev.h, e0 + 1)
                    t.transpose(pt3[:, 0:NM], self.Trev[:, P:2 * P],
                                self.identf[0:NM, 0:NM]).then_inc(smm.h, 1)
                    for dt in range(DT):
                        if dt >= 1:
                            t.wait_ge(sac.h, a0 + dt)
                        t.matmul(pk[:, 0:P], self.cstack[0:NM, dt * P:(dt + 1) * P],
                                 self.Tre[:, 0:P], start=True, stop=True
                                 ).then_inc(smm.h, 1)

                @blk.vector
                def _(v):
                    v.memset(krev[:], 0.0)
                    v.wait_ge(smm.h, m0 + 1)
                    v.tensor_copy(self.Fstack[:, 0:NM], pt3[:, 0:NM]).then_inc(sev.h, 1)
                    v.wait_ge(smm.h, m0 + 2)
                    v.tensor_copy(self.Fstack[:, NM:2 * NM], pt3[:, 0:NM]
                                  ).then_inc(sev.h, 1)
                    v.wait_ge(sac.h, a0 + DT)
                    v.tensor_copy(
                        bass.AP(krev, 0, [[DT * 256, P], [256, DT], [1, 255]]),
                        bass.AP(self.k_ext, 254, [[DT * 256, P], [256, DT], [-1, 255]])
                    ).then_inc(sev.h, 1)

                @blk.scalar
                def _(s):
                    for dt in range(DT):
                        s.wait_ge(smm.h, m0 + 2 + dt + 1)
                        s.activation(self.k_ext[:, dt * 256 + 127: dt * 256 + 255],
                                     pk[:, 0:P], AF.Copy).then_inc(sac.h, 1)

                @blk.gpsimd
                def _(g):
                    g.wait_ge(sev.h, e0 + 3)
                    g.dma_start(
                        self.k_dram.rearrange("(dt p) j -> p dt j", p=P),
                        krev.ap().rearrange("p (dt j) -> p dt j", dt=DT)
                    ).then_inc(self.s_gld.h, 16)
                    g.wait_ge(self.s_gld.h, self.s_gld.v + 16)
            self.s_gld.v += 16
            sev.v = e0 + 3
            sac.v = a0 + DT
            smm.v = m0 + 2 + DT

    def emit_conv(self, b, xn_tc, y_ct, y_spill):
        nc = self.nc
        smm, sev, sac = self.s_mm, self.s_ev, self.s_ac
        sgp, ssp = self.s_gp, self.s_sp
        with ExitStack() as stk:
            def SB(nm, shape, dt_):
                return stk.enter_context(nc.sbuf_tensor(self.nm(nm), shape, dt_))

            def PS(nm, shape, dt_=F32):
                return stk.enter_context(nc.psum_tensor(self.nm(nm), shape, dt_))

            S_sb = SB("cv_S", [P, D], F32)
            PM = SB("cv_PM", [P, D], BF16)
            Ibuf = SB("cv_I", [P, P * NCH], BF16)
            Kb8 = [SB(f"cv_K{j}", [P, 16 * P], BF16) for j in range(8)]
            pstate = PS("cv_ps", [P, D])
            py = PS("cv_py", [P, D])
            pI = [PS(f"cv_pI{i}", [P, 512]) for i in range(2)]
            ptb = [PS(f"cv_pt{i}", [P, P], BF16) for i in range(2)]
            m0, e0, a0 = smm.v, sev.v, sac.v
            g0 = sgp[0].v, sgp[1].v

            with nc.Block(f"cvB1_{b}") as blk:
                @blk.gpsimd
                def _(g):
                    g.memset(S_sb[:], 0.0)
                    g.memset(PM[:], 0.0).then_inc(self.s_gch.h, 1)
                    self.s_gch.v += 1

                gch0 = self.s_gch.v

                @blk.tensor
                def _(t):
                    t.wait_ge(self.s_gch.h, gch0)
                    for c in range(NCH):
                        if c >= 1:
                            t.wait_ge(sev.h, e0 + 3 * c)
                            t.wait_ge(sac.h, a0 + c)
                        for ht in range(DT):
                            ins = t.matmul(py[:, ht * P:(ht + 1) * P],
                                           PM[:, ht * P:(ht + 1) * P],
                                           self.Estack[:], start=True, stop=True)
                            if ht == DT - 1:
                                ins.then_inc(smm.h, 1)
                        if c >= 1:
                            t.wait_ge(sev.h, e0 + 3 * c)
                        for half in range(2):
                            t.matmul(pstate[:, half * 512:(half + 1) * 512],
                                     self.A_bd[:], S_sb[:, half * 512:(half + 1) * 512],
                                     start=True, stop=False)
                            ins = t.matmul(pstate[:, half * 512:(half + 1) * 512],
                                           self.Fstack[:],
                                           xn_tc[:, c * D + half * 512:
                                                 c * D + (half + 1) * 512],
                                           start=False, stop=True)
                            if half == 1:
                                ins.then_inc(smm.h, 1)

                @blk.scalar
                def _(s):
                    for c in range(NCH):
                        s.wait_ge(smm.h, m0 + 2 * c + 1)
                        s.activation(
                            y_ct.ap().rearrange("p (dt t) -> p dt t", dt=DT)
                                [:, :, c * P:(c + 1) * P],
                            py.ap().rearrange("p (dt q) -> p dt q", dt=DT),
                            AF.Copy).then_inc(sac.h, 1)

                @blk.vector
                def _(v):
                    for c in range(NCH):
                        v.wait_ge(smm.h, m0 + 2 * c + 2)
                        v.tensor_tensor(PM[:], self.cstack[:], pstate[:], ALU.mult
                                        ).then_inc(sev.h, 1)
                        v.tensor_copy(S_sb[:], pstate[:]).then_inc(sev.h, 2)
            smm.v = m0 + 2 * NCH
            sac.v = a0 + NCH
            sev.v = e0 + 3 * NCH

            m0, e0, a0 = smm.v, sev.v, smm.v
            gp0 = [s.v for s in sgp]
            with nc.Block(f"cvB2_{b}") as blk:
                NG = DT
                NB16 = P // 16
                MVg = NB16 + NCH
                EVg = NB16 + NCH

                @blk.gpsimd
                def _(g):
                    for gi in range(NG):
                        if gi >= 1:
                            g.wait_ge(smm.h, m0 + MVg * (gi - 1) + NB16)
                        for hh in range(8):
                            src = bass.AP(
                                self.k_dram_h,
                                (gi * P + hh * 16) * 256,
                                [[1, P], [256, 16], [1, P]])
                            dst = bass.AP(
                                Kb8[hh], 0,
                                [[16 * P, P], [P, 16], [1, P]])
                            g.dma_start(dst, src).then_inc(sgp[gi % 2].h, 16)

                @blk.tensor
                def _(t):
                    for gi in range(NG):
                        gb_m = m0 + MVg * gi
                        gb_e = e0 + EVg * gi
                        t.wait_ge(sgp[gi % 2].h, gp0[gi % 2] + 16 * 8 * (gi // 2 + 1))
                        for b16 in range(NB16):
                            if b16 >= 2 or gi >= 1:
                                pv = (gb_e + b16 - 2 + 1) if b16 >= 2 else \
                                     (e0 + EVg * (gi - 1) + b16 + 6 + 1)
                                t.wait_ge(sev.h, pv)
                            for hh in range(16):
                                h0 = b16 * 16 + hh
                                rhs = bass.AP(xn_tc, gi * P + h0,
                                              [[NCH * D, P], [D, NCH]])
                                ins = t.matmul(
                                    pI[b16 % 2][:, hh * NCH:(hh + 1) * NCH],
                                    Kb8[b16][:, hh * P:(hh + 1) * P], rhs,
                                    start=True, stop=True)
                                if hh == 15:
                                    ins.then_inc(smm.h, 1)
                        t.wait_ge(sev.h, gb_e + NB16)
                        for c in range(NCH):
                            if c >= 2 or gi >= 1:
                                pv = (gb_e + NB16 + c - 2 + 1) if c >= 2 else \
                                     (e0 + EVg * (gi - 1) + NB16 + c + 30 + 1)
                                t.wait_ge(sev.h, pv)
                            ins = t.transpose(
                                ptb[c % 2][:],
                                bass.AP(Ibuf, c, [[P * NCH, P], [NCH, P]]),
                                self.ident[:])
                            ins.then_inc(smm.h, 1)

                @blk.vector
                def _(v):
                    for gi in range(NG):
                        gb_m = m0 + MVg * gi
                        gb_e = e0 + EVg * gi
                        for b16 in range(NB16):
                            v.wait_ge(smm.h, gb_m + b16 + 1)
                            if gi >= 1 and b16 == 0:
                                v.wait_ge(smm.h, m0 + MVg * (gi - 1) + MVg)
                            v.tensor_copy(Ibuf[:, b16 * 16 * NCH:(b16 + 1) * 16 * NCH],
                                          pI[b16 % 2][:]).then_inc(sev.h, 1)
                        for c in range(NCH):
                            v.wait_ge(smm.h, gb_m + NB16 + c + 1)
                            sl = y_ct[:, gi * L + c * P: gi * L + (c + 1) * P]
                            v.tensor_tensor(
                                sl, sl,
                                bass.AP(ptb[c % 2], 127, [[P, P], [-1, P]]),
                                ALU.add).then_inc(sev.h, 1)
            smm.v = m0 + MVg * NG
            sev.v = e0 + EVg * NG
            for par in range(2):
                sgp[par].v = gp0[par] + 16 * 8 * (NG // 2)

            sp0 = ssp[b % 2].v
            with nc.Block(f"cvB3_{b}") as blk:
                @blk.gpsimd
                def _(g):
                    g.dma_start(
                        y_spill.rearrange("dt p t -> dt p t")[:, :, b * L:(b + 1) * L]
                            .rearrange("dt p t -> p dt t"),
                        y_ct.ap().rearrange("p (dt t) -> p dt t", dt=DT)
                    ).then_inc(ssp[b % 2].h, 16)

                @blk.sync
                def _(sy):
                    sy.wait_ge(ssp[b % 2].h, sp0 + 16)
            ssp[b % 2].v = sp0 + 16

    def emit_wkv_tables(self):
        nc = self.nc
        sev, sac, smm, sgl = self.s_ev, self.s_ac, self.s_mm, self.s_gld
        vch, sch = self.s_vch, self.s_sch
        self.WmT = self.sb("WmT", [P, NH * 2 * RQ], BF16)
        self.wbrow = self.sb("wbrow", [P, NH * RQ], BF16)
        self.wbcol = self.sb("wbcol", [P, 2 * NH], F32)
        self.wkcol = self.sb("wkcol", [P, 2 * NH], F32)
        self.wscol = self.sb("wscol", [P, NH], F32)
        with ExitStack() as stk:
            def SB(nm, shape, dt_):
                return stk.enter_context(nc.sbuf_tensor(self.nm(nm), shape, dt_))

            tdr = SB("w_tdr", [1, NH], F32)
            tfr = SB("w_tfr", [1, NH], F32)
            lnw = SB("w_lnw", [P, NH], F32)
            nlw = SB("w_nlw", [P, NH], F32)
            ubc = SB("w_ubc", [P, NH], F32)
            io2 = SB("w_io2", [P, P], F32)
            iq = SB("w_iq", [P, RQ], F32)
            pcol = SB("w_pcol", [P, 2], F32)
            Mt = SB("w_M", [P, RQ], F32)
            Mc = SB("w_Mc", [P, RQ], F32)
            W1 = SB("w_W1", [P, RQ], F32)
            m1 = SB("w_m1", [P, RQ], F32)
            pb = stk.enter_context(nc.psum_tensor(self.nm("w_pb"), [P, NH], F32))
            l0, e0, a0, m0 = sgl.v, sev.v, sac.v, smm.v
            g0 = self.s_g.v

            with nc.Block("wtab") as blk:
                @blk.gpsimd
                def _(g):
                    g.dma_start(tdr[:], self.w["t_decay"][:]).then_inc(sgl.h, 16)
                    g.dma_start(tfr[:], self.w["t_faaaa"][:]).then_inc(sgl.h, 16)
                    g.iota(iq[:], [[1, RQ]], channel_multiplier=0,
                           allow_small_or_imprecise_dtypes=True)
                    g.iota(pcol[:, 0:1], [[0, 1]], channel_multiplier=1,
                           allow_small_or_imprecise_dtypes=True)
                    g.iota(pcol[:, 1:2], [[0, 1]], base=P, channel_multiplier=1,
                           allow_small_or_imprecise_dtypes=True).then_inc(self.s_g.h, 1)

                @blk.scalar
                def _(s):
                    s.wait_ge(sgl.h, l0 + 32)
                    s.activation(tdr[:], tdr[:], AF.Exp, scale=1.0
                                 ).then_inc(sch.h, 1)
                    self.s_sch.v += 1
                    s.wait_ge(sch.h, self.s_sch.v)
                    s.activation(tdr[:], tdr[:], AF.Copy, scale=-1.0
                                 ).then_inc(sac.h, 1) if False else None
                    s.mul(tdr[:], tdr[:], -1.0).then_inc(sac.h, 1)

                @blk.tensor
                def _(t):
                    t.wait_ge(sac.h, a0 + 1)
                    t.matmul(pb[:], self.ones_colf[:], tdr[:], start=True, stop=True
                             ).then_inc(smm.h, 1)
                    t.wait_ge(sev.h, e0 + 1)
                    t.matmul(pb[:], self.ones_colf[:], tfr[:], start=True, stop=True
                             ).then_inc(smm.h, 1)

                @blk.vector
                def _(v):
                    v.wait_ge(smm.h, m0 + 1)
                    v.tensor_copy(lnw[:], pb[:])
                    v.tensor_scalar(nlw[:], pb[:], -1.0, None, ALU.mult
                                    ).then_inc(sev.h, 1)
                    v.wait_ge(smm.h, m0 + 2)
                    v.tensor_copy(ubc[:], pb[:]).then_inc(sev.h, 1)
            sgl.v = l0 + 32
            sac.v = a0 + 1
            smm.v = m0 + 2
            sev.v = e0 + 2
            self.s_g.v = g0 + 1

            e0, a0 = sev.v, sac.v
            with nc.Block("wtab2") as blk:
                @blk.vector
                def _(v):
                    v.wait_ge(self.s_g.h, g0 + 1)
                    i = 0
                    for h in range(NH):
                        for pt in range(2):
                            if i >= 1:
                                v.wait_ge(sac.h, a0 + 2 * i)
                            v.tensor_scalar(Mt[:], iq[:], pcol[:, pt:pt + 1], None,
                                            ALU.subtract).then_inc(sev.h, 1)
                            v.wait_ge(sev.h, e0 + 3 * i + 1)
                            v.tensor_scalar(Mc[:], Mt[:], 0.0, None, ALU.max
                                            ).then_inc(sev.h, 1)
                            v.tensor_scalar(m1[:], Mt[:], 0.5, None, ALU.is_gt
                                            ).then_inc(self.s_vch.h, 1)
                            self.s_vch.v += 1
                            v.wait_ge(self.s_vch.h, self.s_vch.v)
                            v.wait_ge(sac.h, a0 + 2 * i + 2)
                            v.tensor_tensor(W1[:], W1[:], m1[:], ALU.mult
                                            ).then_inc(self.s_vch.h, 1)
                            self.s_vch.v += 1
                            v.wait_ge(self.s_vch.h, self.s_vch.v)
                            v.tensor_scalar(m1[:], Mt[:], 0.0, ubc[:, h:h + 1],
                                            ALU.is_equal, ALU.mult
                                            ).then_inc(self.s_vch.h, 1)
                            self.s_vch.v += 1
                            v.wait_ge(self.s_vch.h, self.s_vch.v)
                            v.tensor_tensor(
                                self.WmT[:, (h * 2 + pt) * RQ:(h * 2 + pt + 1) * RQ],
                                W1[:], m1[:], ALU.add).then_inc(sev.h, 1)
                            i += 1
                    v.wait_ge(sac.h, a0 + 2 * NH * 2)
                    v.wait_ge(sev.h, e0 + 3 * NH * 2)
                    v.tensor_scalar(W1[:, 0:NH], lnw[:], 255.0, None, ALU.mult
                                    ).then_inc(sev.h, 1)

                @blk.scalar
                def _(s):
                    i = 0
                    for h in range(NH):
                        for pt in range(2):
                            s.wait_ge(sev.h, e0 + 3 * i + 2)
                            s.activation(W1[:], Mc[:], AF.Exp,
                                         scale=lnw[:, h:h + 1], bias=nlw[:, h:h + 1]
                                         ).then_inc(sac.h, 2)
                            i += 1
                    s.wait_ge(sev.h, e0 + 3 * NH * 2 + 1)
                    for h in range(NH):
                        for pt in range(2):
                            s.activation(self.wbcol[:, h * 2 + pt:h * 2 + pt + 1],
                                         pcol[:, pt:pt + 1], AF.Exp,
                                         scale=lnw[:, h:h + 1])
                            s.activation(self.wkcol[:, h * 2 + pt:h * 2 + pt + 1],
                                         pcol[:, pt:pt + 1], AF.Exp,
                                         scale=nlw[:, h:h + 1],
                                         bias=W1[:, h:h + 1])
                        s.activation(self.wscol[:, h:h + 1], lnw[:, h:h + 1],
                                     AF.Exp, scale=256.0)
                        ins = s.activation(self.wbrow[:, h * RQ:(h + 1) * RQ],
                                           iq[:], AF.Exp, scale=lnw[:, h:h + 1])
                        if h == NH - 1:
                            ins.then_inc(sac.h, 1)
            sev.v = e0 + 3 * NH * 2 + 1
            sac.v = a0 + 2 * NH * 2 + 1

    def emit_lerp_proj(self, b, xn1_ct, wT, rkv):
        nc = self.nc
        smm, sev, sac = self.s_mm, self.s_ev, self.s_ac
        ssp = self.s_sp
        TC = 512
        nch = L // TC
        with ExitStack() as stk:
            def SB(nm, shape, dt_):
                return stk.enter_context(nc.sbuf_tensor(self.nm(nm), shape, dt_))

            def PS(nm, shape, dt_=F32):
                return stk.enter_context(nc.psum_tensor(self.nm(nm), shape, dt_))

            om = SB("g1_om", [P, 3 * DT], F32)
            xl = [SB(f"g1_x{i}", [P, DT * TC], BF16) for i in range(3)]
            tsc = SB("g1_ts", [P, 2 * TC], F32)
            stg = [SB(f"g1_s{i}", [P, DT * TC], BF16) for i in range(2)]
            pp = [PS(f"g1_p{i}", [P, TC]) for i in range(2)]
            m0, e0, a0 = smm.v, sev.v, sac.v
            sp0 = [s.v for s in ssp]
            names = ["maa_k", "maa_v", "maa_r"]
            bn = ["bk", "bv", "br"]
            MV, EV, AV = 24, 48, 24

            with nc.Block(f"g1_{b}") as blk:
                @blk.vector
                def _(v):
                    for j in range(3):
                        v.tensor_scalar(om[:, j * DT:(j + 1) * DT],
                                        self.vecs[names[j]][:], -1.0, 1.0,
                                        ALU.mult, ALU.add)
                    v.tensor_copy(tsc[:, 0:1], self.cst[:, 0:1]).then_inc(sev.h, 1)
                    for c in range(nch):
                        ce = e0 + 1 + EV * c
                        if c >= 1:
                            v.wait_ge(smm.h, m0 + MV * c)
                        for j in range(3):
                            for dt in range(DT):
                                ii = j * DT + dt
                                ts2 = tsc[:, (ii % 2) * TC:(ii % 2) * TC + TC]
                                if ii >= 2:
                                    v.wait_ge(sev.h, ce + 2 * (ii - 2) + 1)
                                elif c >= 1:
                                    v.wait_ge(sev.h, e0 + 1 + EV * (c - 1)
                                              + 2 * (ii + 22) + 1)
                                base = c * TC
                                xsl = xn1_ct[:, dt * L + base: dt * L + base + TC]
                                mc = self.vecs[names[j]][:, dt:dt + 1]
                                oc = om[:, j * DT + dt: j * DT + dt + 1]
                                dst = xl[j][:, dt * TC:(dt + 1) * TC]
                                if c == 0:
                                    sh = xn1_ct[:, dt * L: dt * L + TC - 1]
                                    v.tensor_scalar(ts2[:, 1:TC], sh, mc, None,
                                                    ALU.mult).then_inc(self.s_vch.h, 1)
                                    self.s_vch.v += 1
                                    v.wait_ge(self.s_vch.h, self.s_vch.v)
                                    v.scalar_tensor_tensor(
                                        dst[:, 1:TC], xsl[:, 1:TC], oc,
                                        ts2[:, 1:TC], ALU.mult, ALU.add
                                        ).then_inc(sev.h, 1)
                                    v.tensor_scalar(dst[:, 0:1], xsl[:, 0:1], oc,
                                                    None, ALU.mult).then_inc(sev.h, 1)
                                else:
                                    sh = xn1_ct[:, dt * L + base - 1:
                                                dt * L + base + TC - 1]
                                    v.tensor_scalar(ts2, sh, mc, None,
                                                    ALU.mult).then_inc(self.s_vch.h, 1)
                                    self.s_vch.v += 1
                                    v.wait_ge(self.s_vch.h, self.s_vch.v)
                                    v.scalar_tensor_tensor(
                                        dst, xsl, oc, ts2, ALU.mult, ALU.add
                                        ).then_inc(sev.h, 2)

                @blk.tensor
                def _(t):
                    for c in range(nch):
                        cb_e = e0 + 1 + EV * c
                        for j in range(3):
                            t.wait_ge(sev.h, cb_e + (j + 1) * 16)
                            for ot in range(DT):
                                i = c * MV + j * DT + ot
                                if i >= 2:
                                    c2, r2 = divmod(i - 2, MV)
                                    t.wait_ge(sac.h, a0 + AV * c2 + r2 + 1)
                                for it in range(DT):
                                    ins = t.matmul(
                                        pp[ot % 2][:],
                                        wT[:, j * DT * D + it * D + ot * P:
                                           j * DT * D + it * D + (ot + 1) * P],
                                        xl[j][:, it * TC:(it + 1) * TC],
                                        start=(it == 0), stop=(it == DT - 1))
                                    if it == DT - 1:
                                        ins.then_inc(smm.h, 1)

                @blk.scalar
                def _(s):
                    for c in range(nch):
                        cb_m, cb_a = m0 + MV * c, a0 + AV * c
                        for j in range(3):
                            cj = c * 3 + j
                            for ot in range(DT):
                                s.wait_ge(smm.h, cb_m + j * DT + ot + 1)
                                if cj >= 2 and ot == 0:
                                    s.wait_ge(ssp[cj % 2].h,
                                              sp0[cj % 2] + 16 * ((cj - 2) // 2 + 1))
                                s.activation(stg[cj % 2][:, ot * TC:(ot + 1) * TC],
                                             pp[ot % 2][:], AF.Identity, scale=1.0,
                                             bias=self.vecs[bn[j]][:, ot:ot + 1]
                                             ).then_inc(sac.h, 1)

                @blk.gpsimd
                def _(g):
                    for c in range(nch):
                        for j in range(3):
                            cj = c * 3 + j
                            g.wait_ge(sac.h, a0 + AV * c + (j + 1) * DT)
                            t0 = b * L + c * TC
                            g.dma_start(
                                rkv[j].rearrange("dt p t -> p dt t")[:, :, t0:t0 + TC],
                                stg[cj % 2].ap().rearrange(
                                    "p (dt t) -> p dt t", dt=DT)
                            ).then_inc(ssp[cj % 2].h, 16)

                @blk.sync
                def _(sy):
                    for par in range(2):
                        sy.wait_ge(ssp[par].h, sp0[par] + 16 * (nch * 3 // 2))

            for par in range(2):
                ssp[par].v = sp0[par] + 16 * (nch * 3 // 2)
            smm.v = m0 + MV * nch
            sev.v = e0 + 1 + EV * nch
            sac.v = a0 + AV * nch

    def emit_wkv(self, b, rkv, yt_dram):
        nc = self.nc
        smm, sev, sac, sgc = self.s_mm, self.s_ev, self.s_ac, self.s_gch
        slp, slp2, slp3, slp4, ssp = (self.s_lp, self.s_lp2, self.s_lp3,
                                      self.s_lp4, self.s_sp)
        NLT = NB * L
        NHT = NRC * NH
        with ExitStack() as stk:
            def SB(nm, shape, dt_):
                return stk.enter_context(nc.sbuf_tensor(self.nm(nm), shape, dt_))

            def PS(nm, shape, dt_=F32):
                return stk.enter_context(nc.psum_tensor(self.nm(nm), shape, dt_))

            kc = [SB(f"g2_k{i}", [P, DT * RQ], BF16) for i in range(2)]
            rc = [SB(f"g2_r{i}", [P, DT * RQ], BF16) for i in range(2)]
            kTb = [SB(f"g2_kT{i}", [P, 2 * D], BF16) for i in range(2)]
            vTb = [SB(f"g2_vT{i}", [P, 2 * D], BF16) for i in range(2)]
            at2 = [SB(f"g2_at{i}", [P, 2 * RQ], BF16) for i in range(2)]
            kw2 = [SB(f"g2_kw{i}", [P, 2 * HD], BF16) for i in range(2)]
            rb2 = [SB(f"g2_rb{i}", [P, RQ], BF16) for i in range(2)]
            ytg = [SB(f"g2_yt{i}", [P, 2 * D], BF16) for i in range(2)]
            S_sb = SB("g2_S", [P, (NH // 2) * HD], F32)
            S_bf = SB("g2_Sb", [P, (NH // 2) * HD], BF16)
            pAT2 = [PS(f"g2_pAT{i}", [P, 2 * RQ]) for i in range(2)]
            py1_ = [PS(f"g2_py1{i}", [P, 2 * HD]) for i in range(2)]
            pS_ = [PS(f"g2_pS{i}", [P, HD]) for i in range(2)]
            m0, e0, a0 = smm.v, sev.v, sac.v
            lp0 = [s.v for s in slp]
            lq0 = [s.v for s in slp2]
            lr0 = [s.v for s in slp3]
            ls0 = [s.v for s in slp4]
            sp0 = [s.v for s in ssp]
            g0 = sgc.v
            at_p, y2_p, sd_p = {}, {}, {}
            mctr = [m0]
            sev_m, sev_r, sev_s = {}, {}, {}
            cv = e0
            sev_m[0] = cv + 1
            sev_r[0] = cv + 2
            cv += 2
            for G_ in range(NHT):
                if G_ + 1 < NHT:
                    sev_m[G_ + 1] = cv + 1
                    sev_r[G_ + 1] = cv + 2
                    cv += 2
                sev_s[G_] = cv + 1
                cv += 1

            def minc(ins, tab, G):
                ins.then_inc(smm.h, 1)
                mctr[0] += 1
                tab[G] = mctr[0]

            with nc.Block(f"g2_{b}") as blk:
                @blk.tensor
                def _(t):
                    t.wait_ge(sgc.h, g0 + 1)

                    def at_op(G2):
                        c2, h2 = divmod(G2, NH)
                        if h2 == 0:
                            t.wait_ge(slp[c2 % 2].h, lp0[c2 % 2] + 16 * (c2 // 2 + 1))
                            t.wait_ge(slp3[c2 % 2].h, lr0[c2 % 2] + 16 * (c2 // 2 + 1))
                            t.wait_ge(slp4[c2 % 2].h, ls0[c2 % 2] + 256 * (c2 // 2 + 1))
                        if G2 >= 2:
                            t.wait_ge(sev.h, sev_m[G2 - 2])
                        hb2 = (h2 % 2) * HD
                        hc2 = (h2 // 2) * RQ
                        ksl2 = kc[c2 % 2][hb2:hb2 + HD, hc2:hc2 + RQ]
                        rsl2 = rc[c2 % 2][hb2:hb2 + HD, hc2:hc2 + RQ]
                        for mt in range(2):
                            ins = t.matmul(pAT2[h2 % 2][:, mt * RQ:(mt + 1) * RQ],
                                           ksl2[:, mt * P:(mt + 1) * P], rsl2,
                                           start=True, stop=True)
                            if mt == 1:
                                minc(ins, at_p, G2)

                    at_op(0)
                    at_op(1)
                    for G in range(NHT):
                        c, h = divmod(G, NH)
                        hb = (h % 2) * HD
                        hcol = (h // 2) * HD
                        t.wait_ge(sev.h, sev_r[G])
                        if G >= 2:
                            t.wait_ge(sac.h, a0 + 2 * (G - 2) + 2)
                        if c >= 1:
                            t.wait_ge(sgc.h, g0 + (c - 1) * NH + h + 2)
                        for qt in range(2):
                            for kt in range(2):
                                t.matmul(py1_[h % 2][:, qt * HD:(qt + 1) * HD],
                                         at2[h % 2][:, kt * RQ + qt * P:
                                                    kt * RQ + (qt + 1) * P],
                                         vTb[c % 2][:, kt * D + h * HD:
                                                    kt * D + (h + 1) * HD],
                                         start=(kt == 0), stop=False)
                            ins = t.matmul(py1_[h % 2][:, qt * HD:(qt + 1) * HD],
                                           rb2[h % 2][hb:hb + HD, qt * P:(qt + 1) * P],
                                           S_bf[hb:hb + HD, hcol:hcol + HD],
                                           start=False, stop=True)
                            if qt == 1:
                                minc(ins, y2_p, G)
                        t.wait_ge(sac.h, a0 + 2 * G + 1)
                        if G >= 2:
                            t.wait_ge(sev.h, sev_s[G - 2])
                        for qt in range(2):
                            ins = t.matmul(pS_[h % 2][hb:hb + HD, :],
                                           kw2[h % 2][:, qt * HD:(qt + 1) * HD],
                                           vTb[c % 2][:, qt * D + h * HD:
                                                      qt * D + (h + 1) * HD],
                                           start=(qt == 0), stop=(qt == 1),
                                           tile_position=(0, hb))
                            if qt == 1:
                                minc(ins, sd_p, G)
                        if G + 2 < NHT:
                            at_op(G + 2)

                @blk.sync
                def _(sy):
                    for c in range(NRC):
                        t0 = b * L + c * RQ
                        if c >= 2:
                            sy.wait_ge(smm.h, sd_p[(c - 1) * NH - 1])
                            sy.wait_ge(sac.h, a0 + 2 * ((c - 1) * NH - 1) + 1)
                            sy.wait_ge(sev.h, sev_r[(c - 1) * NH - 1])
                        for j, buf, sem in [(0, kc, slp), (2, rc, slp3)]:
                            sy.dma_start(
                                buf[c % 2].ap().rearrange("p (dt t) -> p dt t", dt=DT),
                                rkv[j].rearrange("dt p t -> p dt t")[:, :, t0:t0 + RQ]
                            ).then_inc(sem[c % 2].h, 16)
                        for j, dstb, sem in [(0, kTb, slp2), (1, vTb, slp4)]:
                            for qt in range(2):
                                for dt in range(DT):
                                    o0 = (j * DT + dt) * P * NLT + t0 + qt * P
                                    sy.dma_start(
                                        dstb[c % 2][:, qt * D + dt * P:
                                                    qt * D + (dt + 1) * P],
                                        bass.AP(rkv.tensor, o0, [[NLT, P], [1, P]]),
                                        transpose=True).then_inc(sem[c % 2].h, 16)

                @blk.scalar
                def _(s):
                    s.wait_ge(sgc.h, g0 + 1)
                    for G in range(NHT):
                        c, h = divmod(G, NH)
                        if h == 0:
                            s.wait_ge(slp2[c % 2].h, lq0[c % 2] + 256 * (c // 2 + 1))
                        if G >= 2:
                            s.wait_ge(smm.h, sd_p[G - 2])
                        for qt in range(2):
                            ins = s.activation(
                                kw2[h % 2][:, qt * HD:(qt + 1) * HD],
                                kTb[c % 2][:, qt * D + h * HD: qt * D + (h + 1) * HD],
                                AF.Identity,
                                scale=self.wkcol[:, h * 2 + qt: h * 2 + qt + 1])
                            if qt == 1:
                                ins.then_inc(sac.h, 1)
                        s.wait_ge(smm.h, y2_p[G])
                        if h == 0 and c >= 2:
                            s.wait_ge(ssp[c % 2].h,
                                      sp0[c % 2] + 16 * ((c - 2) // 2 + 1))
                        s.activation(
                            ytg[c % 2].ap().rearrange("p (qt d) -> p qt d", qt=2
                                                      )[:, :, h * HD:(h + 1) * HD],
                            py1_[h % 2][:], AF.Copy).then_inc(sac.h, 1)

                @blk.vector
                def _(v):
                    v.wait_ge(sgc.h, g0 + 1)

                    def vm(G2):
                        c2, h2 = divmod(G2, NH)
                        hb3 = (h2 % 2) * HD
                        hcq = (h2 // 2) * RQ
                        if h2 == 0:
                            v.wait_ge(slp3[c2 % 2].h, lr0[c2 % 2] + 16 * (c2 // 2 + 1))
                        v.wait_ge(smm.h, at_p[G2])
                        if G2 >= 2:
                            v.wait_ge(smm.h, y2_p[G2 - 2])
                        v.tensor_tensor(
                            at2[G2 % 2][:, 0:2 * RQ],
                            pAT2[G2 % 2][:, 0:2 * RQ],
                            self.WmT[:, (h2 * 2) * RQ:(h2 * 2 + 2) * RQ], ALU.mult
                            ).then_inc(sev.h, 1)
                        v.tensor_tensor(
                            rb2[G2 % 2][hb3:hb3 + HD, :],
                            rc[c2 % 2][hb3:hb3 + HD, hcq:hcq + RQ],
                            self.wbrow[hb3:hb3 + HD, h2 * RQ:(h2 + 1) * RQ],
                            ALU.mult).then_inc(sev.h, 1)

                    vm(0)
                    for G in range(NHT):
                        c, h = divmod(G, NH)
                        hb2 = (h % 2) * HD
                        hc2 = (h // 2) * HD
                        if G + 1 < NHT:
                            vm(G + 1)
                        v.wait_ge(smm.h, sd_p[G])
                        v.scalar_tensor_tensor(
                            S_sb[hb2:hb2 + HD, hc2:hc2 + HD],
                            S_sb[hb2:hb2 + HD, hc2:hc2 + HD],
                            self.wscol[hb2:hb2 + HD, h:h + 1], pS_[h % 2][hb2:hb2 + HD, :],
                            ALU.mult, ALU.add).then_inc(sev.h, 1)

                @blk.gpsimd
                def _(g):
                    g.memset(S_sb[:], 0.0)
                    g.memset(S_bf[:], 0.0).then_inc(sgc.h, 1)
                    for c in range(NRC):
                        for h in range(NH):
                            G = c * NH + h
                            hb2 = (h % 2) * HD
                            hc2 = (h // 2) * HD
                            g.wait_ge(sev.h, sev_s[G])
                            g.tensor_copy(S_bf[hb2:hb2 + HD, hc2:hc2 + HD],
                                          S_sb[hb2:hb2 + HD, hc2:hc2 + HD]
                                          ).then_inc(sgc.h, 1)
                        g.wait_ge(sac.h, a0 + 2 * (c + 1) * NH)
                        t0 = b * L + c * RQ
                        g.dma_start(
                            yt_dram[t0:t0 + RQ, :].rearrange("(qt p) d -> p qt d", p=P),
                            ytg[c % 2].ap().rearrange("p (qt d) -> p qt d", qt=2)
                        ).then_inc(ssp[c % 2].h, 16)
                    for par in range(2):
                        g.wait_ge(ssp[par].h, sp0[par] + 16 * (NRC // 2))

            for par in range(2):
                slp[par].v = lp0[par] + 16 * ((NRC + 1) // 2 if par == 0 else NRC // 2)
                slp2[par].v = lq0[par] + 256 * ((NRC + 1) // 2 if par == 0 else NRC // 2)
                slp3[par].v = lr0[par] + 16 * ((NRC + 1) // 2 if par == 0 else NRC // 2)
                slp4[par].v = ls0[par] + 256 * ((NRC + 1) // 2 if par == 0 else NRC // 2)
                ssp[par].v = sp0[par] + 16 * (NRC // 2)
            smm.v = mctr[0]
            sev.v = cv
            sac.v = a0 + 2 * NHT
            sgc.v = g0 + 1 + NHT

    def emit_wo(self, b, yt_dram, woT, s1_spill):
        nc = self.nc
        smm, sev, sac = self.s_mm, self.s_ev, self.s_ac
        slp, ssp = self.s_lp, self.s_sp
        TC = 512
        nch = L // TC
        with ExitStack() as stk:
            def SB(nm, shape, dt_):
                return stk.enter_context(nc.sbuf_tensor(self.nm(nm), shape, dt_))

            def PS(nm, shape, dt_=F32):
                return stk.enter_context(nc.psum_tensor(self.nm(nm), shape, dt_))

            ycc = [SB(f"g3_yc{i}", [P, DT * TC], BF16) for i in range(2)]
            stg = [SB(f"g3_s{i}", [P, DT * TC], BF16) for i in range(2)]
            pp = [PS(f"g3_p{i}", [P, TC]) for i in range(2)]
            m0, e0, a0 = smm.v, sev.v, sac.v
            lp0 = [s.v for s in slp]
            sp0 = [s.v for s in ssp]
            MV, EV, AV = DT, 1, DT

            with nc.Block(f"g3_{b}") as blk:
                @blk.sync
                def _(sy):
                    for c in range(nch):
                        t0 = b * L + c * TC
                        if c >= 2:
                            sy.wait_ge(smm.h, m0 + MV * (c - 2) + DT)
                        for dt in range(DT):
                            sy.dma_start(
                                ycc[c % 2][:, dt * TC:(dt + 1) * TC],
                                bass.AP(yt_dram.tensor, t0 * D + dt * P,
                                        [[D, TC], [1, P]]),
                                transpose=True).then_inc(slp[c % 2].h, 16)

                @blk.tensor
                def _(t):
                    for c in range(nch):
                        cm, ca = m0 + MV * c, a0 + AV * c
                        t.wait_ge(slp[c % 2].h, lp0[c % 2] + 16 * DT * (c // 2 + 1))
                        for ot in range(DT):
                            if ot >= 2 or c >= 1:
                                pv = (ca + ot - 2 + 1) if ot >= 2 else \
                                     (a0 + AV * (c - 1) + ot + 6 + 1)
                                t.wait_ge(sac.h, pv)
                            for it in range(DT):
                                ins = t.matmul(pp[ot % 2][:],
                                               woT[:, it * D + ot * P:
                                                   it * D + (ot + 1) * P],
                                               ycc[c % 2][:, it * TC:(it + 1) * TC],
                                               start=(it == 0), stop=(it == DT - 1))
                                if it == DT - 1:
                                    ins.then_inc(smm.h, 1)

                @blk.scalar
                def _(s):
                    for c in range(nch):
                        cm, ca = m0 + MV * c, a0 + AV * c
                        cj = c
                        for ot in range(DT):
                            s.wait_ge(smm.h, cm + ot + 1)
                            if ot == 0 and cj >= 2:
                                s.wait_ge(ssp[cj % 2].h,
                                          sp0[cj % 2] + 16 * ((cj - 2) // 2 + 1))
                            s.activation(stg[cj % 2][:, ot * TC:(ot + 1) * TC],
                                         pp[ot % 2][:], AF.Identity, scale=1.0,
                                         bias=self.vecs["bo"][:, ot:ot + 1]
                                         ).then_inc(sac.h, 1)

                @blk.gpsimd
                def _(g):
                    for c in range(nch):
                        g.wait_ge(sac.h, a0 + AV * c + AV)
                        t0 = b * L + c * TC
                        g.dma_start(
                            s1_spill.rearrange("dt p t -> p dt t")[:, :, t0:t0 + TC],
                            stg[c % 2].ap().rearrange("p (dt t) -> p dt t", dt=DT)
                        ).then_inc(ssp[c % 2].h, 16)
                    for par in range(2):
                        g.wait_ge(ssp[par].h, sp0[par] + 16 * (nch // 2))

            for par in range(2):
                slp[par].v = lp0[par] + 16 * DT * (nch // 2)
                ssp[par].v = sp0[par] + 16 * (nch // 2)
            smm.v = m0 + MV * nch
            sev.v = e0
            sac.v = a0 + AV * nch

    def emit_ffn1_combine(self, b, xn1_ct, cbT, s1_src, x1_src, dst):
        nc = self.nc
        smm, sev, sac = self.s_mm, self.s_ev, self.s_ac
        slp, slp2, ssp = self.s_lp, self.s_lp2, self.s_sp
        TC = 512
        nch = L // TC
        with ExitStack() as stk:
            def SB(nm, shape, dt_):
                return stk.enter_context(nc.sbuf_tensor(self.nm(nm), shape, dt_))

            def PS(nm, shape, dt_=F32):
                return stk.enter_context(nc.psum_tensor(self.nm(nm), shape, dt_))

            g1c = SB("h_g1", [P, DT * TC], BF16)
            s1c = [SB(f"h_s1{i}", [P, DT * TC], BF16) for i in range(2)]
            x1f0 = SB("h_x10", [P, 4 * D], F32)
            x1f = [x1f0, x1f0]
            sgt = SB("h_sgt", [P, 4 * D], BF16)
            scbc = SB("h_scb", [P, DT], F32)
            pp = [PS(f"h_p{i}", [P, TC]) for i in range(2)]
            ptr = [PS(f"h_pt{i}", [P, D], BF16) for i in range(2)]
            m0, e0, a0 = smm.v, sev.v, sac.v
            lp0 = [s.v for s in slp]
            lq0 = [s.v for s in slp2]
            sp0 = [s.v for s in ssp]
            MV, EV, AV = DT + 4, DT + 1, DT + 4

            with nc.Block(f"h_{b}") as blk:
                @blk.vector
                def _(v):
                    v.tensor_scalar(scbc[:], self.vecs["SCB"][:], 1.0 / 127.0,
                                    None, ALU.mult).then_inc(sev.h, 1)
                    for c in range(nch):
                        ce = e0 + 1 + EV * c
                        v.wait_ge(slp[c % 2].h, lp0[c % 2] + 16 * (c // 2 + 1))
                        if c >= 1:
                            v.wait_ge(smm.h, m0 + MV * (c - 1) + MV)
                        for dt in range(DT):
                            v.wait_ge(sac.h, a0 + AV * c + dt + 1)
                            ins = v.tensor_tensor(s1c[c % 2][:, dt * TC:(dt + 1) * TC],
                                                  s1c[c % 2][:, dt * TC:(dt + 1) * TC],
                                                  g1c[:, dt * TC:(dt + 1) * TC],
                                                  ALU.mult)
                            ins.then_inc(sev.h, 1)
                        v.wait_ge(slp2[c % 2].h, lq0[c % 2] + 16 * (c // 2 + 1))
                        v.wait_ge(sac.h, a0 + AV * c + DT + 4)
                        v.tensor_tensor(x1f[c % 2][:], x1f[c % 2][:], sgt[:],
                                        ALU.add).then_inc(sev.h, 1)

                @blk.tensor
                def _(t):
                    for c in range(nch):
                        cm, ce, ca = m0 + MV * c, e0 + 1 + EV * c, a0 + AV * c
                        for ot in range(DT):
                            if ot >= 2 or c >= 1:
                                pv = (ca + ot - 2 + 1) if ot >= 2 else \
                                     (a0 + AV * (c - 1) + ot + 6 + 1)
                                t.wait_ge(sac.h, pv)
                            for it in range(DT):
                                ins = t.matmul(pp[ot % 2][:],
                                               cbT[:, it * D + ot * P:
                                                   it * D + (ot + 1) * P],
                                               xn1_ct.ap().rearrange(
                                                   "p (dt t) -> p dt t", dt=DT)
                                                   [:, it, c * TC:(c + 1) * TC]
                                               if False else
                                               xn1_ct[:, it * L + c * TC:
                                                      it * L + c * TC + TC],
                                               start=(it == 0), stop=(it == DT - 1))
                                if it == DT - 1:
                                    ins.then_inc(smm.h, 1)
                        for tt in range(4):
                            t.wait_ge(sev.h, ce + DT)
                            if tt >= 2 or c >= 1:
                                pv = (ca + DT + tt - 2 + 1) if tt >= 2 else \
                                     (a0 + AV * (c - 1) + DT + tt + 2 + 1)
                                t.wait_ge(sac.h, pv)
                            for dt in range(DT):
                                ins = t.transpose(ptr[tt % 2][:, dt * P:(dt + 1) * P],
                                                  s1c[c % 2][:, dt * TC + tt * P:
                                                              dt * TC + (tt + 1) * P],
                                                  self.ident[:])
                                if dt == DT - 1:
                                    ins.then_inc(smm.h, 1)

                @blk.scalar
                def _(s):
                    for c in range(nch):
                        cm, ca = m0 + MV * c, a0 + AV * c
                        for ot in range(DT):
                            s.wait_ge(smm.h, cm + ot + 1)
                            if ot == 0 and c >= 1:
                                s.wait_ge(sev.h, e0 + 1 + EV * (c - 1) + DT)
                            s.wait_ge(sev.h, e0 + 1)
                            s.activation(g1c[:, ot * TC:(ot + 1) * TC],
                                         pp[ot % 2][:], AF.Silu,
                                         scale=scbc[:, ot:ot + 1],
                                         bias=self.vecs["ffn_bias"][:, ot:ot + 1]
                                         ).then_inc(sac.h, 1)
                        for tt in range(4):
                            s.wait_ge(smm.h, cm + DT + tt + 1)
                            if tt == 0 and c >= 1:
                                s.wait_ge(sev.h, e0 + 1 + EV * (c - 1) + DT + 1)
                            s.activation(sgt[:, tt * D:(tt + 1) * D],
                                         ptr[tt % 2][:], AF.Copy).then_inc(sac.h, 1)

                @blk.sync
                def _(sy):
                    for c in range(nch):
                        t0 = b * L + c * TC
                        if c >= 2:
                            sy.wait_ge(smm.h, m0 + MV * (c - 2) + MV)
                        sy.dma_start(
                            s1c[c % 2].ap().rearrange("p (dt t) -> p dt t", dt=DT),
                            s1_src.rearrange("dt p t -> p dt t")[:, :, t0:t0 + TC]
                        ).then_inc(slp[c % 2].h, 16)
                        if c >= 1:
                            sy.wait_ge(ssp[(c - 1) % 2].h,
                                       sp0[(c - 1) % 2] + 16 * ((c - 1) // 2 + 1))
                        sy.dma_start(
                            x1f[c % 2].ap().rearrange("p (tt d) -> p tt d", tt=4),
                            x1_src[b, c * TC:(c + 1) * TC, :]
                                .rearrange("(tt p) d -> p tt d", p=P)
                        ).then_inc(slp2[c % 2].h, 16)

                @blk.gpsimd
                def _(g):
                    for c in range(nch):
                        g.wait_ge(sev.h, e0 + 1 + EV * (c + 1))
                        g.dma_start(
                            dst[b, c * TC:(c + 1) * TC, :]
                                .rearrange("(tt p) d -> p tt d", p=P),
                            x1f[c % 2].ap().rearrange("p (tt d) -> p tt d", tt=4)
                        ).then_inc(ssp[c % 2].h, 16)
                    for par in range(2):
                        g.wait_ge(ssp[par].h, sp0[par] + 16 * (nch // 2))

            for par in range(2):
                slp[par].v = lp0[par] + 16 * (nch // 2)
                slp2[par].v = lq0[par] + 16 * (nch // 2)
                ssp[par].v = sp0[par] + 16 * (nch // 2)
            smm.v = m0 + MV * nch
            sev.v = e0 + 1 + EV * nch
            sac.v = a0 + AV * nch

    def emit_fence(self):
        nc = self.nc
        with nc.Block(self.nm("fence")) as blk:
            @blk.tensor
            def _(t):
                t.wait_ge(self.s_mm.h, self.s_mm.v)

            @blk.vector
            def _(v):
                v.wait_ge(self.s_ev.h, self.s_ev.v)
                v.wait_ge(self.s_vch.h, self.s_vch.v)

            @blk.scalar
            def _(s):
                s.wait_ge(self.s_ac.h, self.s_ac.v)
                s.wait_ge(self.s_sch.h, self.s_sch.v)

            @blk.gpsimd
            def _(g):
                g.wait_ge(self.s_g.h, self.s_g.v)
                g.wait_ge(self.s_gch.h, self.s_gch.v)

    def emit_spill_rows(self, sb, dram, b):
        nc = self.nc
        sp = self.s_sp[b % 2]
        v0 = sp.v
        with nc.Block(self.nm("sprow")) as blk:
            @blk.gpsimd
            def _(g):
                g.dma_start(
                    bass.AP(dram if not hasattr(dram, 'tensor') else dram.tensor,
                            b * L * D, [[D, P], [P * D, NCH], [1, D]]),
                    bass.AP(sb, 0, [[NCH * D, P], [D, NCH], [1, D]])
                ).then_inc(sp.h, 16)
                g.wait_ge(sp.h, v0 + 16)
        sp.v = v0 + 16

    def emit_copy_sb2dram(self, sb, dram, b):
        nc = self.nc
        sp = self.s_sp[b % 2]
        v0 = sp.v
        with nc.Block(self.nm("sb2d")) as blk:
            @blk.gpsimd
            def _(g):
                g.dma_start(
                    dram.rearrange("dt p t -> p dt t")[:, :, b * L:(b + 1) * L],
                    sb.ap().rearrange("p (dt t) -> p dt t", dt=DT)
                ).then_inc(sp.h, 16)
                g.wait_ge(sp.h, v0 + 16)
        sp.v = v0 + 16

    def emit_copy_dram2sb(self, dram, sb, b):
        nc = self.nc
        lp = self.s_lp[b % 2]
        v0 = lp.v
        with nc.Block(self.nm("d2sb")) as blk:
            @blk.sync
            def _(sy):
                sy.dma_start(
                    sb.ap().rearrange("p (dt t) -> p dt t", dt=DT),
                    dram.rearrange("dt p t -> p dt t")[:, :, b * L:(b + 1) * L]
                ).then_inc(lp.h, 16)
                sy.wait_ge(lp.h, v0 + 16)
        lp.v = v0 + 16


    def emit_moe_routed(self, b, xn_rows, g_ct, wga):
        nc = self.nc
        smm, sev, sac, sgc = self.s_mm, self.s_ev, self.s_ac, self.s_gch
        slp, slp2, sgp, sdn, ssp = self.s_lp, self.s_lp2, self.s_gp, self.s_dn, self.s_sp
        NCHK = ECAP // GCH
        with ExitStack() as stk:
            def SB(nm, shape, dt_):
                return stk.enter_context(nc.sbuf_tensor(self.nm(nm), shape, dt_))

            def PS(nm, shape, dt_=F32):
                return stk.enter_context(nc.psum_tensor(self.nm(nm), shape, dt_))

            mxs = self.r_mxs
            score_tc = self.r_sc
            m01 = self.r_m01
            didx_f = self.r_didxf
            didx_u = self.r_didxu
            jnk8 = self.r_jnk
            offrow = self.r_off
            offrow_f = self.r_offf
            rstk = ExitStack()
            pcomb = rstk.enter_context(nc.psum_tensor(self.nm("r_comb"), [P, NE], F32))
            pcnt = rstk.enter_context(nc.psum_tensor(self.nm("r_cnt"), [1, NE], F32))
            m0, e0, a0 = smm.v, sev.v, sac.v
            g0 = self.s_g.v
            with nc.Block(f"rroute_{b}") as blk:
                @blk.gpsimd
                def _(g):
                    g.iota(offrow_f[:], [[ECAP, NE]], channel_multiplier=0,
                           allow_small_or_imprecise_dtypes=True).then_inc(self.s_g.h, 1)
                self.s_g.v = g0 + 1

                @blk.vector
                def _(v):
                    v.wait_ge(self.s_g.h, g0 + 1)
                    v.tensor_copy(offrow[:], offrow_f[:]).then_inc(self.s_vch.h, 1)
                    self.s_vch.v += 1
                    v.wait_ge(self.s_vch.h, self.s_vch.v)
                    for i in range(NCH):
                        lg = self.lg_tc[:, (b * NCH + i) * NE:(b * NCH + i + 1) * NE]
                        v.reduce_max(mxs[:, i % 2:i % 2 + 1], lg, axis=AX.X
                                     ).then_inc(sev.h, 1)
                        v.wait_ge(sac.h, a0 + i + 1)
                        v.tensor_copy(score_tc[:, i:i + 1],
                                      mxs[:, 2 + i % 2:3 + i % 2])
                        if i >= 2:
                            v.wait_ge(smm.h, m0 + 3 * (i - 1))
                        for e in range(NE):
                            v.tensor_scalar(m01[i % 2][:, e:e + 1], lg[:, e:e + 1],
                                            mxs[:, i % 2:i % 2 + 1], None,
                                            ALU.is_equal)
                        v.tensor_copy(mxs[:, 4:5], self.cst[:, 0:1]
                                      ).then_inc(sev.h, 1)
                        v.wait_ge(smm.h, m0 + 3 * i + 3)
                        v.scalar_tensor_tensor(jnk8[:], m01[i % 2][:], 1.0,
                                               pcomb[:], ALU.mult, ALU.mult,
                                               accum_out=didx_f[:, i:i + 1]
                                               ).then_inc(sev.h, 1)
                        v.wait_ge(sev.h, e0 + 4 * i + 3)
                        v.tensor_scalar(didx_f[:, i:i + 1], didx_f[:, i:i + 1],
                                        float(NE * ECAP - 1), None, ALU.min)
                        v.tensor_tensor(offrow[:], offrow[:], pcnt[:], ALU.add
                                        ).then_inc(sev.h, 1)
                    v.wait_ge(sev.h, e0 + 4 * NCH)
                    v.tensor_copy(didx_u[:], didx_f[:]).then_inc(sev.h, 1)

                @blk.scalar
                def _(s):
                    for i in range(NCH):
                        s.wait_ge(sev.h, e0 + 4 * i + 1)
                        s.activation(mxs[:, 2 + i % 2:3 + i % 2],
                                     mxs[:, i % 2:i % 2 + 1], AF.Sigmoid
                                     ).then_inc(sac.h, 1)

                @blk.tensor
                def _(t):
                    for i in range(NCH):
                        t.wait_ge(sev.h, e0 + 4 * i + 2)
                        if i >= 1:
                            t.wait_ge(sev.h, e0 + 4 * i)
                        t.matmul(pcomb[:], self.LTb[:], m01[i % 2][:],
                                 start=True, stop=False).then_inc(smm.h, 1)
                        t.matmul(pcomb[:], self.ones_colf[:],
                                 offrow[:], start=False, stop=True
                                 ).then_inc(smm.h, 1)
                        t.matmul(pcnt[:], self.onesc_b[:], m01[i % 2][:],
                                 start=True, stop=True).then_inc(smm.h, 1)
            smm.v = m0 + 3 * NCH
            sev.v = e0 + 4 * NCH + 1
            sac.v = a0 + NCH

            xst = [SB(f"s_x{i}", [P, D], BF16) for i in range(2)]
            xss = [SB(f"s_xs{i}", [P, D], BF16) for i in range(2)]
            zrow = SB("s_z", [P, D], BF16)
            lp0 = [s.v for s in slp]
            gp0 = [s.v for s in sgp]
            e0, g0 = sev.v, self.s_gch.v
            with nc.Block(f"rscat_{b}") as blk:
                @blk.sync
                def _(sy):
                    for i in range(NCH):
                        if i >= 2:
                            sy.wait_ge(sev.h, e0 + (i - 2) + 1)
                        sy.dma_start(
                            xst[i % 2][:],
                            bass.AP(xn_rows.tensor, (b * L + i * P) * D,
                                    [[D, P], [1, D]])).then_inc(slp[i % 2].h, 16)

                @blk.vector
                def _(v):
                    v.memset(zrow[:], 0.0).then_inc(self.s_vch.h, 1)
                    self.s_vch.v += 1
                    v.wait_ge(self.s_vch.h, self.s_vch.v)
                    for i in range(NCH):
                        v.wait_ge(slp[i % 2].h, lp0[i % 2] + 16 * (i // 2 + 1))
                        if i >= 2:
                            v.wait_ge(sgp[0].h, gp0[0] + 16 * (i - 1))
                        v.tensor_scalar(xss[i % 2][:], xst[i % 2][:],
                                        score_tc[:, i:i + 1], None, ALU.mult
                                        ).then_inc(sev.h, 1)

                @blk.gpsimd
                def _(g):
                    if b == 0:
                        g.wait_ge(sev.h, e0)
                        for z in range(NE * ECAP // P):
                            g.wait_ge(self.s_vch.h, self.s_vch.v)
                            g.dma_start(
                                bass.AP(self.xg_dram.tensor, z * P * D,
                                        [[D, P], [1, D]]),
                                zrow[:]).then_inc(self.s_gst.h, 16)
                            self.s_gst.v += 16
                        g.wait_ge(self.s_gst.h, self.s_gst.v)
                    for i in range(NCH):
                        g.wait_ge(sev.h, e0 + i + 1)
                        if i >= 1:
                            g.wait_ge(sgp[0].h, gp0[0] + 16 * i)
                        g.indirect_dma_start(
                            bass.AP(self.xg_dram.tensor, 0, [[D, NE * ECAP], [1, D]]),
                            bass.IndirectOffsetOnAxis(didx_u[:, i:i + 1], 0),
                            xss[i % 2][:], None).then_inc(sgp[0].h, 16)
                    g.wait_ge(sgp[0].h, gp0[0] + 16 * NCH)
            for par in range(2):
                slp[par].v = lp0[par] + 16 * (NCH // 2)
            sgp[0].v = gp0[0] + 16 * NCH
            sev.v = e0 + NCH
            rstk.close()

            shg = self.shg_dram
            estk = ExitStack()

            def ESB(nm, shape, dt_):
                return estk.enter_context(nc.sbuf_tensor(self.nm(nm), shape, dt_))

            wdn = ESB("rm_wdn", [P, FT * D], BF16)
            xs_ct = ESB("rm_xs", [P, DT * GCH], BF16)
            h_all = ESB("rm_h", [P, FT * GCH], BF16)
            hs0 = ESB("rm_hs", [P, GCH], BF16)
            eo_ct = ESB("rm_eo", [P, DT * GCH], BF16)
            eoT = [ESB(f"rm_eoT{i}", [P, D], BF16) for i in range(2)]
            ptbA = PS("rm_ptA", [P, P], BF16)
            ptbB = PS("rm_ptB", [P, P], BF16)
            ptbp = [ptbA, ptbB]
            pg = estk.enter_context(nc.psum_tensor(self.nm("rm_pg"), [P, GCH], F32))
            pu = estk.enter_context(nc.psum_tensor(self.nm("rm_pu"), [P, GCH], F32))
            pd = estk.enter_context(nc.psum_tensor(self.nm("rm_pd"), [P, DT * GCH], F32))
            m0, e0, a0 = smm.v, sev.v, sac.v
            lq0 = [s.v for s in slp2]
            gp0 = [s.v for s in sgp]
            dn0 = [s.v for s in sdn]
            sp0 = [s.v for s in ssp]
            NCK = [3] * NE + [16]
            QTOT = sum(NCK)
            EV = 16
            qof = [sum(NCK[:e]) for e in range(NE + 1)]
            GW = [(GCH if (e == NE or ch < 2) else ECAP - 2 * GCH)
                  for e in range(NE + 1) for ch in range(NCK[e])]
            QT2 = [g_ // P for g_ in GW]
            MVs = [40 + 8 * s_ for s_ in QT2]
            AVs = [24 + 8 * s_ for s_ in QT2]
            PM = [0]
            PA = [0]
            for q_ in range(QTOT):
                PM.append(PM[-1] + MVs[q_])
                PA.append(PA[-1] + AVs[q_])
            SPaf = []
            _spc = [0, 0]
            for q_ in range(QTOT):
                _spc[q_ % 2] += 16 * QT2[q_]
                SPaf.append(_spc[q_ % 2]) if False else SPaf.append(_spc[q_ % 2])

            def srcrow(e, ch):
                if e < NE:
                    return (e * ECAP + ch * GCH) * D
                return (b * L + ch * GCH) * D

            with nc.Block(f"rexp_{b}") as blk:
                @blk.gpsimd
                def _(g):
                    for e in range(NE + 1):
                        q0 = qof[e]
                        eg = self.w["expert_gate"][e] if e < NE else self.w["shared_gate"]
                        eu = self.w["expert_up"][e] if e < NE else self.w["shared_up"]
                        ed = self.w["expert_down"][e] if e < NE else self.w["shared_down"]
                        if e >= 1:
                            g.wait_ge(smm.h, m0 + PM[q0 - 1] + 16)
                        for mat, co in ((eg, 0), (eu, DT * FF)):
                            g.dma_start(
                                wga[:, co:co + DT * FF].rearrange(
                                    "p (it f) -> p it f", it=DT)[:, :, 0:FF // 2],
                                mat.rearrange("(it p) f -> p it f", p=P
                                              )[:, :, 0:FF // 2]
                            ).then_inc(sgp[e % 2].h, 16)
                        g.wait_ge(sgp[e % 2].h, gp0[e % 2] + 64 * (e // 2) + 32)
                        if e >= 1:
                            g.wait_ge(smm.h, m0 + PM[q0 - 1] + 32)
                        for mat, co in ((eg, 0), (eu, DT * FF)):
                            g.dma_start(
                                wga[:, co:co + DT * FF].rearrange(
                                    "p (it f) -> p it f", it=DT)[:, :, FF // 2:FF],
                                mat.rearrange("(it p) f -> p it f", p=P
                                              )[:, :, FF // 2:FF]
                            ).then_inc(sgp[e % 2].h, 16)
                        if e >= 1:
                            g.wait_ge(smm.h, m0 + PM[q0])
                        g.dma_start(wdn.ap().rearrange("p (ft o) -> p ft o", ft=FT),
                                    ed.rearrange("(ft p) o -> p ft o", p=P)
                                    ).then_inc(sdn[e % 2].h, 16)
                        def st(e, ch):
                            q = qof[e] + ch
                            g.wait_ge(sac.h, a0 + PA[q] + AVs[q])
                            dstb = self.eog_dram.tensor if e < NE else shg.tensor
                            for qt in range(QT2[q]):
                                g.dma_start(
                                    bass.AP(dstb, srcrow(e, ch) - (b * L * D if e == NE else 0)
                                            + qt * P * D, [[D, P], [1, D]]),
                                    eoT[qt][:]).then_inc(ssp[q % 2].h, 16)
                        if e >= 1:
                            st(e - 1, NCK[e - 1] - 1)
                        for ch in range(NCK[e] - 1):
                            st(e, ch)
                    st(NE, NCK[NE] - 1)
                    if b == 0:
                        for nm in ["Wr", "Wk", "Wv", "Wo", "CB"]:
                            g.dma_start(self.wbf[nm], self.w[nm]
                                        ).then_inc(self.s_x.h, 16)
                            self.s_x.v += 16
                    for par in range(2):
                        g.wait_ge(ssp[par].h, sp0[par] + _spc[par])

                @blk.sync
                def _(sy):
                    for e in range(NE + 1):
                        for ch in range(NCK[e]):
                            q = qof[e] + ch
                            if q >= 1:
                                sy.wait_ge(smm.h, m0 + PM[q - 1] + 32)
                            src = (self.xg_dram.tensor if e < NE
                                   else xn_rows.tensor)
                            for dt in range(DT):
                                sy.dma_start(
                                    xs_ct[:, dt * GCH:dt * GCH + GW[q]],
                                    bass.AP(src, srcrow(e, ch) + dt * P,
                                            [[D, GW[q]], [1, P]]),
                                    transpose=True).then_inc(slp2[q % 2].h, 16)

                @blk.tensor
                def _(t):
                    for e in range(NE + 1):
                        t.wait_ge(sgp[e % 2].h, gp0[e % 2] + 64 * (e // 2) + 32)
                        for ch in range(NCK[e]):
                            q = qof[e] + ch
                            gw = GW[q]
                            cb_m, cb_e, cb_a = m0 + PM[q], e0 + EV * q, a0 + PA[q]
                            pb_a = a0 + PA[q - 1]
                            pb_e = e0 + EV * (q - 1)
                            t.wait_ge(slp2[q % 2].h,
                                      lq0[q % 2] + 16 * DT * (q // 2 + 1))
                            for ft in range(FT):
                                if ch == 0 and ft == FT // 2:
                                    t.wait_ge(sgp[e % 2].h,
                                              gp0[e % 2] + 64 * (e // 2) + 64)
                                if ft >= 1 or q >= 1:
                                    pv = (cb_a + ft) if ft >= 1 else \
                                         (pb_a + 16)
                                    t.wait_ge(sac.h, pv)
                                for it in range(DT):
                                    ins = t.matmul(
                                        pg[:, 0:gw],
                                        wga[:, it * FF + ft * P: it * FF + (ft + 1) * P],
                                        xs_ct[:, it * GCH:it * GCH + gw],
                                        start=(it == 0), stop=(it == DT - 1))
                                    if it == DT - 1:
                                        ins.then_inc(smm.h, 1)
                                if ft >= 1 or q >= 1:
                                    pv = (cb_e + ft) if ft >= 1 else \
                                         (pb_e + 16)
                                    t.wait_ge(sev.h, pv)
                                for it in range(DT):
                                    ins = t.matmul(
                                        pu[:, 0:gw],
                                        wga[:, DT * FF + it * FF + ft * P:
                                            DT * FF + it * FF + (ft + 1) * P],
                                        xs_ct[:, it * GCH:it * GCH + gw],
                                        start=(it == 0), stop=(it == DT - 1))
                                    if it == DT - 1:
                                        ins.then_inc(smm.h, 1)
                            if ch == 0:
                                t.wait_ge(sdn[e % 2].h, dn0[e % 2] + 16 * (e // 2 + 1))
                            t.wait_ge(sev.h, cb_e + 16)
                            if q >= 1:
                                t.wait_ge(sac.h, pb_a + 24)
                            for ot in range(DT):
                                for ft in range(FT):
                                    ins = t.matmul(
                                        pd[:, ot * GCH:ot * GCH + gw],
                                        wdn[:, ft * D + ot * P: ft * D + (ot + 1) * P],
                                        h_all[:, ft * GCH:ft * GCH + gw],
                                        start=(ft == 0), stop=(ft == FT - 1))
                                    if ft == FT - 1:
                                        ins.then_inc(smm.h, 1)
                            t.wait_ge(sac.h, cb_a + 24)
                            NJ = 8 * QT2[q]
                            NJp = 8 * QT2[q - 1]
                            for j in range(NJ):
                                qt, dt = divmod(j, DT)
                                if j >= 2 or q >= 1:
                                    pv = (cb_a + 24 + j - 2 + 1) if j >= 2 else \
                                         (pb_a + 24 + NJp - 2 + j + 1)
                                    t.wait_ge(sac.h, pv)
                                t.transpose(ptbp[j % 2][:, 0:P],
                                            eo_ct[:, dt * GCH + qt * P:
                                                  dt * GCH + (qt + 1) * P],
                                            self.ident[:]).then_inc(smm.h, 1)

                @blk.scalar
                def _(s):
                    for e in range(NE + 1):
                        for ch in range(NCK[e]):
                            q = qof[e] + ch
                            gw = GW[q]
                            cb_m, cb_e, cb_a = m0 + PM[q], e0 + EV * q, a0 + PA[q]
                            for ft in range(FT):
                                s.wait_ge(smm.h, cb_m + 2 * ft + 1)
                                if ft >= 1:
                                    s.wait_ge(sev.h, cb_e + ft)
                                elif q >= 1:
                                    s.wait_ge(sev.h, e0 + EV * (q - 1) + 16)
                                s.activation(hs0[:, 0:gw], pg[:, 0:gw],
                                             AF.Silu).then_inc(sac.h, 1)
                            for ot in range(DT):
                                s.wait_ge(smm.h, cb_m + 40)
                                if ot == 0 and q >= 1:
                                    s.wait_ge(smm.h, m0 + PM[q])
                                s.activation(eo_ct[:, ot * GCH:ot * GCH + gw],
                                             pd[:, ot * GCH:ot * GCH + gw],
                                             AF.Copy).then_inc(sac.h, 1)
                            for j in range(8 * QT2[q]):
                                qt, dt = divmod(j, DT)
                                s.wait_ge(smm.h, cb_m + 40 + j + 1)
                                if j == 0 and q >= 1:
                                    s.wait_ge(ssp[(q - 1) % 2].h,
                                              sp0[(q - 1) % 2] + SPaf[q - 1])
                                s.activation(eoT[qt][:, dt * P:(dt + 1) * P],
                                             ptbp[j % 2][:, 0:P], AF.Copy
                                             ).then_inc(sac.h, 1)

                @blk.vector
                def _(v):
                    for e in range(NE + 1):
                        for ch in range(NCK[e]):
                            q = qof[e] + ch
                            gw = GW[q]
                            cb_m, cb_e, cb_a = m0 + PM[q], e0 + EV * q, a0 + PA[q]
                            for ft in range(FT):
                                v.wait_ge(sac.h, cb_a + ft + 1)
                                v.wait_ge(smm.h, cb_m + 2 * ft + 2)
                                if ft == 0 and q >= 1:
                                    v.wait_ge(smm.h, m0 + PM[q - 1] + 40)
                                v.tensor_tensor(h_all[:, ft * GCH:ft * GCH + gw],
                                                hs0[:, 0:gw], pu[:, 0:gw],
                                                ALU.mult).then_inc(sev.h, 1)

            smm.v = m0 + PM[QTOT]
            sev.v = e0 + EV * QTOT
            sac.v = a0 + PA[QTOT]
            for par in range(2):
                npar = (QTOT + 1) // 2 if par == 0 else QTOT // 2
                slp2[par].v = lq0[par] + 16 * DT * npar
                ssp[par].v = sp0[par] + _spc[par]
            ne_par = [(NE + 1 + 1) // 2, (NE + 1) // 2]
            for par in range(2):
                sgp[par].v = gp0[par] + 64 * ne_par[par]
                sdn[par].v = dn0[par] + 16 * ne_par[par]

            estk.close()
            ptg = [PS(f"rm_ptg{i}", [P, P], BF16) for i in range(2)]
            mgt = [SB(f"rm_mg{i}", [P, D], BF16) for i in range(2)]
            sht = [SB(f"rm_sh{i}", [P, D], BF16) for i in range(2)]
            srow = SB("rm_srow", [P, D], BF16)
            m0, e0, a0 = smm.v, sev.v, sac.v
            gp0 = [s.v for s in sgp]
            lp0 = [s.v for s in slp]
            with nc.Block(f"rgath_{b}") as blk:
                @blk.gpsimd
                def _(g):
                    for i in range(NCH):
                        if i >= 2:
                            g.wait_ge(sev.h, e0 + (i - 2) + 1)
                        g.indirect_dma_start(
                            mgt[i % 2][:],
                            None,
                            bass.AP(self.eog_dram.tensor, 0,
                                    [[D, NE * ECAP], [1, D]]),
                            bass.IndirectOffsetOnAxis(didx_u[:, i:i + 1], 0)
                        ).then_inc(sgp[i % 2].h, 16)

                @blk.sync
                def _(sy):
                    for i in range(NCH):
                        if i >= 2:
                            sy.wait_ge(sev.h, e0 + (i - 2) + 1)
                        sy.dma_start(
                            sht[i % 2][:],
                            bass.AP(shg.tensor, i * P * D, [[D, P], [1, D]])
                        ).then_inc(slp[i % 2].h, 16)
                    sy.wait_ge(sac.h, a0 + DT * NCH + DT)

                @blk.vector
                def _(v):
                    for i in range(NCH):
                        v.wait_ge(sgp[i % 2].h, gp0[i % 2] + 16 * (i // 2 + 1))
                        v.wait_ge(slp[i % 2].h, lp0[i % 2] + 16 * (i // 2 + 1))
                        if i >= 1:
                            v.wait_ge(smm.h, m0 + DT * i)
                        v.tensor_tensor(srow[:], mgt[i % 2][:], sht[i % 2][:],
                                        ALU.add).then_inc(sev.h, 1)

                @blk.tensor
                def _(t):
                    for i in range(NCH):
                        t.wait_ge(sev.h, e0 + i + 1)
                        for dt in range(DT):
                            j = i * DT + dt
                            if j >= 2:
                                t.wait_ge(sac.h, a0 + j - 2 + 1)
                            t.transpose(ptbp[j % 2][:, 0:P],
                                        srow[:, dt * P:(dt + 1) * P],
                                        self.ident[:]).then_inc(smm.h, 1)

                @blk.scalar
                def _(s):
                    for i in range(NCH):
                        for dt in range(DT):
                            j = i * DT + dt
                            s.wait_ge(smm.h, m0 + j + 1)
                            s.activation(g_ct[:, dt * L + i * P:
                                              dt * L + (i + 1) * P],
                                         ptbp[j % 2][:, 0:P], AF.Copy
                                         ).then_inc(sac.h, 1)
                    s.wait_ge(sac.h, a0 + DT * NCH)
                    for dt in range(DT):
                        s.activation(g_ct[:, dt * L:(dt + 1) * L],
                                     g_ct[:, dt * L:(dt + 1) * L], AF.Silu
                                     ).then_inc(sac.h, 1)
            smm.v = m0 + NCH * DT
            sev.v = e0 + NCH
            sac.v = a0 + DT * NCH + DT
            for par in range(2):
                sgp[par].v = gp0[par] + 16 * (NCH // 2)
                slp[par].v = lp0[par] + 16 * (NCH // 2)


def build_program():
    nc = bass.Bass("TRN2", target_bir_lowering=False)
    mk = MK(nc)
    mk.emit_setup()
    s0_ct = nc.dram_tensor("s0_ct", [DT, P, NB * L], BF16, kind="Internal").ap()
    mk.emit_common()
    mk.emit_fence()
    with ExitStack() as l0stk:
        mk.emit_conv_tables(l0stk)
        mk.emit_fence()
        for b in range(NB):
            mk.emit_ln(b, mk.x_in, mk.arena1, mk.xn_ct_dram, layer=0)
            mk.emit_fence()
            mk.emit_spill_rows(mk.arena1, mk.xn_tc2_dram, b)
            mk.emit_conv(b, mk.arena1, mk.arena2, mk.y_ct_dram)
            mk.emit_fence()
            if b == 0:
                mk.emit_wcast2(0)
    for b in range(NB):
        for nmw, off in [("gate_W", 0), ("ss_W", 1), ("fin_W", 2)]:
            mk.emit_load_wT(mk.nm(nmw), mk.wbf[nmw], mk.arena2[:],
                            dst_off=off * DT * D, need=(mk.s_xw, 48))
        mk.emit_gate_ss_fin(b, mk.xn_ct_dram, mk.y_ct_dram, mk.arena2[:],
                            mk.arena1[:])
        mk.emit_fence()
        mk.emit_copy_sb2dram(mk.arena1, s0_ct, b)
        mk.emit_moe_routed(b, mk.xn_tc2_dram, mk.arena2[:], mk.arena1[:])
        mk.emit_fence()
        mk.emit_combine(b, s0_ct, mk.arena2[:], mk.x_in, mk.x1_dram)
        mk.emit_fence()
    with ExitStack() as l1stk:
        mk.emit_wkv_tables()
        mk.emit_fence()
        for b in range(NB):
            for nmw, off in [("Wk", 0), ("Wv", 1), ("Wr", 2), ("Wo", 3)]:
                mk.emit_load_wT(mk.nm(nmw), mk.wbf[nmw], mk.arena2[:],
                                dst_off=off * DT * D)
            mk.emit_ln(b, mk.x1_dram, None, None, layer=1, ct_sbuf=mk.arena1)
            mk.emit_fence()
            mk.emit_lerp_proj(b, mk.arena1, mk.arena2[:], mk.rkv_dram)
            mk.emit_fence()
            mk.emit_wkv(b, mk.rkv_dram, mk.yt_dram)
            mk.emit_fence()
            mk.emit_wo(b, mk.yt_dram, mk.arena2[:, 3 * DT * D:], mk.s1_ct_dram)
            mk.emit_fence()
            mk.emit_load_wT(mk.nm("CB"), mk.wbf["CB"], mk.arena2[:], dst_off=0)
            mk.emit_ffn1_combine(b, mk.arena1, mk.arena2[:, 0:DT * D],
                                 mk.s1_ct_dram, mk.x1_dram, mk.out)
            mk.emit_fence()
    return nc


_REN = {
    "ss_W": "L0_attn_ss_W", "ss_b": "L0_attn_ss_b", "gate_W": "L0_attn_gate_W",
    "gate_b": "L0_attn_gate_b", "fin_W": "L0_attn_fin_W", "fin_b": "L0_attn_fin_b",
    "in_proj": "L0_attn_in_proj", "out_proj": "L0_attn_out_proj",
    "frequencies": "L0_attn_frequencies", "decays": "L0_attn_decays",
    "router_w": "L0_ffn_router_w", "expert_gate": "L0_ffn_expert_gate",
    "expert_up": "L0_ffn_expert_up", "expert_down": "L0_ffn_expert_down",
    "shared_gate": "L0_ffn_shared_gate", "shared_up": "L0_ffn_shared_up",
    "shared_down": "L0_ffn_shared_down", "maa_k": "L1_attn_time_maa_k",
    "maa_v": "L1_attn_time_maa_v", "maa_r": "L1_attn_time_maa_r",
    "t_decay": "L1_attn_time_decay", "t_faaaa": "L1_attn_time_faaaa",
    "Wr": "L1_attn_Wr", "br": "L1_attn_br", "Wk": "L1_attn_Wk", "bk": "L1_attn_bk",
    "Wv": "L1_attn_Wv", "bv": "L1_attn_bv", "Wo": "L1_attn_Wo", "bo": "L1_attn_bo",
    "CB": "L1_ffn_CB", "SCB": "L1_ffn_SCB", "ffn_bias": "L1_ffn_bias",
}

_NCORES = 8


def kernel(**inputs):
    nc = build_program()
    x = np.ascontiguousarray(inputs["L0_attn_x"])
    in_maps = []
    for core in range(_NCORES):
        im = {"x": x[core * NB:(core + 1) * NB]}
        for knm, src in _REN.items():
            im[knm] = inputs[src]
        in_maps.append(im)
    res = run_bass_kernel_spmd(nc, in_maps, core_ids=list(range(_NCORES)))
    return np.concatenate([r["out"] for r in res.results], axis=0)

